# version 08
# baseline (speedup 1.0000x reference)
import numpy as np
import concourse.bass as bass
import concourse.mybir as mybir
from concourse.bass_utils import run_bass_kernel_spmd
from contextlib import ExitStack

F32 = mybir.dt.float32
BF16 = mybir.dt.bfloat16
AF = mybir.ActivationFunctionType
ALU = mybir.AluOpType
AX = mybir.AxisListType

P = 128
T = 2048
D = 1024
DC = D // P
NT = T // 512
NSEQ = 4
NCORES = 8
H = 16
NEG = -1e30

ENG_SEM = {"sync": "Y", "tensor": "T", "vector": "V", "scalar": "S",
           "gpsimd": "G"}


class Emitter:
    ENGINES = ("sync", "tensor", "vector", "scalar", "gpsimd")

    def __init__(self, nc, sems):
        self.nc = nc
        self.sems = sems
        self.cur = None
        self.h = None

    def snap(self):
        return [(s, self.cnt[s]) for s in self.cnt]

    def start(self, name, handle):
        self.cur = name
        self.h = handle
        self.cnt = {k: 0 for k in self.sems}
        self.know = {e: {k: 0 for k in self.sems} for e in self.ENGINES}
        self.dma_rr = {"Y": 0, "GD": 0}

    def _wait(self, eng, sem, val):
        if val > self.know[eng][sem]:
            self.know[eng][sem] = val
            if eng == self.cur:
                self.h.wait_ge(self.sems[sem], val)

    def op(self, eng, fn, waits=(), sem=None, inc=1, lp=False):
        flat = []
        for w in waits:
            if w is None:
                continue
            if isinstance(w, list):
                flat.extend(x for x in w if x is not None)
            else:
                flat.append(w)
        for s, v in flat:
            if v > 0:
                self._wait(eng, s, v)
        if sem is None:
            sem = ENG_SEM[eng]
        if eng == self.cur:
            if lp:
                with self.nc.allow_low_precision("bf16 ok within tolerance"):
                    inst = fn(self.h)
            else:
                inst = fn(self.h)
            if inc:
                inst.then_inc(self.sems[sem], inc)
        if inc:
            self.cnt[sem] += inc
        return (sem, self.cnt[sem])

    def t(self, fn, waits=(), inc=1):
        return self.op("tensor", fn, waits, "T", inc)

    def v(self, fn, waits=(), inc=1, lp=False):
        return self.op("vector", fn, waits, "V", inc, lp)

    def s(self, fn, waits=(), inc=1):
        return self.op("scalar", fn, waits, "S", inc)

    def g(self, fn, waits=(), inc=1):
        return self.op("gpsimd", fn, waits, "G", inc)

    def _dma(self, eng, prefix, nsem, out, in_, waits):
        idx = self.dma_rr[prefix] % nsem
        self.dma_rr[prefix] += 1
        sem = f"{prefix}{idx}"
        waits = list(waits) + [(sem, self.cnt[sem])]
        return self.op(eng, lambda h: h.dma_start(out=out, in_=in_),
                       waits, sem, 16)

    def dy(self, out, in_, waits=()):
        return self._dma("sync", "Y", 4, out, in_, waits)

    def dg(self, out, in_, waits=()):
        return self._dma("gpsimd", "GD", 4, out, in_, waits)


def ceil_div(a, b):
    return (a + b - 1) // b



def host_prep(inputs, core):
    b0 = core * NSEQ
    x = np.asarray(inputs["L0_attn_x"])
    d = {}
    d["xin"] = np.ascontiguousarray(
        x[b0:b0 + NSEQ].transpose(0, 2, 1)).astype(np.float32)
    return d


def host_prep_shared(inputs):
    g = lambda k: np.ascontiguousarray(np.asarray(inputs[k], dtype=np.float32))
    d = {}
    d["w_dqt"] = np.ascontiguousarray(g("L0_attn_W_DQ").T).reshape(32, P, D)
    d["w_uq"] = g("L0_attn_W_UQ").reshape(32, P, D)
    d["w_qr"] = g("L0_attn_W_QR").reshape(32, P, 512)
    d["w_dkv"] = g("L0_attn_W_DKV").reshape(DC, P, 256)
    d["w_uk"] = g("L0_attn_W_UK").reshape(2, P, D)
    d["w_uv"] = g("L0_attn_W_UV").reshape(2, P, D)
    d["w_kr"] = g("L0_attn_W_KR").reshape(DC, P, 32)
    d["w_o"] = g("L0_attn_W_O").reshape(DC, P, D)
    t_ = np.arange(T, dtype=np.float64)
    inv = 1.0 / 10000.0 ** (np.arange(0, 32, 2, dtype=np.float64) / 32)
    fr = t_[:, None] * inv[None, :]
    emb = np.concatenate([fr, fr], -1)
    sc = (np.cos(emb) + np.sin(emb)).astype(np.float32)
    d["rope_rep"] = np.ascontiguousarray(
        np.tile(sc.T, (4, 1))).astype(np.float32)
    tri01 = np.triu(np.ones((P, P), dtype=np.float32))
    d["tri01"] = tri01
    ident = np.eye(P, dtype=np.float32)
    d["ident_f"] = ident
    d["routerT"] = np.ascontiguousarray(
        g("L0_ffn_router_w").T).reshape(DC, P, 64)
    d["corr_rep"] = np.ascontiguousarray(np.broadcast_to(
        g("L0_ffn_correction_bias")[None, :], (P, 64)))
    d["up_w"] = g("L0_ffn_up_w").reshape(64, DC, P, 512)
    d["dn_w"] = g("L0_ffn_down_w").reshape(64, 4, P, D)
    d["shexp_up"] = np.ascontiguousarray(
        g("L0_ffn_shared_up_w").reshape(DC, P, 4, 512).transpose(2, 0, 1, 3))
    d["shexp_dn"] = g("L0_ffn_shared_down_w").reshape(4, 4, P, D)
    oh = np.zeros((64, 64 * P), dtype=np.float32)
    for e in range(64):
        oh[e, e * P:(e + 1) * P] = 1.0
    d["onehot_f"] = oh
    params = g("L1_attn_params")
    rows, cols = np.tril_indices(64)
    tril = np.zeros((16, 64, 64), dtype=np.float32)
    tril[:, rows, cols] = params
    trilT_f = np.zeros((P, 1024), dtype=np.float32)
    for h_ in range(16):
        trilT_f[0:64, h_ * 64:(h_ + 1) * 64] = tril[h_].T
    d["trilT_f"] = trilT_f
    d["wvT"] = np.ascontiguousarray(g("L1_attn_Wv").T).reshape(DC, P, D)
    d["woT2"] = np.ascontiguousarray(g("L1_attn_Wo").T).reshape(DC, P, D)
    d["bv_rep"] = np.ascontiguousarray(np.broadcast_to(
        g("L1_attn_bv")[None, :], (P, D)))
    d["boF_in"] = np.ascontiguousarray(g("L1_attn_bo").reshape(DC, P).T)
    d["wq_n"] = g("L1_ffn_W_Q").reshape(DC, P, D)
    d["wk_n"] = g("L1_ffn_W_K").reshape(DC, P, D)
    d["winv_n"] = g("L1_ffn_W_inv").reshape(DC, P, D)
    d["winvT"] = np.ascontiguousarray(g("L1_ffn_W_inv").T).reshape(DC, P, D)
    flat = lambda a: np.ascontiguousarray(
        a.transpose(1, 0, 2)).reshape(D, 512).reshape(DC, P, 512)
    d["vfF"] = flat(g("L1_ffn_V_fwd"))
    d["wfF"] = flat(g("L1_ffn_W_fwd"))
    d["vroleIF"] = flat(g("L1_ffn_W_inv_exp"))
    d["wroleIF"] = flat(g("L1_ffn_V_inv"))
    uflat = lambda a: np.ascontiguousarray(
        a.transpose(0, 2, 1)).reshape(512, D).reshape(4, P, D)
    d["ufui"] = np.concatenate([uflat(g("L1_ffn_U_fwd")),
                                uflat(g("L1_ffn_U_inv"))], axis=0)
    d["rw1T"] = np.ascontiguousarray(g("L1_ffn_router_w1").T).reshape(DC, P, D)
    d["rw2T_in"] = np.ascontiguousarray(
        g("L1_ffn_router_w2").T).reshape(DC, P, 8)
    d["woT3"] = np.ascontiguousarray(g("L1_ffn_W_O").T).reshape(DC, P, D)
    rep8 = np.zeros((8, 512), dtype=np.float32)
    for c in range(4):
        for pp in range(128):
            rep8[2 * c + pp // 64, c * P + pp] = 1.0
    d["rep8_in"] = rep8
    d["bfw_in"] = g("L1_ffn_b_fwd")
    d["binv_in"] = g("L1_ffn_b_inv")
    d["b1F_in"] = np.ascontiguousarray(g("L1_ffn_router_b1").reshape(DC, P).T)
    d["b2rep_in"] = np.ascontiguousarray(np.broadcast_to(
        g("L1_ffn_router_b2")[None, :], (P, 8)))
    d["ebrep_in"] = np.ascontiguousarray(np.broadcast_to(
        g("L1_ffn_expert_bias")[None, :], (P, 8)))
    d["alpha_in"] = np.full((P, 1), float(np.asarray(inputs["L1_ffn_alpha_bi"])),
                            dtype=np.float32)
    d["invn_in"] = np.ascontiguousarray(np.broadcast_to(
        (1.0 / np.arange(1, T + 1, dtype=np.float64)).astype(np.float32)[None, :],
        (P, T)))[None]
    return d



def alloc_phaseA(nc, st, bufs):
    sb = lambda n, s, dt: st.enter_context(nc.sbuf_tensor(n, s, dt))
    B = bufs
    B.xbf = sb("xbf", [P, DC * T], BF16)
    B.stage = [sb(f"stage{i}", [P, T], F32) for i in range(2)]
    B.qc = sb("qc", [P, DC * T], BF16)
    B.qr = sb("qr", [P, 4 * T], BF16)
    B.ckv = sb("ckv", [P, 2 * T], BF16)
    B.krs = sb("krs", [32, T], BF16)
    B.qstage = [sb(f"qstage{i}", [P, T], BF16) for i in range(2)]
    B.kstage = [sb(f"kstage{i}", [P, T], BF16) for i in range(2)]
    B.vh = [sb(f"vh{i}", [P, 16 * 66], BF16) for i in range(2)]
    B.probsT = sb("probsT", [P, 8 * 512], BF16)
    B.weffF = sb("weffF", [P, DC * D], BF16)
    B.wqreffF = sb("wqreffF", [P, DC * 512], BF16)
    B.wdkvF = sb("wdkvF", [P, DC * 256], BF16)
    B.wukF = sb("wukF", [P, 2 * D], BF16)
    B.wuvF = sb("wuvF", [P, 2 * D], BF16)
    B.wkrF = sb("wkrF", [P, DC * 32], BF16)
    B.woF = sb("woF", [P, DC * D], BF16)
    B.rope = sb("rope", [P, T], BF16)
    B.tri01 = sb("tri01_sb", [P, P], BF16)
    B.rrow = sb("rrow", [1, 512], BF16)
    B.rbc = sb("rbc", [64, 512], BF16)
    B.wstageA = lambda i, o=0, w=D: B.stage[i][:, o:o + w]
    B.wstageB = lambda i, o=0, w=D: B.stage[i][:, D + o:D + o + w]
    B.wbfA = lambda i, o=0, w=D: B.qstage[i][:, o: o + w]
    B.wbfB = lambda i, o=0, w=D: B.kstage[i][:, o: o + w]
    B.statrow = sb("statrow", [1, 512], F32)
    B.statbf = sb("statbf", [1, 512], BF16)
    B.statbf2 = sb("statbf2", [1, 512], BF16)
    B.sq = [sb(f"sq{i}", [P, 512], BF16) for i in range(2)]
    B.ones1 = sb("ones1", [1, P], BF16)
    B.ones128 = sb("ones128", [P, 1], BF16)


def emit_load_x(E, B, xin, s_, tok):
    fence = tok.get("fence", [])
    cast = [tok.get("st0"), tok.get("st1")]
    for c in range(DC):
        wl = [cast[c % 2], tok.get(f"ost{c % 2}")] + fence
        ld = E.dy(B.stage[c % 2][:], xin[s_, c * P:(c + 1) * P, :], waits=wl)
        cast[c % 2] = E.v(
            lambda h, c=c: h.tensor_copy(B.xbf[:, c * T:(c + 1) * T],
                                         B.stage[c % 2][:]),
            waits=[ld] + fence)
    tok["st0"] = cast[0]
    tok["st1"] = cast[1]
    tok["xbf"] = cast[(DC - 1) % 2]
    tok["xbf2"] = cast[DC % 2]


def emit_weff_prologue(E, B, W, PS, scratch, tok0):
    sweeps = (
        [("w_uq", n, dh, "weff") for n in range(2) for dh in range(2)]
        + [("w_qr", 0, dh, "qreff") for dh in range(2)]
    )
    lastA = [tok0.get("st0"), tok0.get("st1")]
    lastB = [tok0.get("st0"), tok0.get("st1")]
    ev_tok = None
    for rhs_name, n, dh, out_name in sweeps:
        mm_tok = None
        rw = int(W[rhs_name].shape[2])
        for kc in range(32):
            i = kc % 2
            la = E.dy(B.wstageA(i), W["w_dqt"][kc],
                      waits=[lastA[i]] if lastA[i] else [])
            lb = E.dy(B.wstageB(i, 0, rw),
                      W[rhs_name][kc], waits=[lastB[i]] if lastB[i] else [])
            ca = E.v(lambda h, i=i: h.tensor_copy(B.wbfA(i), B.wstageA(i)),
                     waits=[la])
            cb = E.v(lambda h, i=i, rw=rw:
                     h.tensor_copy(B.wbfB(i, 0, rw), B.wstageB(i, 0, rw)),
                     waits=[lb])
            for dj in range(4):
                dc = dh * 4 + dj
                c0 = n * 512
                mm_tok = E.t(
                    lambda h, kc=kc, dc=dc, c0=c0, dj=dj, i=i:
                    h.matmul(PS.big[:, dj * 512:(dj + 1) * 512],
                             B.wbfA(i, dc * P, P),
                             B.wbfB(i, c0, 512),
                             start=(kc == 0), stop=(kc == 31)),
                    waits=[ca, cb] + ([ev_tok] if (kc == 0 and ev_tok) else []))
            lastA[i] = mm_tok
            lastB[i] = mm_tok
        for dj in range(4):
            dc = dh * 4 + dj
            if out_name == "weff":
                dst = B.weffF[:, dc * D + n * 512: dc * D + n * 512 + 512]
            else:
                dst = B.wqreffF[:, dc * 512: dc * 512 + 512]
            ev_tok = E.v(
                lambda h, dst=dst, dj=dj:
                h.tensor_copy(dst, PS.big[:, dj * 512:(dj + 1) * 512]),
                waits=[mm_tok])
    d1 = E.dg(scratch["weff"], B.weffF[:], waits=[ev_tok])
    d2 = E.dg(scratch["qreff"], B.wqreffF[:], waits=[ev_tok])
    tok0["st0"] = lastA[0]
    tok0["st1"] = lastA[1]
    return ev_tok, (d1, d2)


def emit_small_weights_load(E, B, W, tok):
    jobs = [("w_dkv", B.wdkvF, 256), ("w_uk", B.wukF, D), ("w_uv", B.wuvF, D),
            ("w_kr", B.wkrF, 32), ("w_o", B.woF, D)]
    last = [tok.get("st0"), tok.get("st1")]
    casts = []
    for name, dst, cw in jobs:
        nch = W[name].shape[0]
        for c in range(nch):
            i = len(casts) % 2
            wl = [last[i]] if last[i] else []
            ld = E.dy(B.stage[i][:, 0:cw], W[name][c], waits=wl)
            ct = E.v(lambda h, dst=dst, c=c, cw=cw, i=i:
                     h.tensor_copy(dst[:, c * cw:(c + 1) * cw],
                                   B.stage[i][:, 0:cw]), waits=[ld])
            last[i] = ct
            casts.append(ct)
    tok["wsmall"] = casts[-1]
    tok["st0"] = last[0]
    tok["st1"] = last[1]


def emit_phaseA_seq(E, B, PS, tok, with_out=None):
    fence = tok.get("fence", [])
    xw = [tok["xbf"], tok["xbf2"]]

    banks = [PS.a, PS.b, PS.c]
    bank_free = [tok.get("psa_free"), tok.get("psb_free"), tok.get("psc_free")]
    bi = 0

    def next_bank():
        nonlocal bi
        b = bi % 3
        bi += 1
        return b

    banks4 = [PS.a[:, 0:512], PS.b[:, 0:512], PS.c[:, 0:512],
              PS.big[:, 0:512]]
    b4_free = [tok.get("psa_free"), tok.get("psb_free"),
               tok.get("psc_free"), None]

    def proj4(w_img, wof, M, nkc, dst_fn, wtok, rope_mul=False, base=0):
        nonlocal b4_free
        mm4 = [None] * 4
        for kc in range(nkc):
            for nt in range(NT):
                mm4[nt] = E.t(
                    lambda h, kc=kc, nt=nt, M=M, wof=wof:
                    h.matmul(banks4[nt][0:M, 0:512] if M < P else banks4[nt],
                             w_img[:, wof(kc): wof(kc) + M],
                             B.xbf[:, kc * T + nt * 512: kc * T + nt * 512 + 512],
                             start=(kc == 0), stop=(kc == nkc - 1)),
                    waits=([wtok, xw[0], xw[1], b4_free[nt]] + fence)
                    if kc == 0 else [])
        for nt in range(NT):
            if rope_mul:
                b4_free[nt] = E.v(
                    lambda h, nt=nt, M=M: h.tensor_tensor(
                        dst_fn(nt),
                        banks4[nt][0:M, 0:512] if M < P else banks4[nt],
                        B.rope[0:M, nt * 512: nt * 512 + 512] if M < P
                        else B.rope[:, nt * 512: nt * 512 + 512],
                        op=ALU.mult),
                    waits=[mm4[nt], tok["rope"]] + fence)
            else:
                b4_free[nt] = E.v(
                    lambda h, nt=nt, M=M: h.tensor_copy(
                        dst_fn(nt),
                        banks4[nt][0:M, 0:512] if M < P else banks4[nt]),
                    waits=[mm4[nt]] + fence)

    for oc in range(2):
        proj4(B.wdkvF, lambda kc, oc=oc: kc * 256 + oc * P, P, DC,
              lambda nt, oc=oc: B.ckv[:, oc * T + nt * 512: oc * T + nt * 512 + 512],
              tok["wsmall"])
    tok["ckv"] = b4_free[3]
    ckv_toks = [tok["ckv"]]
    proj4(B.wkrF, lambda kc: kc * 32, 32, DC,
          lambda nt: B.krs[:, nt * 512: nt * 512 + 512],
          tok["wsmall"], rope_mul=True)
    tok["krs"] = b4_free[3]
    for oc in range(DC):
        proj4(B.weffF, lambda kc, oc=oc: kc * D + oc * P, P, DC,
              lambda nt, oc=oc: B.qc[:, oc * T + nt * 512: oc * T + nt * 512 + 512],
              tok["weff"])
    qc_tok = b4_free[3]
    for c in range(4):
        proj4(B.wqreffF, lambda kc, c=c: kc * 512 + c * P, P, DC,
              lambda nt, c=c: B.qr[:, c * T + nt * 512: c * T + nt * 512 + 512],
              tok["weff"], rope_mul=True)
    qr_tok = b4_free[3]
    tok["psa_free"] = b4_free[0]
    tok["psb_free"] = b4_free[1]
    tok["psc_free"] = b4_free[2]

    banks = [PS.a, PS.b, PS.c]
    bank_free = [b4_free[0], b4_free[1], b4_free[2]]
    bi = 0

    def next_bank():
        nonlocal bi
        b = bi % 3
        bi += 1
        return b

    stage_free = [tok.get("stage_free0"), tok.get("stage_free1")]
    vh_free = [tok.get("vh_free0"), tok.get("vh_free1")]
    vh_ones = [tok.get("vh_ones0"), tok.get("vh_ones1")]
    y_toks = []
    for h_ in range(H):
        i = h_ % 2
        kst = []
        for nt in range(NT):
            b = next_bank()
            mm = None
            for kc in range(2):
                mm = E.t(lambda h, h_=h_, nt=nt, kc=kc, b=b:
                         h.matmul(banks[b][0:64, 0:512],
                                  B.wukF[:, kc * D + h_ * 64: kc * D + h_ * 64 + 64],
                                  B.ckv[:, kc * T + nt * 512: kc * T + nt * 512 + 512],
                                  start=(kc == 0), stop=(kc == 1)),
                         waits=([bank_free[b], stage_free[i]] + ckv_toks)
                         if kc == 0 else [])
            bank_free[b] = E.v(lambda h, i=i, nt=nt, b=b:
                               h.tensor_copy(B.kstage[i][0:64, nt * 512: nt * 512 + 512],
                                             banks[b][0:64, 0:512]), waits=[mm])
            kst.append(bank_free[b])
        kst.append(E.v(lambda h, i=i: h.tensor_copy(B.kstage[i][64:96, :],
                                                    B.krs[:]),
                       waits=[tok["krs"], stage_free[i]]))
        ASCL = 1.0 / float(np.sqrt(96.0))
        qst1 = E.v(lambda h, i=i, h_=h_:
                   h.tensor_scalar_mul(B.qstage[i][0:64, :],
                                       B.qc[(h_ % 2) * 64:(h_ % 2) * 64 + 64,
                                            (h_ // 2) * T:(h_ // 2) * T + T],
                                       ASCL),
                   waits=[qc_tok, stage_free[i]])
        r0 = (h_ % 4) * 32
        if r0 < 96:
            qst2 = E.v(lambda h, i=i, h_=h_, r0=r0:
                       h.tensor_scalar_mul(B.qstage[i][64:96, :],
                                           B.qr[r0:r0 + 32,
                                                (h_ // 4) * T:(h_ // 4) * T + T],
                                           ASCL),
                       waits=[qr_tok, qst1])
        else:
            qd = E.dg(B.qstage[i][64:96, :],
                      B.qr[96:128, (h_ // 4) * T:(h_ // 4) * T + T],
                      waits=[qr_tok, qst1])
            qst2 = E.v(lambda h, i=i:
                       h.tensor_scalar_mul(B.qstage[i][64:96, :],
                                           B.qstage[i][64:96, :], ASCL),
                       waits=[qd, qst1])
        if vh_ones[i] is None:
            vh_ones[i] = E.v(lambda h, i=i: h.memset(B.vh[i][:], 1.0),
                             waits=[vh_free[i]] if vh_free[i] else [])
        vtoks = []
        for tt in range(16):
            b = next_bank()
            mm = None
            for kc in range(2):
                mm = E.t(lambda h, h_=h_, tt=tt, kc=kc, b=b:
                         h.matmul(banks[b][:, 0:64],
                                  B.ckv[:, kc * T + tt * P: kc * T + tt * P + P],
                                  B.wuvF[:, kc * D + h_ * 64: kc * D + h_ * 64 + 64],
                                  start=(kc == 0), stop=(kc == 1)),
                         waits=[bank_free[b], vh_free[i], vh_ones[i]]
                         if kc == 0 else [])
            bank_free[b] = E.v(lambda h, i=i, tt=tt, b=b:
                               h.tensor_copy(B.vh[i][:, tt * 66: tt * 66 + 64],
                                             banks[b][:, 0:64]), waits=[mm])
            vtoks.append(bank_free[b])

        kq_waits = kst + [qst2] + vtoks
        half_free = [None, None]
        last_av = tok.get("av_prev")
        prob_tok = {}
        pool_toks = []
        for qg in range(NT):
            nkt = 4 * qg + 4
            av = None
            for kb in range(ceil_div(nkt, 2)):
                kts = [kt for kt in (2 * kb, 2 * kb + 1) if kt < nkt]
                hf = kb % 2
                mms = []
                for kt in kts:
                    sl = (hf * 2 + (kt % 2)) * 512
                    mms.append(E.t(
                        lambda h, kt=kt, qg=qg, sl=sl:
                        h.matmul(PS.big[:, sl:sl + 512],
                                 B.kstage[i][0:96, kt * P: kt * P + P],
                                 B.qstage[i][0:96, qg * 512: qg * 512 + 512],
                                 start=True, stop=True),
                        waits=kq_waits + [half_free[hf], last_av]))
                sl0 = hf * 1024
                rg = kts[0] % 8
                region_free = av if kts[0] >= 8 else last_av
                ex = E.s(
                    lambda h, rg=rg, nk_=len(kts), sl0=sl0:
                    h.activation(B.probsT[:, rg * 512: rg * 512 + nk_ * 512],
                                 PS.big[:, sl0: sl0 + nk_ * 512], AF.Exp),
                    waits=mms + [region_free] + pool_toks)
                half_free[hf] = ex
                for kt in kts:
                    prob_tok[kt] = ex
                if nkt > 8 and kts[-1] == 7:
                    avb0 = PS.a if qg % 2 == 0 else PS.c
                    for kt2 in range(8):
                        av = E.t(lambda h, i=i, kt2=kt2, avb0=avb0,
                                 first=(kt2 == 0):
                                 h.matmul(avb0[0:65, 0:512],
                                          B.vh[i][:, kt2 * 66: kt2 * 66 + 65],
                                          B.probsT[:, kt2 * 512: kt2 * 512 + 512],
                                          start=first, stop=False),
                                 waits=([prob_tok[k2] for k2 in range(8)] +
                                        [tok.get(f"yev{qg % 2}"), tok.get("rrow_used")])
                                 if kt2 == 0 else [])
            pool_toks = []
            for kt in range(4 * qg, 4 * qg + 4):
                dq = kt - 4 * qg
                rg = kt % 8
                if dq > 0:
                    pool_toks.append(E.g(
                        lambda h, rg=rg, dq=dq:
                        h.memset(B.probsT[:, rg * 512: rg * 512 + dq * P], 0.0),
                        waits=[prob_tok[kt]]))
                pool_toks.append(E.g(
                    lambda h, rg=rg, dq=dq:
                    h.tensor_tensor(B.probsT[:, rg * 512 + dq * P: rg * 512 + dq * P + P],
                                    B.probsT[:, rg * 512 + dq * P: rg * 512 + dq * P + P],
                                    B.tri01[:], op=ALU.mult),
                    waits=[prob_tok[kt]]))
            avb = PS.a if qg % 2 == 0 else PS.c
            k_lo = 8 if nkt > 8 else 0
            for kt in range(k_lo, nkt):
                av = E.t(lambda h, i=i, kt=kt, avb=avb,
                         first=(kt == k_lo and k_lo == 0),
                         last=(kt == nkt - 1):
                         h.matmul(avb[0:65, 0:512],
                                  B.vh[i][:, kt * 66: kt * 66 + 65],
                                  B.probsT[:, (kt % 8) * 512: (kt % 8) * 512 + 512],
                                  start=first, stop=last),
                         waits=([prob_tok[k2] for k2 in range(k_lo, nkt)] + pool_toks +
                                [tok.get(f"yev{qg % 2}"), tok.get("rrow_used")])
                         if kt == k_lo else [])
            last_av = av
            rr0 = E.s(lambda h, avb=avb: h.activation(B.rrowf[:],
                                                      avb[64:65, 0:512],
                                                      AF.Ln),
                      waits=[av, tok.get("rbc_used"), tok.get("rrowf_used")])
            rr = E.s(lambda h: h.activation(B.rrow[:], B.rrowf[:], AF.Exp,
                                            scale=-1.0),
                     waits=[rr0])
            tok["rrowf_used"] = rr
            bc = E.t(lambda h: h.matmul(PS.b[0:64, 0:512], B.ones1[:, 0:64],
                                        B.rrow[:], start=True, stop=True),
                     waits=[rr, tok.get("rbc_used")])
            rbc_ev = E.v(lambda h: h.tensor_copy(B.rbc[:], PS.b[0:64, 0:512]),
                         waits=[bc, tok.get("yev")])
            tok["rbc_used"] = rbc_ev
            tok["rrow_used"] = rbc_ev
            yev = E.v(lambda h, h_=h_, qg=qg, avb=avb:
                      h.tensor_tensor(B.qc[(h_ % 2) * 64:(h_ % 2) * 64 + 64,
                                           (h_ // 2) * T + qg * 512:
                                           (h_ // 2) * T + qg * 512 + 512],
                                      avb[0:64, 0:512], B.rbc[:],
                                      op=ALU.mult), waits=[rbc_ev])
            tok["yev"] = yev
            tok[f"yev{qg % 2}"] = yev
            y_toks.append(yev)
            bank_free[0 if qg % 2 == 0 else 2] = yev
            bank_free[1] = rbc_ev
        tok["av_prev"] = last_av
        stage_free[i] = last_av
        vh_free[i] = last_av
    tok["vh_ones0"] = vh_ones[0]
    tok["vh_ones1"] = vh_ones[1]
    tok["stage_free0"] = stage_free[0]
    tok["stage_free1"] = stage_free[1]
    tok["vh_free0"] = vh_free[0]
    tok["vh_free1"] = vh_free[1]

    emit_out_ln(E, B, PS, tok,
                w_img=B.woF, w_tok=tok["wsmall"], src=B.qc,
                src_toks=[y_toks[-1]], nk=DC, with_out=with_out,
                bank_free=[bank_free[0], bank_free[1], bank_free[2]])


def emit_out_ln(E, B, PS, tok, w_img, w_tok, src, src_toks, nk, with_out,
                bank_free=None, bias_cols=None, bias_tok=()):
    banks4 = [PS.a[:, 0:512], PS.b[:, 0:512], PS.c[:, 0:512],
              PS.big[:, 0:512]]
    if bank_free is None:
        bank_free = [None, None, None]
    b4_free = [bank_free[0], bank_free[1], bank_free[2], None]
    extra = [tok.get("yev"), tok.get("rbc_used"), tok.get("av_prev")]
    bias_tok = list(bias_tok)
    xr_toks = []
    for oc in range(DC):
        mm4 = [None] * 4
        for kc in range(nk):
            for nt in range(NT):
                mm4[nt] = E.t(
                    lambda h, oc=oc, nt=nt, kc=kc:
                    h.matmul(banks4[nt],
                             w_img[:, kc * D + oc * P: kc * D + oc * P + P],
                             src[:, kc * T + nt * 512: kc * T + nt * 512 + 512],
                             start=(kc == 0), stop=(kc == nk - 1)),
                    waits=(src_toks + [w_tok, b4_free[nt]] + extra)
                    if kc == 0 else [])
        for nt in range(NT):
            if bias_cols is None:
                b4_free[nt] = E.v(
                    lambda h, oc=oc, nt=nt:
                    h.tensor_tensor(B.xbf[:, oc * T + nt * 512: oc * T + nt * 512 + 512],
                                    banks4[nt],
                                    B.xbf[:, oc * T + nt * 512: oc * T + nt * 512 + 512],
                                    op=ALU.add), waits=[mm4[nt]])
            else:
                b4_free[nt] = E.v(
                    lambda h, oc=oc, nt=nt:
                    h.scalar_tensor_tensor(
                        B.xbf[:, oc * T + nt * 512: oc * T + nt * 512 + 512],
                        banks4[nt],
                        bias_cols[:, oc:oc + 1],
                        B.xbf[:, oc * T + nt * 512: oc * T + nt * 512 + 512],
                        op0=ALU.add, op1=ALU.add),
                    waits=[mm4[nt]] + bias_tok)
            xr_toks.append(b4_free[nt])
    tok["psa_free"] = b4_free[0]
    tok["psb_free"] = b4_free[1]
    tok["psc_free"] = b4_free[2]
    emit_ln(E, B, PS, tok, xr_toks, with_out)


def emit_ln(E, B, PS, tok, xr_toks, with_out):
    onetok = tok["ones"]
    out_toks = []
    prev_bc = []
    for nt in range(NT):
        mm1 = None
        for oc in range(DC):
            mm1 = E.t(lambda h, oc=oc, nt=nt:
                      h.matmul(PS.c[0:1, 0:512], B.ones128[:],
                               B.xbf[:, oc * T + nt * 512: oc * T + nt * 512 + 512],
                               start=(oc == 0), stop=(oc == DC - 1)),
                      waits=(xr_toks + [onetok, tok.get("psc_free")])
                      if oc == 0 else [])
        m_ev = E.v(lambda h, nt=nt:
                   h.tensor_scalar_mul(B.statrow[:],
                                       PS.c[0:1, 0:512], 1.0 / D),
                   waits=[mm1] + prev_bc)
        mbf = E.v(lambda h, nt=nt:
                  h.tensor_copy(B.statbf[:],
                                B.statrow[:]),
                  waits=[m_ev])
        mm2 = None
        sqt = [None, None]
        for oc in range(DC):
            sqt[oc % 2] = E.s(
                lambda h, oc=oc, nt=nt:
                h.activation(B.sq[oc % 2][:],
                             B.xbf[:, oc * T + nt * 512: oc * T + nt * 512 + 512],
                             AF.Square),
                waits=xr_toks + ([mm2] if mm2 else []))
            mm2 = E.t(lambda h, oc=oc, nt=nt:
                      h.matmul(PS.c[0:1, 0:512], B.ones128[:],
                               B.sq[oc % 2][:],
                               start=(oc == 0), stop=(oc == DC - 1)),
                      waits=[sqt[oc % 2]] + ([m_ev] if oc == 0 else []))
        e2 = E.v(lambda h, nt=nt:
                 h.scalar_tensor_tensor(B.statrow[:],
                                        B.statrow[:],
                                        -1.0,
                                        B.statrow[:],
                                        op0=ALU.mult, op1=ALU.mult),
                 waits=[mm2, mbf])
        e3 = E.v(lambda h, nt=nt:
                 h.scalar_tensor_tensor(B.statrow[:],
                                        PS.c[0:1, 0:512], 1.0 / D,
                                        B.statrow[:],
                                        op0=ALU.mult, op1=ALU.add),
                 waits=[e2])
        tok["psc_free"] = e3
        e3b = E.v(lambda h: h.tensor_scalar_add(B.statrow[:], B.statrow[:],
                                                1e-5), waits=[e3])
        e4 = E.s(lambda h, nt=nt:
                 h.activation(B.statrow[:],
                              B.statrow[:],
                              AF.Ln), waits=[e3b])
        e5 = E.s(lambda h, nt=nt:
                 h.activation(B.statrow[:],
                              B.statrow[:],
                              AF.Exp, scale=-0.5), waits=[e4])
        rbf = E.v(lambda h, nt=nt:
                  h.tensor_copy(B.statbf2[:],
                                B.statrow[:]),
                  waits=[e5])
        bcm = E.t(lambda h, nt=nt:
                  h.matmul(PS.a[:, 0:512], B.ones1[:],
                           B.statbf[:],
                           start=True, stop=True),
                  waits=[mbf, tok.get("psa_free"), tok.get("av_prev"),
                         tok.get("yev")])
        bcr = E.t(lambda h, nt=nt:
                  h.matmul(PS.b[:, 0:512], B.ones1[:],
                           B.statbf2[:],
                           start=True, stop=True),
                  waits=[rbf, tok.get("psb_free"), tok.get("rbc_used")])
        prev_bc = [bcm, bcr]
        nrm = []
        for oc in range(DC):
            t1 = E.v(lambda h, oc=oc, nt=nt:
                     h.tensor_tensor(B.xbf[:, oc * T + nt * 512: oc * T + nt * 512 + 512],
                                     B.xbf[:, oc * T + nt * 512: oc * T + nt * 512 + 512],
                                     PS.a[:, 0:512], op=ALU.subtract),
                     waits=[bcm] + xr_toks)
            t2 = E.v(lambda h, oc=oc, nt=nt:
                     h.tensor_tensor(B.xbf[:, oc * T + nt * 512: oc * T + nt * 512 + 512],
                                     B.xbf[:, oc * T + nt * 512: oc * T + nt * 512 + 512],
                                     PS.b[:, 0:512], op=ALU.mult),
                     waits=[t1, bcr])
            nrm.append(t2)
            if with_out is not None:
                cf = E.s(lambda h, oc=oc, nt=nt, i=(oc % 2):
                         h.activation(B.stage[i][:, nt * 512: nt * 512 + 512],
                                      B.xbf[:, oc * T + nt * 512: oc * T + nt * 512 + 512],
                                      AF.Copy),
                         waits=[t2, tok.get(f"ost{oc % 2}")])
                od = E.dg(with_out[oc * P:(oc + 1) * P, nt * 512: nt * 512 + 512],
                          B.stage[oc % 2][:, nt * 512: nt * 512 + 512],
                          waits=[cf])
                tok[f"ost{oc % 2}"] = od
        tok["psa_free"] = nrm[-1]
        tok["psb_free"] = nrm[-1]
        out_toks.extend(nrm)
    tok["xbf"] = out_toks[-1]
    tok["xbf2"] = out_toks[-2]
    tok["ln_done"] = out_toks
    tok["fence"] = [(s, E.cnt[s]) for s in ("T", "V", "S", "G")]
    for k in ("av_prev", "yev", "rbc_used", "rrow_used"):
        tok.pop(k, None)



class NS:
    pass


def build_phaseA(nseq=NSEQ, final=False):
    nc = bass.Bass("TRN2", target_bir_lowering=False)
    xin = nc.dram_tensor("xin", [nseq, D, T], F32, kind="ExternalInput").ap()
    xout = nc.dram_tensor("xout", [nseq, D, T], F32, kind="ExternalOutput").ap()
    W = {}
    for name, shp in [("w_dqt", (32, P, D)), ("w_uq", (32, P, D)),
                      ("w_qr", (32, P, 512)), ("w_dkv", (DC, P, 256)),
                      ("w_uk", (2, P, D)), ("w_uv", (2, P, D)),
                      ("w_kr", (DC, P, 32)), ("w_o", (DC, P, D)),
                      ("rope_rep", (P, T)), ("tri01", (P, P))]:
        W[name] = nc.dram_tensor(name, list(shp), F32, kind="ExternalInput").ap()
    scratch = {
        "weff": nc.dram_tensor("weff_s", [P, DC * D], BF16, kind="Internal").ap(),
        "qreff": nc.dram_tensor("qreff_s", [P, DC * 512], BF16, kind="Internal").ap(),
    }

    with ExitStack() as st:
        B = NS()
        alloc_phaseA(nc, st, B)
        PS = NS()
        PS.big = st.enter_context(nc.psum_tensor("psbig", [P, 2048], F32))
        PS.a = st.enter_context(nc.psum_tensor("psa", [P, 512], F32))
        PS.b = st.enter_context(nc.psum_tensor("psb", [P, 512], F32))
        PS.c = st.enter_context(nc.psum_tensor("psc", [P, 512], F32))
        sems = {}
        for s in ("T", "V", "S", "G", "Y0", "Y1", "Y2", "Y3",
                  "GD0", "GD1", "GD2", "GD3"):
            sems[s] = st.enter_context(nc.semaphore(s))
        block = st.enter_context(nc.Block())
        E = Emitter(nc, sems)

        def prog():
            tok = {}
            c1 = E.v(lambda h: h.memset(B.ones1[:], 1.0))
            c2 = E.v(lambda h: h.memset(B.ones128[:], 1.0), waits=[c1])
            tok["ones"] = c2
            ld = E.dy(B.stage[0][:, 0:P], W["tri01"])
            tok["tri"] = E.v(lambda h: h.tensor_copy(B.tri01[:], B.stage[0][:, 0:P]),
                             waits=[ld, c2])
            ldr = E.dy(B.stage[1][:], W["rope_rep"])
            tok["rope"] = E.v(lambda h: h.tensor_copy(B.rope[:], B.stage[1][:]),
                              waits=[ldr])
            tok["st0"] = tok["tri"]
            tok["st1"] = tok["rope"]
            emit_small_weights_load(E, B, W, tok)
            wtok, _ = emit_weff_prologue(E, B, W, PS, scratch, tok)
            tok["weff"] = wtok
            for s_ in range(nseq):
                emit_load_x(E, B, xin, s_, tok)
                emit_phaseA_seq(E, B, PS, tok, with_out=xout[s_, :, :])
            for k in ("GD0", "GD1", "GD2", "GD3"):
                E.op("gpsimd",
                     lambda h, k=k: h.wait_ge(sems[k], E.cnt[k]), inc=0)

        for name in Emitter.ENGINES:
            getattr(block, name)(lambda h, name=name: (E.start(name, h), prog()))
    return nc


PHASE_INPUTS = {
    "A": ("w_dqt", "w_uq", "w_qr", "w_dkv", "w_uk", "w_uv", "w_kr", "w_o",
          "rope_rep", "tri01"),
    "B": ("routerT", "corr_rep", "up_w", "dn_w", "shexp_up", "shexp_dn",
          "onehot_f", "ident_f"),
    "C": ("trilT_f", "wvT", "woT2", "bv_rep", "boF_in", "ident_f"),
    "D": ("wq_n", "wk_n", "winv_n", "winvT", "vfF", "wfF", "vroleIF",
          "wroleIF", "ufui", "rw1T", "woT3", "rw2T_in", "rep8_in", "bfw_in",
          "binv_in", "b1F_in", "b2rep_in", "ebrep_in", "alpha_in", "invn_in",
          "ident_f"),
}


FUSED_INPUTS = (
    "w_dqt", "w_uq", "w_qr", "w_dkv", "w_uk", "w_uv", "w_kr", "w_o",
    "rope_rep", "tri01", "routerT", "corr_rep", "up_w", "dn_w", "shexp_up",
    "shexp_dn", "onehot_f", "trilT_f", "wvT", "woT2", "bv_rep", "boF_in",
    "wq_n", "wk_n", "winv_n", "winvT", "vfF", "wfF", "vroleIF", "wroleIF",
    "ufui", "rw1T", "woT3", "rw2T_in", "rep8_in", "bfw_in", "binv_in",
    "b1F_in", "b2rep_in", "ebrep_in", "alpha_in", "invn_in", "ident_f")


def kernel(**inputs):
    W = host_prep_shared(inputs)
    x = np.asarray(inputs["L0_attn_x"], dtype=np.float32)
    xc = [np.ascontiguousarray(x[c * NSEQ:(c + 1) * NSEQ].transpose(0, 2, 1))
          for c in range(NCORES)]
    nc = build_fused(nseq=NSEQ)
    in_maps = []
    for c in range(NCORES):
        m = {"xin": xc[c]}
        for k in FUSED_INPUTS:
            m[k] = W[k]
        in_maps.append(m)
    res = run_bass_kernel_spmd(nc, in_maps, core_ids=list(range(NCORES)))
    cur = [np.asarray(r["xout"]) for r in res.results]
    out = np.concatenate([c.transpose(0, 2, 1) for c in cur], axis=0)
    return np.ascontiguousarray(out.astype(np.float32))



NEXP = 68
GRPS = [list(range(4))] + [[4 + 4 * g + j for j in range(4)] for g in range(16)]


def alloc_phaseB(nc, st, bufs):
    sb = lambda n, s, dt: st.enter_context(nc.sbuf_tensor(n, s, dt))
    B = bufs
    B.xbf = sb("xbf", [P, DC * T], BF16)
    B.stage = [sb(f"stage{i}", [P, T], F32) for i in range(2)]
    B.sq = [sb(f"sq{i}", [P, 512], BF16) for i in range(2)]
    B.statrow = sb("statrow", [1, 512], F32)
    B.statbf = sb("statbf", [1, 512], BF16)
    B.statbf2 = sb("statbf2", [1, 512], BF16)
    B.ones1 = sb("ones1", [1, P], BF16)
    B.ones128 = sb("ones128", [P, 1], BF16)
    B.identbf = sb("identbf", [P, P], BF16)
    B.routerF = sb("routerF", [P, DC * 64], BF16)
    B.corr = sb("corr", [P, 64], F32)
    B.lgall = sb("lgall", [P, 16 * 64], F32)
    B.sc = sb("sc", [P, 256], F32)
    B.sfc = sb("sfc", [P, 256], F32)
    B.msk = sb("msk", [P, 256], F32)
    B.hit = sb("hit", [P, 256], F32)
    B.sel = sb("sel", [P, 256], F32)
    B.gs = sb("gs", [P, 32], F32)
    B.g2 = sb("g2", [P, 32], F32)
    B.gm = sb("gm", [P, 32], F32)
    B.rm = sb("rm", [P, 4], F32)
    B.rs = sb("rs", [P, 4], F32)
    B.combine = sb("combine", [P, 16 * 64], F32)
    B.cmbbf = sb("cmbbf", [P, 64], BF16)
    B.cT = sb("cT", [64, T], BF16)
    B.onehot = sb("onehot", [64, 64 * P], BF16)
    B.wup = [sb(f"wup{i}", [P, 8 * 512], BF16) for i in range(2)]
    B.wdn = [sb(f"wdn{i}", [P, 4 * D], BF16) for i in range(4)]
    B.hbuf = sb("hbuf", [P, 4 * 4 * 2 * 512], BF16)
    B.rbuf = [sb(f"rbuf{i}", [P, 512], BF16) for i in range(2)]
    B.sqe = [sb(f"sqe{i}", [P, 512], BF16) for i in range(2)]
    B.cbc = [sb(f"cbc{i}", [P, 512], BF16) for i in range(2)]
    B.acc = sb("acc", [P, DC * 1024], F32)


def emit_router_logits(E, B, PS, tok):
    fence = tok.get("fence", [])
    xw = [tok["xbf"], tok["xbf2"]]
    lbank = [PS.c[:, 0:64], PS.big[:, 1536:1600]]
    lfree = [tok.get("psc_free"), None]
    e0 = None
    for tt in range(16):
        lb = tt % 2
        mm = None
        for kc in range(DC):
            mm = E.t(lambda h, tt=tt, kc=kc, lb=lb:
                     h.matmul(lbank[lb],
                              B.xbf[:, kc * T + tt * P: kc * T + tt * P + P],
                              B.routerF[:, kc * 64: kc * 64 + 64],
                              start=(kc == 0), stop=(kc == DC - 1)),
                     waits=([tok["router"], xw[0], xw[1],
                             lfree[lb]] + fence) if kc == 0 else [])
        e0 = E.s(lambda h, tt=tt, lb=lb:
                 h.activation(B.lgall[:, tt * 64: tt * 64 + 64], lbank[lb],
                              AF.Copy),
                 waits=[mm, tok.get("lgall_used")] + fence)
        lfree[lb] = e0
    tok["psc_free"] = lfree[0]
    tok["logits"] = e0


def emit_routing_v(E, B, tok):
    fence = tok.get("fence", [])
    bca = bass.broadcast_tensor_aps
    prev = {}
    cmb_toks = []
    for g4 in range(4):
        sl = slice(g4 * 256, g4 * 256 + 256)
        sc3 = B.sc[:].rearrange("p (t e) -> p t e", e=64)
        sfc3 = B.sfc[:].rearrange("p (t e) -> p t e", e=64)
        sfc6 = B.sfc[:].rearrange("p (t g e) -> p t g e", g=8, e=8)
        msk6 = B.msk[:].rearrange("p (t g e) -> p t g e", g=8, e=8)
        msk3 = B.msk[:].rearrange("p (t e) -> p t e", e=64)
        hit6 = B.hit[:].rearrange("p (t g e) -> p t g e", g=8, e=8)
        gs3 = B.gs[:].rearrange("p (t g) -> p t g", g=8)
        g23 = B.g2[:].rearrange("p (t g) -> p t g", g=8)
        gm3 = B.gm[:].rearrange("p (t g) -> p t g", g=8)
        corr_b = bca(sc3, B.corr[:].rearrange("p (t e) -> p t e", t=1))[1]
        gs_b = bca(sfc6, B.gs[:].rearrange("p (t g e) -> p t g e", g=8, e=1))[1]
        gm_b = bca(sfc6, B.gm[:].rearrange("p (t g e) -> p t g e", g=8, e=1))[1]
        rm_b8 = bca(gs3, B.rm[:].rearrange("p (t g) -> p t g", g=1))[1]
        rm_b64 = bca(sfc3, B.rm[:].rearrange("p (t e) -> p t e", e=1))[1]
        rs_b64 = bca(sfc3, B.rs[:].rearrange("p (t e) -> p t e", e=1))[1]
        s0 = E.s(lambda h, sl=sl: h.activation(B.sc[:], B.lgall[:, sl],
                                               AF.Sigmoid),
                 waits=[tok["logits"], prev.get("sc")] + fence)
        v0 = E.v(lambda h: h.tensor_tensor(sfc3, sc3, corr_b, op=ALU.add),
                 waits=[s0, tok["corr"], prev.get("sfc")] + fence)
        v1 = E.v(lambda h: h.reduce_max(B.gs[:], sfc6, axis=AX.X),
                 waits=[v0, prev.get("gs")] + fence)
        v2 = E.v(lambda h: h.tensor_tensor(hit6, sfc6, gs_b, op=ALU.is_ge),
                 waits=[v1, prev.get("hit")])
        v3 = E.v(lambda h: h.scalar_tensor_tensor(B.msk[:], B.hit[:], NEG,
                                                  B.sfc[:], op0=ALU.mult,
                                                  op1=ALU.add),
                 waits=[v2, prev.get("msk")])
        v4 = E.v(lambda h: h.reduce_max(B.g2[:], msk6, axis=AX.X),
                 waits=[v3, prev.get("g2")])
        v5 = E.v(lambda h: h.tensor_add(B.gs[:], B.gs[:], B.g2[:]),
                 waits=[v4])
        v6 = E.v(lambda h: h.memset(B.gm[:], 0.0),
                 waits=[prev.get("gm"), v5])
        for it in range(4):
            v7 = E.v(lambda h: h.reduce_max(B.rm[:], gs3, axis=AX.X),
                     waits=[v6, prev.get("rm")])
            v8 = E.v(lambda h: h.tensor_tensor(g23, gs3, rm_b8, op=ALU.is_ge),
                     waits=[v7])
            v9 = E.v(lambda h: h.tensor_add(B.gm[:], B.gm[:], B.g2[:]),
                     waits=[v8])
            v6 = E.v(lambda h: h.scalar_tensor_tensor(B.gs[:], B.g2[:], NEG,
                                                      B.gs[:], op0=ALU.mult,
                                                      op1=ALU.add),
                     waits=[v9])
            prev["rm"] = v6
        v10 = E.v(lambda h: h.tensor_tensor(msk6, sfc6, gm_b, op=ALU.mult),
                  waits=[v6])
        v11 = E.v(lambda h: h.memset(B.sel[:], 0.0),
                  waits=[prev.get("sel"), v10])
        for it in range(8):
            v12 = E.v(lambda h: h.reduce_max(B.rm[:], msk3, axis=AX.X),
                      waits=[v11, v10])
            v13 = E.v(lambda h: h.tensor_tensor(
                B.hit[:].rearrange("p (t e) -> p t e", e=64), msk3, rm_b64,
                op=ALU.is_ge), waits=[v12])
            v14 = E.v(lambda h: h.tensor_add(B.sel[:], B.sel[:], B.hit[:]),
                      waits=[v13])
            v11 = E.v(lambda h: h.scalar_tensor_tensor(B.msk[:], B.hit[:], NEG,
                                                       B.msk[:], op0=ALU.mult,
                                                       op1=ALU.add),
                      waits=[v14])
        v15 = E.v(lambda h: h.tensor_mul(B.hit[:], B.sc[:], B.sel[:]),
                  waits=[v11])
        v16 = E.v(lambda h: h.reduce_sum(B.rs[:],
                                         B.hit[:].rearrange("p (t e) -> p t e",
                                                            e=64),
                                         axis=AX.X),
                  waits=[v15, prev.get("rs")])
        v17 = E.v(lambda h: h.tensor_scalar_add(B.rs[:], B.rs[:], 1e-20),
                  waits=[v16])
        v18 = E.v(lambda h: h.reciprocal(B.rs[:], B.rs[:]), waits=[v17])
        v19 = E.v(lambda h, sl=sl:
                  h.scalar_tensor_tensor(
                      B.combine[:, sl].rearrange("p (t e) -> p t e", e=64),
                      B.hit[:].rearrange("p (t e) -> p t e", e=64), 2.5,
                      rs_b64, op0=ALU.mult, op1=ALU.mult),
                  waits=[v18, tok.get("combine_used")])
        cmb_toks.append(v19)
        prev = {"sc": v15, "sfc": v10, "gs": v6, "hit": v19,
                "msk": v11, "g2": v6, "gm": v10, "rm": v18, "rs": v19,
                "sel": v15}
    tok["combine"] = cmb_toks
    tok.pop("combine_used", None)


def emit_cT(E, B, PS, tok):
    prev_c = tok.get("cmbbf_used")
    ev2 = None
    for tt in range(16):
        v20 = E.v(lambda h, tt=tt:
                  h.tensor_copy(B.cmbbf[:],
                                B.combine[:, tt * 64: tt * 64 + 64]),
                  waits=tok["combine"] + [prev_c])
        tr = E.t(lambda h: h.transpose(PS.tr[0:64, 0:P], B.cmbbf[:],
                                       B.identbf[:]),
                 waits=[v20, tok.get("pstr_free"), tok["ident"]])
        ev2 = E.v(lambda h, tt=tt:
                  h.tensor_copy(B.cT[0:64, tt * P: tt * P + P],
                                PS.tr[0:64, 0:P]),
                  waits=[tr, tok.get("cT_used")])
        tok["pstr_free"] = ev2
        prev_c = tr
    tok["cmbbf_used"] = prev_c
    tok["cT"] = ev2
    tok["lgall_used"] = ev2
    tok.pop("cT_used", None)


def emit_phaseB_seq(E, B, PS, W, tok, with_out=None):
    fence = tok.get("fence", [])
    emit_router_logits(E, B, PS, tok)
    emit_routing_v(E, B, tok)
    xr_toks = []
    st_tok = [tok.get("st0"), tok.get("st1")]
    wup_tok = [None, None]
    wup_free = [list(fence), list(fence)]
    wdn_tok = [None, None, None, None]
    wdn_free = [list(fence) for _ in range(4)]
    acc_rd = tok.get("acc_read")
    hb_free = None
    prev_relu = [None, None]
    prev_sq = [None, None]
    prev_cbc = [None, None]
    upbank = [PS.a, PS.b]
    upbank_free = [tok.get("psa_free"), tok.get("psb_free")]
    dn_free = [None] * 4
    first_write = {}

    def load_expert_weights(eidx, half):
        pe = eidx % 2
        pd = eidx % 4
        if eidx < 4:
            upd, dnd = W["shexp_up"][eidx], W["shexp_dn"][eidx]
        else:
            upd, dnd = W["up_w"][eidx - 4], W["dn_w"][eidx - 4]
        toks = []
        for hh in range(2):
            si = (eidx * 2 + hh) % 2
            ld = E.dy(B.stage[si][:].rearrange("p (k w) -> p k w", w=512),
                      upd[4 * hh: 4 * hh + 4].rearrange("k p w -> p k w"),
                      waits=[st_tok[si]] + list(wup_free[pe]))
            ct = E.v(lambda h, pe=pe, hh=hh, si=si:
                     h.tensor_copy(B.wup[pe][:, hh * T: hh * T + T],
                                   B.stage[si][:]),
                     waits=[ld])
            st_tok[si] = ct
            toks.append(ct)
        for hh in range(2):
            si = (eidx * 2 + hh) % 2
            ld = E.dy(B.stage[si][:].rearrange("p (k w) -> p k w", w=D),
                      dnd[2 * hh: 2 * hh + 2].rearrange("k p w -> p k w"),
                      waits=[st_tok[si]] + list(wdn_free[pd]))
            ct = E.v(lambda h, pd=pd, hh=hh, si=si:
                     h.tensor_copy(B.wdn[pd][:, hh * T: hh * T + T],
                                   B.stage[si][:]),
                     waits=[ld])
            st_tok[si] = ct
            toks.append(ct)
        wup_tok[pe] = toks[1]
        wdn_tok[pd] = toks[3]

    for half in range(2):
        t0 = half * 1024
        for gi, grp in enumerate(GRPS):
            if half == 0 and gi == 1:
                emit_cT(E, B, PS, tok)
            up_mm_last = None
            for j, eidx in enumerate(grp):
                load_expert_weights(eidx, half)
                pe = eidx % 2
                for nt in range(2):
                    c0 = t0 + nt * 512
                    if eidx >= 4:
                        e_r = eidx - 4
                        cb = j % 2
                        cm = E.t(lambda h, e_r=e_r, c0=c0:
                                 h.matmul(PS.c[:, 0:512],
                                          B.onehot[0:64, e_r * P: e_r * P + P],
                                          B.cT[0:64, c0: c0 + 512],
                                          start=True, stop=True),
                                 waits=[tok["cT"], tok["onehot"],
                                        tok.get("psc_free")])
                        cv = E.v(lambda h, cb=cb:
                                 h.tensor_copy(B.cbc[cb][:], PS.c[:, 0:512]),
                                 waits=[cm, prev_cbc[cb]])
                        tok["psc_free"] = cv
                    for ch in range(4):
                        ub = ch % 2
                        mm = None
                        for kc in range(DC):
                            mm = E.t(
                                lambda h, pe=pe, ch=ch, kc=kc, c0=c0, ub=ub:
                                h.matmul(upbank[ub][:, 0:512],
                                         B.wup[pe][:, kc * 512 + ch * P:
                                                    kc * 512 + ch * P + P],
                                         B.xbf[:, kc * T + c0: kc * T + c0 + 512],
                                         start=(kc == 0), stop=(kc == DC - 1)),
                                waits=[wup_tok[pe], upbank_free[ub],
                                       tok["xbf"], tok["xbf2"]]
                                if kc == 0 else [])
                        up_mm_last = mm
                        wup_free[pe] = [mm]
                        rb = ch % 2
                        rl = E.s(lambda h, rb=rb, ub=ub:
                                 h.activation(B.rbuf[rb][:], upbank[ub][:, 0:512],
                                              AF.Relu),
                                 waits=[mm, prev_relu[rb]])
                        upbank_free[ub] = rl
                        slot = ((j * 4 + ch) * 2 + nt) * 512
                        if eidx < 4:
                            sqv = E.g(lambda h, rb=rb, slot=slot:
                                      h.tensor_mul(B.hbuf[:, slot: slot + 512],
                                                   B.rbuf[rb][:], B.rbuf[rb][:]),
                                      waits=[rl, hb_free])
                            prev_relu[rb] = sqv
                            prev_sq[rb] = sqv
                        else:
                            sqv = E.v(lambda h, rb=rb:
                                      h.tensor_mul(B.sqe[rb][:], B.rbuf[rb][:],
                                                   B.rbuf[rb][:]),
                                      waits=[rl, prev_sq[rb]])
                            prev_relu[rb] = sqv
                            cmu = E.g(lambda h, rb=rb, cb=(j % 2), slot=slot:
                                      h.tensor_tensor(B.hbuf[:, slot: slot + 512],
                                                      B.sqe[rb][:], B.cbc[cb][:],
                                                      op=ALU.mult),
                                      waits=[sqv, cv, hb_free])
                            prev_sq[rb] = cmu
                            prev_cbc[j % 2] = cmu
            for nt in range(2):
                for oc in range(DC):
                    db = oc % 4
                    mm = None
                    for j in range(4):
                        for ch in range(4):
                            slot = ((j * 4 + ch) * 2 + nt) * 512
                            first = (j == 0 and ch == 0)
                            mm = E.t(
                                lambda h, j=j, ch=ch, oc=oc, slot=slot, db=db,
                                first=first:
                                h.matmul(PS.big[:, db * 512: db * 512 + 512],
                                         B.wdn[j][:, ch * D + oc * P:
                                                   ch * D + oc * P + P],
                                         B.hbuf[:, slot: slot + 512],
                                         start=first,
                                         stop=(j == 3 and ch == 3)),
                                waits=[wdn_tok[0], wdn_tok[1], wdn_tok[2],
                                       wdn_tok[3], dn_free[db], prev_sq[0],
                                       prev_sq[1],
                                       E_last_pool(E)] if first else [])
                    key = (half, nt, oc)
                    if gi == 0:
                        ad = E.v(lambda h, oc=oc, nt=nt, db=db:
                                 h.tensor_copy(
                                     B.acc[:, (oc * 2 + nt) * 512:
                                           (oc * 2 + nt) * 512 + 512],
                                     PS.big[:, db * 512: db * 512 + 512]),
                                 waits=[mm, acc_rd])
                    else:
                        ad = E.v(lambda h, oc=oc, nt=nt, db=db:
                                 h.tensor_tensor(
                                     B.acc[:, (oc * 2 + nt) * 512:
                                           (oc * 2 + nt) * 512 + 512],
                                     PS.big[:, db * 512: db * 512 + 512],
                                     B.acc[:, (oc * 2 + nt) * 512:
                                           (oc * 2 + nt) * 512 + 512],
                                     op=ALU.add),
                                 waits=[mm, first_write[key]])
                    first_write[key] = ad
                    dn_free[db] = ad
            hb_free = mm
            for pd in range(4):
                wdn_free[pd] = [mm]
        for oc in range(DC):
            for nt in range(2):
                ntg = half * 2 + nt
                xr_toks.append(E.v(
                    lambda h, oc=oc, nt=nt, ntg=ntg:
                    h.tensor_tensor(
                        B.xbf[:, oc * T + ntg * 512: oc * T + ntg * 512 + 512],
                        B.acc[:, (oc * 2 + nt) * 512: (oc * 2 + nt) * 512 + 512],
                        B.xbf[:, oc * T + ntg * 512: oc * T + ntg * 512 + 512],
                        op=ALU.add),
                    waits=[first_write[(half, nt, oc)], tok["xbf"],
                           tok["xbf2"]]))
        acc_rd = xr_toks[-1]
    tok["acc_read"] = xr_toks[-1]
    tok["combine_used"] = xr_toks[-1]
    tok["cT_used"] = xr_toks[-1]
    tok["st0"] = st_tok[0]
    tok["st1"] = st_tok[1]
    emit_ln(E, B, PS, tok, xr_toks, with_out)


def E_last_pool(E):
    return ("G", E.cnt["G"])


def build_phaseB(nseq=NSEQ):
    nc = bass.Bass("TRN2", target_bir_lowering=False)
    xin = nc.dram_tensor("xin", [nseq, D, T], F32, kind="ExternalInput").ap()
    xout = nc.dram_tensor("xout", [nseq, D, T], F32, kind="ExternalOutput").ap()
    W = {}
    for name, shp in [("routerT", (DC, P, 64)), ("corr_rep", (P, 64)),
                      ("up_w", (64, DC, P, 512)), ("dn_w", (64, 4, P, D)),
                      ("shexp_up", (4, DC, P, 512)), ("shexp_dn", (4, 4, P, D)),
                      ("onehot_f", (64, 64 * P)), ("ident_f", (P, P))]:
        W[name] = nc.dram_tensor(name, list(shp), F32, kind="ExternalInput").ap()

    with ExitStack() as st:
        B = NS()
        alloc_phaseB(nc, st, B)
        PS = NS()
        PS.big = st.enter_context(nc.psum_tensor("psbig", [P, 2048], F32))
        PS.a = st.enter_context(nc.psum_tensor("psa", [P, 512], F32))
        PS.b = st.enter_context(nc.psum_tensor("psb", [P, 512], F32))
        PS.c = st.enter_context(nc.psum_tensor("psc", [P, 512], F32))
        PS.tr = st.enter_context(nc.psum_tensor("pstr", [P, 1024], BF16))
        sems = {}
        for s in ("T", "V", "S", "G", "Y0", "Y1", "Y2", "Y3",
                  "GD0", "GD1", "GD2", "GD3"):
            sems[s] = st.enter_context(nc.semaphore(s))
        block = st.enter_context(nc.Block())
        E = Emitter(nc, sems)

        def prog():
            tok = {}
            c1 = E.v(lambda h: h.memset(B.ones1[:], 1.0))
            c2 = E.v(lambda h: h.memset(B.ones128[:], 1.0), waits=[c1])
            tok["ones"] = c2
            ld = E.dy(B.stage[0][:, 0:P], W["ident_f"])
            tok["ident"] = E.v(lambda h: h.tensor_copy(B.identbf[:],
                                                       B.stage[0][:, 0:P]),
                               waits=[ld, c2])
            ld2 = E.dy(B.stage[1][:, 0:64], W["corr_rep"])
            tok["corr"] = E.v(lambda h: h.tensor_copy(B.corr[:],
                                                      B.stage[1][:, 0:64]),
                              waits=[ld2])
            rt = None
            for kc in range(DC):
                si = kc % 2
                w0 = [tok["ident"] if si == 0 else tok["corr"]] if kc < 2 else                      [rt]
                ld3 = E.dy(B.stage[si][:, 0:64], W["routerT"][kc], waits=w0)
                rt = E.v(lambda h, kc=kc, si=si:
                         h.tensor_copy(B.routerF[:, kc * 64: kc * 64 + 64],
                                       B.stage[si][:, 0:64]), waits=[ld3])
            tok["router"] = rt
            oh = rt
            for c in range(4):
                si = c % 2
                ld4 = E.dy(B.stage[si][0:64, 0:T],
                           W["onehot_f"][0:64, c * T:(c + 1) * T], waits=[oh])
                oh = E.v(lambda h, c=c, si=si:
                         h.tensor_copy(B.onehot[0:64, c * T:(c + 1) * T],
                                       B.stage[si][0:64, 0:T]), waits=[ld4])
            tok["onehot"] = oh
            tok["st0"] = oh
            tok["st1"] = oh
            for s_ in range(nseq):
                emit_load_x(E, B, xin, s_, tok)
                emit_phaseB_seq(E, B, PS, W, tok, with_out=xout[s_, :, :])
            for k in ("GD0", "GD1", "GD2", "GD3"):
                E.op("gpsimd",
                     lambda h, k=k: h.wait_ge(sems[k], E.cnt[k]), inc=0)

        for name in Emitter.ENGINES:
            getattr(block, name)(lambda h, name=name: (E.start(name, h), prog()))
    return nc



def alloc_phaseC(nc, st, bufs):
    sb = lambda n, s, dt: st.enter_context(nc.sbuf_tensor(n, s, dt))
    B = bufs
    B.xbf = sb("xbf", [P, DC * T], BF16)
    B.stage = [sb(f"stage{i}", [P, T], F32) for i in range(2)]
    B.sq = [sb(f"sq{i}", [P, 512], BF16) for i in range(2)]
    B.statrow = sb("statrow", [1, 512], F32)
    B.statbf = sb("statbf", [1, 512], BF16)
    B.statbf2 = sb("statbf2", [1, 512], BF16)
    B.ones1 = sb("ones1", [1, P], BF16)
    B.ones128 = sb("ones128", [P, 1], BF16)
    B.identbf = sb("identbf", [P, P], BF16)
    B.trilT = sb("trilT", [64, 16 * 64], BF16)
    B.wqk = sb("wqk", [P, DC * P], BF16)
    B.tbuf = sb("tbuf", [P, DC * T], BF16)
    B.vhC = sb("vhC", [P, 16 * 16 * 66], BF16)
    B.probsT = sb("probsT", [P, 8 * 512], BF16)
    B.xst = [sb(f"xst{i}", [P, T], BF16) for i in range(2)]
    B.taug = [sb(f"taug{i}", [P, 512], BF16) for i in range(2)]
    B.mbuf = sb("mbuf", [P, 16], F32)
    B.mbufb = sb("mbufb", [P, 16], BF16)
    B.mtr = sb("mtr", [16, P], BF16)
    B.mrow = [sb(f"mrow{i}", [1, T], BF16) for i in range(2)]
    B.wvF = sb("wvF", [P, DC * D], BF16)
    B.woF = sb("woF", [P, DC * D], BF16)
    B.bvrep = sb("bvrep", [P, D], BF16)
    B.boF = sb("boF", [P, DC], F32)
    B.rrow = sb("rrow", [1, 512], BF16)
    B.rbc = sb("rbc", [64, 512], BF16)


def emit_phaseC_prologue(E, B, PS, W, tok):
    c1 = E.v(lambda h: h.memset(B.ones1[:], 1.0))
    c2 = E.v(lambda h: h.memset(B.ones128[:], 1.0), waits=[c1])
    tok["ones"] = c2
    ld = E.dy(B.stage[0][:, 0:P], W["ident_f"])
    tok["ident"] = E.v(lambda h: h.tensor_copy(B.identbf[:], B.stage[0][:, 0:P]),
                       waits=[ld, c2])
    ld2 = E.dy(B.stage[1][:, 0:1024], W["trilT_f"])
    ttr = E.v(lambda h: h.tensor_copy(B.trilT[:], B.stage[1][0:64, 0:1024]),
              waits=[ld2])
    mz = E.v(lambda h: h.memset(B.wqk[:], 0.0), waits=[ttr])
    wq = None
    for h_ in range(H):
        mm = E.t(lambda h, h_=h_:
                 h.matmul(PS.a[0:64, 0:64],
                          B.trilT[0:64, h_ * 64: h_ * 64 + 64],
                          B.trilT[0:64, h_ * 64: h_ * 64 + 64],
                          start=True, stop=True),
                 waits=[mz, tok.get("psa_free"), wq])
        r0 = (h_ % 2) * 64
        wq = E.v(lambda h, h_=h_, r0=r0:
                 h.tensor_scalar_mul(
                     B.wqk[r0:r0 + 64, (h_ // 2) * P + r0: (h_ // 2) * P + r0 + 64],
                     PS.a[0:64, 0:64], 0.125),
                 waits=[mm])
    tok["wqk"] = wq
    tok["psa_free"] = wq
    jobs = [("wvT", B.wvF, D), ("woT2", B.woF, D)]
    last = [tok["ident"], ttr]
    for name, dst, cw in jobs:
        for c in range(DC):
            i = c % 2
            ldw = E.dy(B.stage[i][:, 0:cw], W[name][c], waits=[last[i]])
            last[i] = E.v(lambda h, dst=dst, c=c, cw=cw, i=i:
                          h.tensor_copy(dst[:, c * cw:(c + 1) * cw],
                                        B.stage[i][:, 0:cw]), waits=[ldw])
    ldb = E.dy(B.stage[0][:, 0:D], W["bv_rep"], waits=[last[0]])
    bv = E.v(lambda h: h.tensor_copy(B.bvrep[:], B.stage[0][:, 0:D]),
             waits=[ldb])
    ldb2 = E.dy(B.boF[:], W["boF_in"], waits=[])
    tok["wsmallC"] = bv
    tok["boF"] = ldb2
    tok["st0"] = bv
    tok["st1"] = last[1]
    o1 = E.v(lambda h: h.memset(B.xst[0][:], 1.0), waits=[bv])
    o2 = E.v(lambda h: h.memset(B.xst[1][:], 1.0), waits=[o1])
    tok["xst_ones"] = o2
    vz = E.v(lambda h: h.memset(B.vhC[:], 1.0), waits=[o2])
    tok["vh_ones"] = vz


def emit_phaseC_seq(E, B, PS, tok, with_out=None):
    fence = tok.get("fence", [])
    xw = [tok["xbf"], tok["xbf2"]]
    banks = [PS.a, PS.b, PS.c]
    bank_free = [tok.get("psa_free"), tok.get("psb_free"), tok.get("psc_free")]
    bi = 0

    def next_bank():
        nonlocal bi
        b = bi % 3
        bi += 1
        return b

    for c in range(DC):
        for nt in range(NT):
            b = next_bank()
            mm = E.t(lambda h, c=c, nt=nt, b=b:
                     h.matmul(banks[b][:, 0:512],
                              B.wqk[:, c * P: c * P + P],
                              B.xbf[:, c * T + nt * 512: c * T + nt * 512 + 512],
                              start=True, stop=True),
                     waits=[tok["wqk"], xw[0], xw[1], bank_free[b]] + fence)
            bank_free[b] = E.v(lambda h, c=c, nt=nt, b=b:
                               h.tensor_copy(B.tbuf[:, c * T + nt * 512:
                                                    c * T + nt * 512 + 512],
                                             banks[b][:, 0:512]),
                               waits=[mm] + fence)
    t_tok = bank_free[(bi - 1) % 3]

    vh_toks = []
    for tt in range(16):
        for dh in range(2):
            b = next_bank()
            mm = None
            for kc in range(DC):
                mm = E.t(lambda h, tt=tt, dh=dh, kc=kc, b=b:
                         h.matmul(banks[b][:, 0:512],
                                  B.xbf[:, kc * T + tt * P: kc * T + tt * P + P],
                                  B.wvF[:, kc * D + dh * 512: kc * D + dh * 512 + 512],
                                  start=(kc == 0), stop=(kc == DC - 1)),
                         waits=[tok["wsmallC"], bank_free[b], tok["vh_ones"]]
                         if kc == 0 else [])
            base = tt * 1056 + dh * 8 * 66
            out3 = B.vhC[:, base: base + 8 * 66].rearrange(
                "p (h w) -> p h w", w=66)[:, :, 0:64]
            ps3 = banks[b].rearrange("p (h w) -> p h w", w=64)
            bv3 = B.bvrep[:, dh * 512: dh * 512 + 512].rearrange(
                "p (h w) -> p h w", w=64)
            bank_free[b] = E.v(lambda h, out3=out3, ps3=ps3, bv3=bv3:
                               h.tensor_tensor(out3, ps3, bv3, op=ALU.add),
                               waits=[mm] + fence)
            vh_toks.append(bank_free[b])
    vh_tok = bank_free[(bi - 1) % 3]

    stage_free = [tok.get("stage_freeC0"), tok.get("stage_freeC1")]
    taug_free = [tok.get("taug_free0"), tok.get("taug_free1")]
    y_toks = []
    half_free = [None, None]
    st = NS()
    st.last_av = tok.get("av_prev")
    st.mh = [None, None]
    st.md = [None, None]

    def emit_mpass(h_):
        mx = tok.get("mbuf_used")
        for qt in range(16):
            rms = []
            for kh in range(2):
                hf = (qt * 2 + kh) % 2
                mm = None
                for kn in range(2):
                    mm = E.t(lambda h, h_=h_, qt=qt, kh=kh, kn=kn, hf=hf:
                             h.matmul(PS.big[:, hf * 1024 + kn * 512:
                                             hf * 1024 + kn * 512 + 512],
                                      B.tbuf[(h_ % 2) * 64:(h_ % 2) * 64 + 64,
                                             (h_ // 2) * T + qt * P:
                                             (h_ // 2) * T + qt * P + P],
                                      B.xbf[(h_ % 2) * 64:(h_ % 2) * 64 + 64,
                                            (h_ // 2) * T + kh * 1024 + kn * 512:
                                            (h_ // 2) * T + kh * 1024 + kn * 512 + 512],
                                      start=True, stop=True),
                             waits=[t_tok, xw[0], xw[1], half_free[hf],
                                    st.last_av])
                rm = E.v(lambda h, qt=qt, kh=kh, hf=hf:
                         h.reduce_max(B.mbufb[0:P, kh: kh + 1],
                                      PS.big[:, hf * 1024: hf * 1024 + 1024],
                                      axis=AX.X),
                         waits=[mm, st.mh[kh], mx, tok.get("mbuf_used")])
                rms.append(rm)
                half_free[hf] = rm
                st.mh[kh] = rm
            mx = E.v(lambda h, qt=qt:
                     h.tensor_tensor(B.mbuf[:, qt: qt + 1],
                                     B.mbufb[0:P, 0:1], B.mbufb[0:P, 1:2],
                                     op=ALU.max),
                     waits=rms + [mx])
        cst = E.v(lambda h: h.tensor_scalar_mul(B.mbufb[:], B.mbuf[:], -1.0),
                  waits=[mx])
        tok["mbuf_used"] = cst
        tr = E.t(lambda h: h.transpose(PS.tr[0:16, 0:P], B.mbufb[:],
                                       B.identbf[:]),
                 waits=[cst, tok.get("pstr_free"), tok["ident"]])
        ev = E.v(lambda h: h.tensor_copy(B.mtr[:], PS.tr[0:16, 0:P]),
                 waits=[tr, tok.get("mtr_used")])
        tok["pstr_free"] = ev
        st.md[h_ % 2] = E.dg(B.mrow[h_ % 2], B.mtr[:],
                             waits=[ev, tok.get(f"mrow_used{h_ % 2}")])
        tok["mtr_used"] = st.md[h_ % 2]

    def emit_attn(h_):
        i = h_ % 2
        xs = E.v(lambda h, i=i, h_=h_:
                 h.tensor_copy(B.xst[i][0:64, :],
                               B.xbf[(h_ % 2) * 64:(h_ % 2) * 64 + 64,
                                     (h_ // 2) * T:(h_ // 2) * T + T]),
                 waits=[xw[0], xw[1], stage_free[i], tok["xst_ones"]])
        prob_tok = {}
        for qg in range(NT):
            i2 = qg % 2
            avb = PS.a if qg % 2 == 0 else PS.c
            tg1 = E.v(lambda h, i2=i2, h_=h_, qg=qg:
                      h.tensor_copy(B.taug[i2][0:64, :],
                                    B.tbuf[(h_ % 2) * 64:(h_ % 2) * 64 + 64,
                                           (h_ // 2) * T + qg * 512:
                                           (h_ // 2) * T + qg * 512 + 512]),
                      waits=[t_tok, taug_free[i2]])
            tg2 = E.v(lambda h, i2=i2, qg=qg, i=i:
                      h.tensor_copy(B.taug[i2][64:65, :],
                                    B.mrow[i][0:1, qg * 512: qg * 512 + 512]),
                      waits=[st.md[i], tg1])
            tok[f"mrow_used{i}"] = tg2
            av = None
            for kb in range(8):
                kts = [2 * kb, 2 * kb + 1]
                hf = kb % 2
                mms = []
                for kt in kts:
                    sl = (hf * 2 + (kt % 2)) * 512
                    mms.append(E.t(
                        lambda h, i=i, i2=i2, kt=kt, sl=sl:
                        h.matmul(PS.big[:, sl:sl + 512],
                                 B.xst[i][0:65, kt * P: kt * P + P],
                                 B.taug[i2][0:65, 0:512],
                                 start=True, stop=True),
                        waits=[xs, tg2, half_free[hf], st.last_av]))
                rg = kts[0] % 8
                region_free = av if kts[0] >= 8 else st.last_av
                ex = E.s(
                    lambda h, rg=rg, sl0=hf * 1024:
                    h.activation(B.probsT[:, rg * 512: rg * 512 + 1024],
                                 PS.big[:, sl0: sl0 + 1024], AF.Exp),
                    waits=mms + [region_free])
                half_free[hf] = ex
                for kt in kts:
                    prob_tok[kt] = ex
                if kts[-1] == 7:
                    for kt2 in range(8):
                        av = E.t(lambda h, h_=h_, kt2=kt2, avb=avb,
                                 first=(kt2 == 0):
                                 h.matmul(avb[0:65, 0:512],
                                          B.vhC[:, kt2 * 1056 + h_ * 66:
                                                kt2 * 1056 + h_ * 66 + 65],
                                          B.probsT[:, kt2 * 512: kt2 * 512 + 512],
                                          start=first, stop=False),
                                 waits=([prob_tok[k2] for k2 in range(8)] +
                                        [tok.get(f"yevC{qg % 2}"),
                                         tok.get("rrow_used")] + vh_toks)
                                 if kt2 == 0 else [])
            for kt in range(8, 16):
                av = E.t(lambda h, h_=h_, kt=kt, avb=avb, last=(kt == 15):
                         h.matmul(avb[0:65, 0:512],
                                  B.vhC[:, kt * 1056 + h_ * 66:
                                        kt * 1056 + h_ * 66 + 65],
                                  B.probsT[:, (kt % 8) * 512: (kt % 8) * 512 + 512],
                                  start=False, stop=last),
                         waits=[prob_tok[k2] for k2 in range(8, 16)]
                         if kt == 8 else [])
            st.last_av = av
            taug_free[i2] = av
            rr0 = E.s(lambda h, avb=avb: h.activation(B.rrowf[:],
                                                      avb[64:65, 0:512],
                                                      AF.Ln),
                      waits=[av, tok.get("rbc_used"), tok.get("rrowf_used")])
            rr = E.s(lambda h: h.activation(B.rrow[:], B.rrowf[:], AF.Exp,
                                            scale=-1.0),
                     waits=[rr0])
            tok["rrowf_used"] = rr
            bc = E.t(lambda h: h.matmul(PS.b[0:64, 0:512], B.ones1[:, 0:64],
                                        B.rrow[:], start=True, stop=True),
                     waits=[rr, tok.get("rbc_used")])
            rbc_ev = E.v(lambda h: h.tensor_copy(B.rbc[:], PS.b[0:64, 0:512]),
                         waits=[bc, tok.get("yev")])
            tok["rbc_used"] = rbc_ev
            tok["rrow_used"] = rbc_ev
            yev = E.v(lambda h, h_=h_, qg=qg, avb=avb:
                      h.tensor_tensor(B.tbuf[(h_ % 2) * 64:(h_ % 2) * 64 + 64,
                                             (h_ // 2) * T + qg * 512:
                                             (h_ // 2) * T + qg * 512 + 512],
                                      avb[0:64, 0:512], B.rbc[:],
                                      op=ALU.mult), waits=[rbc_ev])
            tok[f"yevC{qg % 2}"] = yev
            tok["yev"] = yev
            y_toks.append(yev)
        stage_free[i] = st.last_av

    emit_mpass(0)
    for h_ in range(H):
        if h_ + 1 < H:
            emit_mpass(h_ + 1)
        emit_attn(h_)

    last_av = st.last_av
    tok["av_prev"] = last_av
    tok["stage_freeC0"] = stage_free[0]
    tok["stage_freeC1"] = stage_free[1]
    tok["taug_free0"] = taug_free[0]
    tok["taug_free1"] = taug_free[1]

    emit_out_ln(E, B, PS, tok, w_img=B.woF, w_tok=tok["wsmallC"],
                src=B.tbuf, src_toks=[y_toks[-1]], nk=DC, with_out=with_out,
                bank_free=[bank_free[0], bank_free[1], bank_free[2]],
                bias_cols=B.boF, bias_tok=[tok["boF"]])


def build_phaseC(nseq=NSEQ):
    nc = bass.Bass("TRN2", target_bir_lowering=False)
    xin = nc.dram_tensor("xin", [nseq, D, T], F32, kind="ExternalInput").ap()
    xout = nc.dram_tensor("xout", [nseq, D, T], F32, kind="ExternalOutput").ap()
    W = {}
    for name, shp in [("trilT_f", (P, 1024)), ("wvT", (DC, P, D)),
                      ("woT2", (DC, P, D)), ("bv_rep", (P, D)),
                      ("boF_in", (P, DC)), ("ident_f", (P, P))]:
        W[name] = nc.dram_tensor(name, list(shp), F32, kind="ExternalInput").ap()

    with ExitStack() as st:
        B = NS()
        alloc_phaseC(nc, st, B)
        PS = NS()
        PS.big = st.enter_context(nc.psum_tensor("psbig", [P, 2048], F32))
        PS.a = st.enter_context(nc.psum_tensor("psa", [P, 512], F32))
        PS.b = st.enter_context(nc.psum_tensor("psb", [P, 512], F32))
        PS.c = st.enter_context(nc.psum_tensor("psc", [P, 512], F32))
        PS.tr = st.enter_context(nc.psum_tensor("pstr", [P, 1024], BF16))
        sems = {}
        for s in ("T", "V", "S", "G", "Y0", "Y1", "Y2", "Y3",
                  "GD0", "GD1", "GD2", "GD3"):
            sems[s] = st.enter_context(nc.semaphore(s))
        block = st.enter_context(nc.Block())
        E = Emitter(nc, sems)

        def prog():
            tok = {}
            emit_phaseC_prologue(E, B, PS, W, tok)
            for s_ in range(nseq):
                emit_load_x(E, B, xin, s_, tok)
                emit_phaseC_seq(E, B, PS, tok, with_out=xout[s_, :, :])
            for k in ("GD0", "GD1", "GD2", "GD3"):
                E.op("gpsimd",
                     lambda h, k=k: h.wait_ge(sems[k], E.cnt[k]), inc=0)

        for name in Emitter.ENGINES:
            getattr(block, name)(lambda h, name=name: (E.start(name, h), prog()))
    return nc



def alloc_phaseD(nc, st, bufs):
    sb = lambda n, s, dt: st.enter_context(nc.sbuf_tensor(n, s, dt))
    B = bufs
    B.xbf = sb("xbf", [P, DC * T], BF16)
    B.stage = [sb(f"stage{i}", [P, T], F32) for i in range(2)]
    B.sq = [sb(f"sq{i}", [P, 512], BF16) for i in range(2)]
    B.statrow = sb("statrow", [1, 512], F32)
    B.statbf = sb("statbf", [1, 512], BF16)
    B.statbf2 = sb("statbf2", [1, 512], BF16)
    B.ones1 = sb("ones1", [1, P], BF16)
    B.ones128 = sb("ones128", [P, 1], BF16)
    B.identbf = sb("identbf", [P, P], BF16)
    B.wbig = sb("wbig", [P, DC * D], BF16)
    B.wbig2 = sb("wbig2", [P, DC * D], BF16)
    B.wv1 = sb("wv1", [P, DC * 512], BF16)
    B.wv2 = sb("wv2", [P, DC * 512], BF16)
    B.xV = sb("xV", [P, 4 * T], BF16)
    B.xVi = sb("xVi", [P, 4 * T], BF16)
    B.yW = sb("yW", [P, 4 * T], BF16)
    B.cum = sb("cum", [P, 4 * T], BF16)
    B.wrep = sb("wrep", [P, 4 * T], BF16)
    B.h8 = sb("h8", [P, DC * P], BF16)
    B.o1 = sb("o1", [P, DC * 512], BF16)
    B.zer = sb("zer", [P, 512], BF16)
    B.carry = sb("carry", [P, 4], F32)
    B.wtok = sb("wtok", [P, 16 * 8], F32)
    B.wtb = sb("wtb", [P, 8], BF16)
    B.wT = sb("wT", [8, T], BF16)
    B.rw2F = sb("rw2F", [P, DC * 8], BF16)
    B.b1F = sb("b1F", [P, DC], F32)
    B.b2eb = sb("b2eb", [P, 8], F32)
    B.alphac = sb("alphac", [P, 1], F32)
    B.bfw = sb("bfw", [8, D], BF16)
    B.binv = sb("binv", [8, D], BF16)
    B.rep8 = sb("rep8", [8, 512], BF16)
    B.invn = sb("invn", [P, T], BF16)
    B.lgD = sb("lgD", [P, 8], F32)
    B.rmD = sb("rmD", [P, 1], F32)


def emit_fold(E, B, PS, lhs_img, lhs_tok, rhs_img, rhs_tok, out_img, dout,
              free_tok, tag, out_free=None):
    toks = []
    ti = 0
    for dc in range(DC):
        for n in range(dout // 512):
            bk = ti % 4
            ti += 1
            mm = None
            for kc in range(DC):
                mm = E.t(lambda h, dc=dc, n=n, kc=kc, bk=bk:
                         h.matmul(PS.big[:, bk * 512: bk * 512 + 512],
                                  lhs_img[:, kc * D + dc * P: kc * D + dc * P + P],
                                  rhs_img[:, kc * dout + n * 512:
                                          kc * dout + n * 512 + 512],
                                  start=(kc == 0), stop=(kc == DC - 1)),
                         waits=[lhs_tok, rhs_tok, free_tok,
                                toks[-4] if len(toks) >= 4 else None]
                         if kc == 0 else [])
            toks.append(E.v(lambda h, dc=dc, n=n, bk=bk:
                            h.tensor_copy(out_img[:, dc * dout + n * 512:
                                                  dc * dout + n * 512 + 512],
                                          PS.big[:, bk * 512: bk * 512 + 512]),
                            waits=[mm, out_free]))
    return toks[-1]


def loadw(E, B, W, name, dst, cw, tok, key=None):
    nch = W[name].shape[0]
    last = None
    for c in range(nch):
        i = c % 2
        ld = E.dy(B.stage[i][:, 0:cw], W[name][c],
                  waits=[tok.get("st0") if i == 0 else tok.get("st1")] +
                  list(tok.get("fence", [])))
        ct = E.v(lambda h, dst=dst, c=c, cw=cw, i=i:
                 h.tensor_copy(dst[:, c * cw:(c + 1) * cw],
                               B.stage[i][:, 0:cw]), waits=[ld])
        if i == 0:
            tok["st0"] = ct
        else:
            tok["st1"] = ct
        last = ct
    if key:
        tok[key] = last
    return last


def emit_phaseD_prologue(E, B, PS, W, scratch, tok):
    c1 = E.v(lambda h: h.memset(B.ones1[:], 1.0))
    c2 = E.v(lambda h: h.memset(B.ones128[:], 1.0), waits=[c1])
    c3 = E.v(lambda h: h.memset(B.zer[:], 0.0), waits=[c2])
    tok["ones"] = c2
    tok["zer"] = c3
    ld = E.dy(B.stage[0][:, 0:P], W["ident_f"])
    tok["ident"] = E.v(lambda h: h.tensor_copy(B.identbf[:], B.stage[0][:, 0:P]),
                       waits=[ld, c3])
    tok["st0"] = tok["ident"]
    tok["rw2"] = loadw(E, B, W, "rw2T_in", B.rw2F, 8, tok)
    tok["invn_tok"] = loadw(E, B, W, "invn_in", B.invn, T, tok)
    ld8 = E.dy(B.stage[0][0:8, 0:512], W["rep8_in"], waits=[tok.get("st0")])
    tok["rep8"] = E.v(lambda h: h.tensor_copy(B.rep8[:], B.stage[0][0:8, 0:512]),
                      waits=[ld8])
    tok["st0"] = tok["rep8"]
    ld9 = E.dy(B.stage[1][0:8, 0:D], W["bfw_in"], waits=[tok.get("st1")])
    tok["bfw"] = E.v(lambda h: h.tensor_copy(B.bfw[:], B.stage[1][0:8, 0:D]),
                     waits=[ld9])
    tok["st1"] = tok["bfw"]
    ldb = E.dy(B.b1F[:], W["b1F_in"], waits=[tok.get("st0")])
    ldb2 = E.dy(B.stage[0][:, 0:8], W["b2rep_in"], waits=[tok.get("st0")])
    ldb3 = E.dy(B.stage[1][:, 0:8], W["ebrep_in"], waits=[tok.get("st1")])
    b2 = E.v(lambda h: h.tensor_add(B.b2eb[:], B.stage[0][:, 0:8],
                                    B.stage[1][:, 0:8]), waits=[ldb2, ldb3])
    lda = E.dy(B.alphac[:], W["alpha_in"], waits=[])
    tok["b1F"] = ldb
    tok["b2eb"] = b2
    tok["alpha"] = lda
    tok["st0"] = b2
    tok["st1"] = b2
    ldc = E.dy(B.stage[0][0:8, 0:D], W["binv_in"], waits=[tok.get("st0")])
    bi = E.v(lambda h: h.tensor_scalar_mul(B.binv[:], B.stage[0][0:8, 0:D],
                                           B.alphac[0:8, 0:1]),
             waits=[ldc, lda])
    tok["binv"] = bi
    tok["st0"] = bi
    t1 = loadw(E, B, W, "wq_n", B.wbig, D, tok)
    t2 = loadw(E, B, W, "vfF", B.wv1, 512, tok)
    f1 = emit_fold(E, B, PS, B.wbig, t1, B.wv1, t2, B.wv2, 512, None, "vf")
    d1 = E.dg(scratch["d_vf"], B.wv2[:], waits=[f1])
    t3 = loadw(E, B, W, "rw1T", B.wbig2, D, tok)
    dlast = [None, None, None, None]
    ti = 0
    f2 = f1
    for dc in range(DC):
        for n in range(2):
            bk = ti % 4
            mm = None
            for kc in range(DC):
                mm = E.t(lambda h, dc=dc, n=n, kc=kc, bk=bk:
                         h.matmul(PS.big[:, bk * 512: bk * 512 + 512],
                                  B.wbig[:, kc * D + dc * P: kc * D + dc * P + P],
                                  B.wbig2[:, kc * D + n * 512: kc * D + n * 512 + 512],
                                  start=(kc == 0), stop=(kc == DC - 1)),
                         waits=[t1, t3, f1, f2] if kc == 0 else [])
            ev = E.v(lambda h, bk=bk:
                     h.tensor_copy(B.wv1[:, bk * 512: bk * 512 + 512],
                                   PS.big[:, bk * 512: bk * 512 + 512]),
                     waits=[mm, dlast[bk], d1])
            dlast[bk] = E.dg(scratch["d_rw1"][:, dc * D + n * 512:
                                              dc * D + n * 512 + 512],
                             B.wv1[:, bk * 512: bk * 512 + 512], waits=[ev])
            f2 = ev
            ti += 1
    d2 = dlast[(ti - 1) % 4]
    t5 = loadw(E, B, W, "winv_n", B.wbig2, D, tok)
    t4 = loadw(E, B, W, "vroleIF", B.wv1, 512, tok)
    f3 = emit_fold(E, B, PS, B.wbig2, t5, B.wv1, t4, B.wv2, 512,
                   dlast[0], "ivf", out_free=d1)
    for dd in dlast:
        f3 = f3
    f4 = emit_fold(E, B, PS, B.wbig, t1, B.wv2, f3, B.wv1, 512, f3, "vi",
                   out_free=d2)
    d3 = E.dg(scratch["d_vi"], B.wv1[:], waits=[f4])
    t6 = loadw(E, B, W, "wk_n", B.wbig, D, tok)
    t7 = loadw(E, B, W, "wfF", B.wv2, 512, tok)
    f5 = emit_fold(E, B, PS, B.wbig, t6, B.wv2, t7, B.wv1, 512, f4,
                   "wf", out_free=d3)
    d4 = E.dg(scratch["d_wf"], B.wv1[:], waits=[f5])
    t8 = loadw(E, B, W, "wroleIF", B.wv2, 512, tok)
    f6 = emit_fold(E, B, PS, B.wbig2, t5, B.wv2, t8, B.wv1, 512, f5, "iwf",
                   out_free=d4)
    f7 = emit_fold(E, B, PS, B.wbig, t6, B.wv1, f6, B.wv2, 512, f6, "wi",
                   out_free=d4)
    d5 = E.dg(scratch["d_wi"], B.wv2[:], waits=[f7])
    tok["folds"] = f7
    tok["folds_dma"] = [d1, d2, d3, d4, d5]
    tok["st0"] = f7
    tok["st1"] = f7


def emit_phaseD_seq(E, B, PS, W, scratch, tok, with_out=None):
    fence = tok.get("fence", [])
    xw = [tok["xbf"], tok["xbf2"]]
    banks = [PS.a, PS.b, PS.c]
    bank_free = [tok.get("psa_free"), tok.get("psb_free"), tok.get("psc_free")]
    bi = 0

    def next_bank():
        nonlocal bi
        b = bi % 3
        bi += 1
        return b

    ld = E.dy(B.wbig[:], scratch["d_rw1"],
              waits=tok["folds_dma"] + [tok.get("wbig_free")] + fence)
    prevh = None
    prevw = {}
    for tt in range(16):
        b = next_bank()
        ht = None
        for oc in range(DC):
            mm = None
            for kc in range(DC):
                mm = E.t(lambda h, tt=tt, oc=oc, kc=kc, b=b:
                         h.matmul(banks[b][:, 0:P],
                                  B.wbig[:, kc * D + oc * P: kc * D + oc * P + P],
                                  B.xbf[:, kc * T + tt * P: kc * T + tt * P + P],
                                  start=(kc == 0), stop=(kc == DC - 1)),
                         waits=[ld, xw[0], xw[1], bank_free[b]] + fence
                         if kc == 0 else [])
            hh = E.s(lambda h, oc=oc, b=b:
                     h.activation(B.h8[:, oc * P: oc * P + P],
                                  banks[b][:, 0:P], AF.Gelu,
                                  bias=B.b1F[:, oc: oc + 1]),
                     waits=[mm, tok["b1F"], prevh])
            bank_free[b] = hh
            ht = hh
        prevh = ht
        b2 = next_bank()
        mm = None
        for kc in range(DC):
            mm = E.t(lambda h, kc=kc, b2=b2:
                     h.matmul(banks[b2][:, 0:8],
                              B.h8[:, kc * P: kc * P + P],
                              B.rw2F[:, kc * 8: kc * 8 + 8],
                              start=(kc == 0), stop=(kc == DC - 1)),
                     waits=[ht, bank_free[b2], tok.get("rw2")]
                     if kc == 0 else [])
        l0 = E.v(lambda h, b2=b2: h.tensor_add(B.lgD[:], banks[b2][:, 0:8],
                                               B.b2eb[:]),
                 waits=[mm, tok["b2eb"], prevw.get("lg")] + fence)
        bank_free[b2] = l0
        l1 = E.v(lambda h: h.reduce_max(B.rmD[:], B.lgD[:], axis=AX.X),
                 waits=[l0, prevw.get("rm")])
        l2 = E.v(lambda h: h.tensor_scalar_mul(B.rmD[:], B.rmD[:], -1.0),
                 waits=[l1])
        l3 = E.s(lambda h: h.activation(B.lgD[:], B.lgD[:], AF.Exp,
                                        bias=B.rmD[:, 0:1]), waits=[l2])
        l4 = E.v(lambda h: h.reduce_sum(B.rmD[:], B.lgD[:], axis=AX.X),
                 waits=[l3])
        l5 = E.v(lambda h: h.reciprocal(B.rmD[:], B.rmD[:]), waits=[l4])
        l6 = E.v(lambda h, tt=tt:
                 h.tensor_scalar_mul(B.wtok[:, tt * 8: tt * 8 + 8],
                                     B.lgD[:], B.rmD[:, 0:1]),
                 waits=[l5, tok.get("wtok_used")])
        l7b = E.v(lambda h, tt=tt:
                  h.tensor_copy(B.wtb[:], B.wtok[:, tt * 8: tt * 8 + 8]),
                  waits=[l6, prevw.get("wtb")])
        tr = E.t(lambda h: h.transpose(PS.tr[0:8, 0:P], B.wtb[:], B.identbf[:]),
                 waits=[l7b, tok.get("pstr_free"), tok["ident"]])
        ev = E.v(lambda h, tt=tt:
                 h.tensor_copy(B.wT[:, tt * P: tt * P + P], PS.tr[0:8, 0:P]),
                 waits=[tr, tok.get("wT_used")])
        tok["pstr_free"] = ev
        prevw = {"lg": l6, "rm": l6, "wtb": tr}
        tok["wtok_wr"] = l6
    wT_tok = ev
    tok.pop("wT_used", None)
    tok.pop("wtok_used", None)

    wrep_toks = []
    for c in range(4):
        for nt in range(NT):
            b = next_bank()
            mm = E.t(lambda h, c=c, nt=nt, b=b:
                     h.matmul(banks[b][:, 0:512],
                              B.rep8[:, c * P: c * P + P],
                              B.wT[:, nt * 512: nt * 512 + 512],
                              start=True, stop=True),
                     waits=[wT_tok, tok["rep8"], bank_free[b]])
            bank_free[b] = E.v(lambda h, c=c, nt=nt, b=b:
                               h.tensor_copy(B.wrep[:, c * T + nt * 512:
                                                    c * T + nt * 512 + 512],
                                             banks[b][:, 0:512]),
                               waits=[mm, tok.get("wrep_used")])
            wrep_toks.append(bank_free[b])
    wrep_tok = wrep_toks[-1]
    tok.pop("wrep_used", None)

    banks4 = [PS.a[:, 0:512], PS.b[:, 0:512], PS.c[:, 0:512],
              PS.big[:, 0:512]]
    b4_free = [bank_free[0], bank_free[1], bank_free[2], None]

    def xv_pass(eff_scr, dst, dst_used, wslot, wfree):
        lw = E.dy(wslot[:], eff_scr, waits=[wfree] + fence)
        last = None
        for c in range(4):
            mm4 = [None] * 4
            for kc in range(DC):
                for nt in range(NT):
                    mm4[nt] = E.t(
                        lambda h, c=c, nt=nt, kc=kc, wslot=wslot:
                        h.matmul(banks4[nt],
                                 wslot[:, kc * 512 + c * P:
                                       kc * 512 + c * P + P],
                                 B.xbf[:, kc * T + nt * 512:
                                       kc * T + nt * 512 + 512],
                                 start=(kc == 0), stop=(kc == DC - 1)),
                        waits=[lw, xw[0], xw[1], b4_free[nt]]
                        if kc == 0 else [])
            for nt in range(NT):
                b4_free[nt] = E.v(
                    lambda h, c=c, nt=nt, dst=dst:
                    h.tensor_copy(dst[:, c * T + nt * 512:
                                      c * T + nt * 512 + 512],
                                  banks4[nt]),
                    waits=[mm4[nt], dst_used])
                last = b4_free[nt]
        return last, lw

    def cum_pass(src_tok):
        cz = E.v(lambda h: h.memset(B.carry[:], 0.0),
                 waits=[tok.get("carry_used"), src_tok])
        ctk = cz
        cts = []
        for nt in range(NT):
            for c in range(4):
                sc = E.v(lambda h, c=c, nt=nt:
                         h.tensor_tensor_scan(
                             B.cum[:, c * T + nt * 512: c * T + nt * 512 + 512],
                             B.yW[:, c * T + nt * 512: c * T + nt * 512 + 512],
                             B.zer[:], B.carry[:, c: c + 1],
                             op0=ALU.add, op1=ALU.add),
                         waits=[ctk, tok.get("cum_used"), tok["zer"]])
                ctk = E.v(lambda h, c=c, nt=nt:
                          h.tensor_copy(B.carry[:, c: c + 1],
                                        B.cum[:, c * T + nt * 512 + 511:
                                              c * T + nt * 512 + 512]),
                          waits=[sc])
                cts.append(sc)
        tok["carry_used"] = ctk
        return cts[-1]

    def prod_pass(xv_buf, xv_tok, cum_tok, use_alpha):
        pt = None
        for c in range(4):
            for nt in range(NT):
                sl = slice(c * T + nt * 512, c * T + nt * 512 + 512)
                p1 = E.v(lambda h, sl=sl, xv_buf=xv_buf:
                         h.tensor_mul(xv_buf[:, sl], xv_buf[:, sl],
                                      B.cum[:, sl]),
                         waits=[xv_tok, cum_tok, pt])
                if use_alpha:
                    pt = E.v(lambda h, sl=sl, xv_buf=xv_buf:
                             h.scalar_tensor_tensor(
                                 xv_buf[:, sl], xv_buf[:, sl],
                                 B.alphac[:, 0:1], B.wrep[:, sl],
                                 op0=ALU.mult, op1=ALU.mult),
                             waits=[p1, wrep_tok, tok["alpha"]])
                else:
                    pt = E.v(lambda h, sl=sl, xv_buf=xv_buf:
                             h.tensor_mul(xv_buf[:, sl], xv_buf[:, sl],
                                          B.wrep[:, sl]),
                             waits=[p1, wrep_tok])
        return pt

    xvf_tok, _ = xv_pass(scratch["d_vf"], B.xV, tok.get("xV_used"),
                         B.wv1, tok.get("wv1_free"))
    ywf_tok, _ = xv_pass(scratch["d_wf"], B.yW, tok.get("yW_used"),
                         B.wv2, tok.get("wv2_free"))
    cumf = cum_pass(ywf_tok)
    tok["cum_used"] = None
    pwf = prod_pass(B.xV, xvf_tok, cumf, False)
    xvi_tok, wv1ld = xv_pass(scratch["d_vi"], B.xVi, tok.get("xVi_used"),
                             B.wv1, pwf)
    ywi_tok, wv2ld = xv_pass(scratch["d_wi"], B.yW, cumf, B.wv2, cumf)
    tok["cum_used"] = pwf
    cumi = cum_pass(ywi_tok)
    pwi = prod_pass(B.xVi, xvi_tok, cumi, True)

    ldu = E.dy(B.wbig2[:], scratch["d_uf"], waits=[tok.get("wbig2_free")] + fence)
    ldo = E.dy(B.wbig[:], scratch["d_wo"], waits=[prevh] + fence)
    xr_toks = []
    ot_prev = None
    for nt in range(NT):
        o_toks = []
        for oc in range(DC):
            b = (oc % 2)
            bank = PS.big[:, b * 512: b * 512 + 512]
            mm = None
            for kc in range(4):
                mm = E.t(lambda h, oc=oc, nt=nt, kc=kc, bank=bank:
                         h.matmul(bank,
                                  B.wbig2[:, kc * D + oc * P: kc * D + oc * P + P],
                                  B.xV[:, kc * T + nt * 512: kc * T + nt * 512 + 512],
                                  start=(kc == 0), stop=False),
                         waits=[ldu, pwf, pwi, tok.get(f"ubank{b}")]
                         if kc == 0 else [])
            for kc in range(4):
                mm = E.t(lambda h, oc=oc, nt=nt, kc=kc, bank=bank:
                         h.matmul(bank,
                                  B.wbig2[:, (4 + kc) * D + oc * P:
                                          (4 + kc) * D + oc * P + P],
                                  B.xVi[:, kc * T + nt * 512: kc * T + nt * 512 + 512],
                                  start=False, stop=False))
            mm = E.t(lambda h, oc=oc, nt=nt, bank=bank:
                     h.matmul(bank, B.bfw[:, oc * P: oc * P + P],
                              B.wT[:, nt * 512: nt * 512 + 512],
                              start=False, stop=False),
                     waits=[wT_tok])
            mm = E.t(lambda h, oc=oc, nt=nt, bank=bank:
                     h.matmul(bank, B.binv[:, oc * P: oc * P + P],
                              B.wT[:, nt * 512: nt * 512 + 512],
                              start=False, stop=True),
                     waits=[tok["binv"]])
            ot = E.v(lambda h, oc=oc, nt=nt, bank=bank:
                     h.tensor_tensor(B.o1[:, oc * 512: oc * 512 + 512],
                                     bank,
                                     B.invn[:, nt * 512: nt * 512 + 512],
                                     op=ALU.mult),
                     waits=[mm, tok.get("o1_used"), tok.get("invn_tok")])
            tok[f"ubank{b}"] = ot
            o_toks.append(ot)
        for oc2 in range(DC):
            b = next_bank()
            mm = None
            for kc in range(DC):
                mm = E.t(lambda h, oc2=oc2, nt=nt, kc=kc, b=b:
                         h.matmul(banks[b][:, 0:512],
                                  B.wbig[:, kc * D + oc2 * P: kc * D + oc2 * P + P],
                                  B.o1[:, kc * 512: kc * 512 + 512],
                                  start=(kc == 0), stop=(kc == DC - 1)),
                         waits=[ldo, bank_free[b]] + o_toks if kc == 0 else [])
            xr = E.v(lambda h, oc2=oc2, nt=nt, b=b:
                     h.tensor_tensor(
                         B.xbf[:, oc2 * T + nt * 512: oc2 * T + nt * 512 + 512],
                         banks[b][:, 0:512],
                         B.xbf[:, oc2 * T + nt * 512: oc2 * T + nt * 512 + 512],
                         op=ALU.add), waits=[mm])
            bank_free[b] = xr
            xr_toks.append(xr)
        tok["o1_used"] = xr_toks[-1]
    tok["xV_used"] = xr_toks[-1]
    tok["xVi_used"] = xr_toks[-1]
    tok["yW_used"] = cumi
    tok["wv1_free"] = xr_toks[-1]
    tok["wv2_free"] = xr_toks[-1]
    tok["wbig_free"] = xr_toks[-1]
    tok["wbig2_free"] = xr_toks[-1]
    tok["wrep_used"] = xr_toks[-1]
    tok["wT_used"] = xr_toks[-1]
    tok["wtok_used"] = tok["wtok_wr"]
    tok["cum_used"] = pwi
    tok["xV_used"] = xr_toks[-1]
    emit_ln(E, B, PS, tok, xr_toks, with_out)


def build_phaseD(nseq=NSEQ):
    nc = bass.Bass("TRN2", target_bir_lowering=False)
    xin = nc.dram_tensor("xin", [nseq, D, T], F32, kind="ExternalInput").ap()
    xout = nc.dram_tensor("xout", [nseq, D, T], F32, kind="ExternalOutput").ap()
    W = {}
    for name, shp in [
            ("wq_n", (DC, P, D)), ("wk_n", (DC, P, D)), ("winv_n", (DC, P, D)),
            ("winvT", (DC, P, D)), ("vfF", (DC, P, 512)), ("wfF", (DC, P, 512)),
            ("vroleIF", (DC, P, 512)), ("wroleIF", (DC, P, 512)),
            ("ufui", (DC, P, D)), ("rw1T", (DC, P, D)), ("woT3", (DC, P, D)),
            ("rw2T_in", (DC, P, 8)), ("rep8_in", (8, 512)),
            ("bfw_in", (8, D)), ("binv_in", (8, D)),
            ("b1F_in", (P, DC)), ("b2rep_in", (P, 8)), ("ebrep_in", (P, 8)),
            ("alpha_in", (P, 1)), ("invn_in", (1, P, T)),
            ("ident_f", (P, P))]:
        W[name] = nc.dram_tensor(name, list(shp), F32, kind="ExternalInput").ap()
    scratch = {k: nc.dram_tensor(k, [P, DC * 512], BF16, kind="Internal").ap()
               for k in ("d_vf", "d_wf", "d_vi", "d_wi")}
    scratch["d_rw1"] = nc.dram_tensor("d_rw1", [P, DC * D], BF16,
                                      kind="Internal").ap()
    scratch["d_uf"] = nc.dram_tensor("d_uf", [P, DC * D], BF16,
                                     kind="Internal").ap()
    scratch["d_wo"] = nc.dram_tensor("d_wo", [P, DC * D], BF16,
                                     kind="Internal").ap()

    with ExitStack() as st:
        B = NS()
        alloc_phaseD(nc, st, B)
        PS = NS()
        PS.big = st.enter_context(nc.psum_tensor("psbig", [P, 2048], F32))
        PS.a = st.enter_context(nc.psum_tensor("psa", [P, 512], F32))
        PS.b = st.enter_context(nc.psum_tensor("psb", [P, 512], F32))
        PS.c = st.enter_context(nc.psum_tensor("psc", [P, 512], F32))
        PS.tr = st.enter_context(nc.psum_tensor("pstr", [P, 1024], BF16))
        sems = {}
        for s in ("T", "V", "S", "G", "Y0", "Y1", "Y2", "Y3",
                  "GD0", "GD1", "GD2", "GD3"):
            sems[s] = st.enter_context(nc.semaphore(s))
        block = st.enter_context(nc.Block())
        E = Emitter(nc, sems)

        def prog():
            tok = {}
            emit_phaseD_prologue(E, B, PS, W, scratch, tok)
            lu = loadw(E, B, W, "ufui", B.wbig2, D, tok)
            du = E.dg(scratch["d_uf"], B.wbig2[:], waits=[lu])
            lo = loadw(E, B, W, "woT3", B.wbig, D, tok)
            do = E.dg(scratch["d_wo"], B.wbig[:], waits=[lo])
            tok["folds_dma"] = tok["folds_dma"] + [du, do]
            tok["wbig_free"] = do
            tok["wbig2_free"] = du
            tok["invn_tok"] = tok.get("st1")
            for s_ in range(nseq):
                emit_load_x(E, B, xin, s_, tok)
                emit_phaseD_seq(E, B, PS, W, scratch, tok,
                                with_out=xout[s_, :, :])
            for k in ("GD0", "GD1", "GD2", "GD3"):
                E.op("gpsimd",
                     lambda h, k=k: h.wait_ge(sems[k], E.cnt[k]), inc=0)

        for name in Emitter.ENGINES:
            getattr(block, name)(lambda h, name=name: (E.start(name, h), prog()))
    return nc



POOL_N = 78336


class Carver:
    def __init__(self, pool):
        self.pool = pool
        self.off = 0

    def bf(self, elems, rows=P):
        start = self.off
        self.off += (elems + 1) // 2 * 2
        assert self.off <= POOL_N, f"pool overflow: {self.off} > {POOL_N}"
        ap = self.pool[:, start:start + elems]
        return ap if rows == P else ap[0:rows, :]

    def f32(self, elems, rows=P):
        ap = self.bf(2 * elems)
        ap = ap.bitcast(F32)
        return ap if rows == P else ap[0:rows, :]


def make_shared(nc, st):
    S = NS()
    sb = lambda n, s, dt: st.enter_context(nc.sbuf_tensor(n, s, dt))
    S.xbf = sb("xbf", [P, DC * T], BF16)
    S.stage = [sb(f"stage{i}", [P, T], F32) for i in range(2)]
    S.sq = [sb(f"sq{i}", [P, 512], BF16) for i in range(2)]
    S.statrow = sb("statrow", [1, 512], F32)
    S.statbf = sb("statbf", [1, 512], BF16)
    S.statbf2 = sb("statbf2", [1, 512], BF16)
    S.ones1 = sb("ones1", [1, P], BF16)
    S.ones128 = sb("ones128", [P, 1], BF16)
    S.identbf = sb("identbf", [P, P], BF16)
    S.pool = sb("pool", [P, POOL_N], BF16)
    return S


def _sh(B, S):
    for k in ("xbf", "stage", "sq", "statrow", "statbf", "statbf2",
              "ones1", "ones128", "identbf"):
        setattr(B, k, getattr(S, k))


def carve_A(S):
    B = NS(); _sh(B, S); c = Carver(S.pool)
    B.qc = c.bf(DC * T)
    B.qr = c.bf(4 * T)
    B.ckv = c.bf(2 * T)
    B.krs = c.bf(T, rows=32)
    B.qstage = [c.bf(T) for _ in range(2)]
    B.kstage = [c.bf(T) for _ in range(2)]
    B.vh = [c.bf(16 * 66) for _ in range(2)]
    B.probsT = c.bf(8 * 512)
    B.weffF = c.bf(DC * D)
    B.wqreffF = c.bf(DC * 512)
    B.wdkvF = c.bf(DC * 256)
    B.wukF = c.bf(2 * D)
    B.wuvF = c.bf(2 * D)
    B.wkrF = c.bf(DC * 32)
    B.woF = c.bf(DC * D)
    B.rope = c.bf(T)
    B.tri01 = c.bf(P)
    B.rrow = c.bf(512, rows=1)
    B.rrowf = c.f32(512, rows=1)
    B.rbc = c.bf(512, rows=64)
    B.wstageA = lambda i, o=0, w=D: B.stage[i][:, o:o + w]
    B.wstageB = lambda i, o=0, w=D: B.stage[i][:, D + o:D + o + w]
    B.wbfA = lambda i, o=0, w=D: B.qstage[i][:, o: o + w]
    B.wbfB = lambda i, o=0, w=D: B.kstage[i][:, o: o + w]
    return B


def carve_B(S):
    B = NS(); _sh(B, S); c = Carver(S.pool)
    B.routerF = c.bf(DC * 64)
    B.corr = c.f32(64)
    B.lgall = c.f32(16 * 64)
    B.sc = c.f32(256); B.sfc = c.f32(256)
    B.msk = c.f32(256); B.hit = c.f32(256); B.sel = c.f32(256)
    B.gs = c.f32(32); B.g2 = c.f32(32); B.gm = c.f32(32)
    B.rm = c.f32(4); B.rs = c.f32(4)
    B.combine = c.f32(16 * 64)
    B.cmbbf = c.bf(64)
    B.cT = c.bf(T, rows=64)
    B.onehot = c.bf(64 * P, rows=64)
    B.wup = [c.bf(8 * 512) for _ in range(2)]
    B.wdn = [c.bf(4 * D) for _ in range(4)]
    B.hbuf = c.bf(4 * 4 * 2 * 512)
    B.rbuf = [c.bf(512) for _ in range(2)]
    B.sqe = [c.bf(512) for _ in range(2)]
    B.cbc = [c.bf(512) for _ in range(2)]
    B.acc = c.f32(DC * 1024)
    return B


def carve_C(S):
    B = NS(); _sh(B, S); c = Carver(S.pool)
    B.trilT = c.bf(16 * 64, rows=64)
    B.wqk = c.bf(DC * P)
    B.tbuf = c.bf(DC * T)
    B.vhC = c.bf(16 * 16 * 66)
    B.probsT = c.bf(8 * 512)
    B.xst = [c.bf(T) for _ in range(2)]
    B.taug = [c.bf(512) for _ in range(2)]
    B.mbuf = c.f32(16)
    B.mbufb = c.bf(16)
    B.mtr = c.bf(P, rows=16)
    B.mrow = [c.bf(T, rows=1) for _ in range(2)]
    B.wvF = c.bf(DC * D)
    B.woF = c.bf(DC * D)
    B.bvrep = c.bf(D)
    B.boF = c.f32(DC)
    B.rrow = c.bf(512, rows=1)
    B.rrowf = c.f32(512, rows=1)
    B.rbc = c.bf(512, rows=64)
    return B


def carve_D(S):
    B = NS(); _sh(B, S); c = Carver(S.pool)
    B.wbig = c.bf(DC * D)
    B.wbig2 = c.bf(DC * D)
    B.wv1 = c.bf(DC * 512)
    B.wv2 = c.bf(DC * 512)
    B.xV = c.bf(4 * T)
    B.xVi = c.bf(4 * T)
    B.yW = c.bf(4 * T)
    B.cum = c.bf(4 * T)
    B.wrep = c.bf(4 * T)
    B.h8 = c.bf(DC * P)
    B.o1 = c.bf(DC * 512)
    B.zer = c.bf(512)
    B.carry = c.f32(4)
    B.wtok = c.f32(16 * 8)
    B.wtb = c.bf(8)
    B.wT = c.bf(T, rows=8)
    B.rw2F = c.bf(DC * 8)
    B.b1F = c.f32(DC)
    B.b2eb = c.f32(8)
    B.alphac = c.f32(1)
    B.bfw = c.bf(D, rows=8)
    B.binv = c.bf(D, rows=8)
    B.rep8 = c.bf(512, rows=8)
    B.invn = c.bf(T)
    B.lgD = c.f32(8)
    B.rmD = c.f32(1)
    return B


def emit_castjob(E, B, W, name, scr, cw, tok, rows=P):
    is3 = len(W[name].shape) == 3
    nch = W[name].shape[0] if is3 else 1
    ji = tok.get("_cast_i", 0)
    dmo = None
    for ci in range(nch):
        src = W[name][ci] if is3 else W[name]
        for c0 in range(0, cw, T):
            w = min(T, cw - c0)
            i = ji % 2
            ji += 1
            ld = E.dy(B.stage[i][0:rows, 0:w], src[0:rows, c0:c0 + w],
                      waits=[tok.get("st0") if i == 0 else tok.get("st1")])
            ct = E.v(lambda h, i=i, w=w, rows=rows:
                     h.tensor_copy(B.qc[0:rows, i * T: i * T + w],
                                   B.stage[i][0:rows, 0:w]),
                     waits=[ld, tok.get(f"qcst{i}")])
            base = ci * cw + c0
            dmo = E.dg(scr[0:rows, base: base + w],
                       B.qc[0:rows, i * T: i * T + w], waits=[ct])
            tok[f"qcst{i}"] = dmo
            if i == 0:
                tok["st0"] = ct
            else:
                tok["st1"] = ct
    tok["_cast_i"] = ji
    return dmo


def _ysnap(E):
    return [(s, E.cnt[s]) for s in ("Y0", "Y1", "Y2", "Y3")]


def fused_consts_A(E, B, scr, tok):
    f = tok.get("fence", [])
    for name, dst in [
            ("weff_s", B.weffF), ("qreff_s", B.wqreffF),
            ("wdkv_s", B.wdkvF), ("wuk_s", B.wukF),
            ("wuv_s", B.wuvF), ("wkr_s", B.wkrF),
            ("wo_s", B.woF), ("rope_s", B.rope),
            ("tri_s", B.tri01)]:
        E.dy(dst, scr[name], waits=f)
    ys = _ysnap(E)
    for key in ("weff", "rope", "tri", "wsmall"):
        tok[key] = ys
    for k in ("vh_ones0", "vh_ones1"):
        tok.pop(k, None)


def fused_consts_B(E, B, W, scr, tok):
    f = tok.get("fence", [])
    E.dy(B.routerF, scr["router_s"], waits=f)
    E.dy(B.onehot, scr["onehot_s"], waits=f)
    E.dy(B.corr, W["corr_rep"], waits=f)
    ys = _ysnap(E)
    for key in ("router", "onehot", "corr"):
        tok[key] = ys
    tok.pop("acc_read", None)


def fused_consts_C(E, B, W, scr, tok):
    f = tok.get("fence", [])
    E.dy(B.wqk, scr["wqk_s"], waits=f)
    E.dy(B.wvF, scr["wv_s"], waits=f)
    E.dy(B.woF, scr["wo2_s"], waits=f)
    E.dy(B.bvrep, scr["bv_s"], waits=f)
    E.dy(B.boF, W["boF_in"], waits=f)
    ys = _ysnap(E)
    for key in ("wqk", "wsmallC", "boF"):
        tok[key] = ys
    o1 = E.v(lambda h: h.memset(B.xst[0][:], 1.0), waits=f)
    o2 = E.v(lambda h: h.memset(B.xst[1][:], 1.0), waits=[o1])
    tok["xst_ones"] = o2
    tok["vh_ones"] = E.v(lambda h: h.memset(B.vhC[:], 1.0), waits=[o2])


def fused_consts_D(E, B, W, scr, tok):
    f = tok.get("fence", [])
    E.dy(B.rw2F, scr["rw2_s"], waits=f)
    E.dy(B.rep8, scr["rep8_s"], waits=f)
    E.dy(B.bfw, scr["bfw_s"], waits=f)
    E.dy(B.binv, scr["binv_s"], waits=f)
    E.dy(B.invn, scr["invn_s"], waits=f)
    E.dy(B.b1F, W["b1F_in"], waits=f)
    E.dy(B.alphac, W["alpha_in"], waits=f)
    ys = _ysnap(E)
    for key in ("rw2", "rep8", "bfw", "binv", "invn_tok", "b1F", "alpha"):
        tok[key] = ys
    l1 = E.dy(B.stage[0][:, 0:8], W["b2rep_in"], waits=[tok.get("st0")] + f)
    l2 = E.dy(B.stage[1][:, 0:8], W["ebrep_in"], waits=[tok.get("st1")] + f)
    b2 = E.v(lambda h: h.tensor_add(B.b2eb, B.stage[0][:, 0:8],
                                    B.stage[1][:, 0:8]), waits=[l1, l2])
    tok["b2eb"] = b2
    tok["st0"] = b2
    tok["st1"] = b2
    tok["zer"] = E.v(lambda h: h.memset(B.zer, 0.0), waits=f)
    tok["wbig_free"] = None
    tok["wbig2_free"] = None


def build_fused(nseq=NSEQ):
    nc = bass.Bass("TRN2", target_bir_lowering=False)
    xin = nc.dram_tensor("xin", [nseq, D, T], F32, kind="ExternalInput").ap()
    xout = nc.dram_tensor("xout", [nseq, D, T], F32, kind="ExternalOutput").ap()
    W = {}
    all_ins = [
        ("w_dqt", (32, P, D)), ("w_uq", (32, P, D)), ("w_qr", (32, P, 512)),
        ("w_dkv", (DC, P, 256)), ("w_uk", (2, P, D)), ("w_uv", (2, P, D)),
        ("w_kr", (DC, P, 32)), ("w_o", (DC, P, D)), ("rope_rep", (P, T)),
        ("tri01", (P, P)),
        ("routerT", (DC, P, 64)), ("corr_rep", (P, 64)),
        ("up_w", (64, DC, P, 512)), ("dn_w", (64, 4, P, D)),
        ("shexp_up", (4, DC, P, 512)), ("shexp_dn", (4, 4, P, D)),
        ("onehot_f", (64, 64 * P)),
        ("trilT_f", (P, 1024)), ("wvT", (DC, P, D)), ("woT2", (DC, P, D)),
        ("bv_rep", (P, D)), ("boF_in", (P, DC)),
        ("wq_n", (DC, P, D)), ("wk_n", (DC, P, D)), ("winv_n", (DC, P, D)),
        ("winvT", (DC, P, D)), ("vfF", (DC, P, 512)), ("wfF", (DC, P, 512)),
        ("vroleIF", (DC, P, 512)), ("wroleIF", (DC, P, 512)),
        ("ufui", (DC, P, D)), ("rw1T", (DC, P, D)), ("woT3", (DC, P, D)),
        ("rw2T_in", (DC, P, 8)), ("rep8_in", (8, 512)), ("bfw_in", (8, D)),
        ("binv_in", (8, D)), ("b1F_in", (P, DC)), ("b2rep_in", (P, 8)),
        ("ebrep_in", (P, 8)), ("alpha_in", (P, 1)), ("invn_in", (1, P, T)),
        ("ident_f", (P, P)),
    ]
    for name, shp in all_ins:
        W[name] = nc.dram_tensor(name, list(shp), F32, kind="ExternalInput").ap()
    scr = {}
    for name, shp in [
            ("weff_s", [P, DC * D]), ("qreff_s", [P, DC * 512]),
            ("wdkv_s", [P, DC * 256]), ("wuk_s", [P, 2 * D]),
            ("wuv_s", [P, 2 * D]), ("wkr_s", [P, DC * 32]),
            ("wo_s", [P, DC * D]), ("rope_s", [P, T]), ("tri_s", [P, P]),
            ("router_s", [P, DC * 64]), ("onehot_s", [64, 64 * P]),
            ("wqk_s", [P, DC * P]), ("wv_s", [P, DC * D]),
            ("wo2_s", [P, DC * D]), ("bv_s", [P, D]),
            ("d_vf", [P, DC * 512]), ("d_wf", [P, DC * 512]),
            ("d_vi", [P, DC * 512]), ("d_wi", [P, DC * 512]),
            ("d_rw1", [P, DC * D]), ("d_uf", [P, DC * D]),
            ("d_wo", [P, DC * D]), ("rw2_s", [P, DC * 8]),
            ("rep8_s", [8, 512]), ("bfw_s", [8, D]), ("binv_s", [8, D]),
            ("invn_s", [P, T])]:
        scr[name] = nc.dram_tensor(name, shp, BF16, kind="Internal").ap()

    with ExitStack() as st:
        S = make_shared(nc, st)
        BA, BB, BC, BD = carve_A(S), carve_B(S), carve_C(S), carve_D(S)
        PS = NS()
        PS.big = st.enter_context(nc.psum_tensor("psbig", [P, 2048], F32))
        PS.a = st.enter_context(nc.psum_tensor("psa", [P, 512], F32))
        PS.b = st.enter_context(nc.psum_tensor("psb", [P, 512], F32))
        PS.c = st.enter_context(nc.psum_tensor("psc", [P, 512], F32))
        PS.tr = st.enter_context(nc.psum_tensor("pstr", [P, 1024], BF16))
        sems = {}
        for s in ("T", "V", "S", "G", "Y0", "Y1", "Y2", "Y3",
                  "GD0", "GD1", "GD2", "GD3"):
            sems[s] = st.enter_context(nc.semaphore(s))
        block = st.enter_context(nc.Block())
        E = Emitter(nc, sems)

        def snap():
            return [(s, E.cnt[s]) for s in E.cnt]

        def prog():
            tok = {}
            c1 = E.v(lambda h: h.memset(S.ones1[:], 1.0))
            c2 = E.v(lambda h: h.memset(S.ones128[:], 1.0), waits=[c1])
            tok["ones"] = c2
            ld = E.dy(S.stage[0][:, 0:P], W["ident_f"])
            idt = E.v(lambda h: h.tensor_copy(S.identbf[:], S.stage[0][:, 0:P]),
                      waits=[ld, c2])
            tok["ident"] = idt
            tok["st0"] = idt
            emit_castjob(E, BA, W, "w_dkv", scr["wdkv_s"], 256, tok)
            emit_castjob(E, BA, W, "w_uk", scr["wuk_s"], D, tok)
            emit_castjob(E, BA, W, "w_uv", scr["wuv_s"], D, tok)
            emit_castjob(E, BA, W, "w_kr", scr["wkr_s"], 32, tok)
            emit_castjob(E, BA, W, "w_o", scr["wo_s"], D, tok)
            emit_castjob(E, BA, W, "rope_rep", scr["rope_s"], T, tok)
            emit_castjob(E, BA, W, "tri01", scr["tri_s"], P, tok)
            emit_castjob(E, BA, W, "routerT", scr["router_s"], 64, tok)
            emit_castjob(E, BA, W, "onehot_f", scr["onehot_s"], 64 * P, tok,
                         rows=64)
            emit_castjob(E, BA, W, "wvT", scr["wv_s"], D, tok)
            emit_castjob(E, BA, W, "woT2", scr["wo2_s"], D, tok)
            emit_castjob(E, BA, W, "bv_rep", scr["bv_s"], D, tok)
            emit_castjob(E, BA, W, "ufui", scr["d_uf"], D, tok)
            emit_castjob(E, BA, W, "woT3", scr["d_wo"], D, tok)
            emit_castjob(E, BA, W, "rw2T_in", scr["rw2_s"], 8, tok)
            emit_castjob(E, BA, W, "rep8_in", scr["rep8_s"], 512, tok, rows=8)
            emit_castjob(E, BA, W, "bfw_in", scr["bfw_s"], D, tok, rows=8)
            emit_castjob(E, BA, W, "invn_in", scr["invn_s"], T, tok)
            la = E.dy(BD.alphac, W["alpha_in"], waits=[])
            lb = E.dy(BA.stage[0][0:8, 0:D], W["binv_in"],
                      waits=[tok.get("st0")])
            bi = E.v(lambda h: h.tensor_scalar_mul(BA.qc[0:8, 0:D],
                                                   BA.stage[0][0:8, 0:D],
                                                   BD.alphac[0:8, 0:1]),
                     waits=[lb, la, tok.get("qcst0")])
            tok["qcst0"] = E.dg(scr["binv_s"], BA.qc[0:8, 0:D], waits=[bi])
            tok["st0"] = bi
            tok["fence"] = snap()
            emit_weff_prologue(E, BA, W, PS,
                               {"weff": scr["weff_s"],
                                "qreff": scr["qreff_s"]}, tok)
            ld2 = E.dy(BC.stage[1][:, 0:1024], W["trilT_f"],
                       waits=[tok.get("st1")] + tok["fence"])
            ttr = E.v(lambda h: h.tensor_copy(BC.trilT, BC.stage[1][0:64, 0:1024]),
                      waits=[ld2, ("T", E.cnt["T"])])
            mz = E.v(lambda h: h.memset(BC.wqk, 0.0),
                     waits=[ttr, ("T", E.cnt["T"])])
            wq = None
            for h_ in range(H):
                mm = E.t(lambda h, h_=h_:
                         h.matmul(PS.a[0:64, 0:64],
                                  BC.trilT[0:64, h_ * 64: h_ * 64 + 64],
                                  BC.trilT[0:64, h_ * 64: h_ * 64 + 64],
                                  start=True, stop=True),
                         waits=[mz, wq])
                r0 = (h_ % 2) * 64
                wq = E.v(lambda h, h_=h_, r0=r0:
                         h.tensor_scalar_mul(
                             BC.wqk[r0:r0 + 64,
                                    (h_ // 2) * P + r0: (h_ // 2) * P + r0 + 64],
                             PS.a[0:64, 0:64], 0.125),
                         waits=[mm])
            dq = E.dg(scr["wqk_s"], BC.wqk, waits=[wq])
            tok["psa_free"] = wq
            tok["st1"] = ttr
            tok["fence"] = snap()
            emit_phaseD_prologue_folds(E, BD, PS, W, scr, tok)
            tok["fence"] = snap()
            for s_ in range(nseq):
                emit_load_x(E, S, xin, s_, tok)
                fused_consts_A(E, BA, scr, tok)
                emit_phaseA_seq(E, BA, PS, tok, with_out=None)
                fused_consts_B(E, BB, W, scr, tok)
                emit_phaseB_seq(E, BB, PS, W, tok, with_out=None)
                fused_consts_C(E, BC, W, scr, tok)
                emit_phaseC_seq(E, BC, PS, tok, with_out=None)
                fused_consts_D(E, BD, W, scr, tok)
                emit_phaseD_seq(E, BD, PS, W, scr, tok, with_out=xout[s_, :, :])
            for k in ("GD0", "GD1", "GD2", "GD3"):
                E.op("gpsimd",
                     lambda h, k=k: h.wait_ge(sems[k], E.cnt[k]), inc=0)

        for name in Emitter.ENGINES:
            getattr(block, name)(lambda h, name=name: (E.start(name, h), prog()))
    return nc


def emit_phaseD_prologue_folds(E, B, PS, W, scratch, tok):
    fence = tok.get("fence", [])
    t1 = loadw(E, B, W, "wq_n", B.wbig, D, tok)
    t2 = loadw(E, B, W, "vfF", B.wv1, 512, tok)
    f1 = emit_fold(E, B, PS, B.wbig, t1, B.wv1, t2, B.wv2, 512,
                   fence[0] if fence else None, "vf")
    d1 = E.dg(scratch["d_vf"], B.wv2, waits=[f1])
    t3 = loadw(E, B, W, "rw1T", B.wbig2, D, tok)
    dlast = [None, None, None, None]
    ti = 0
    f2 = f1
    for dc in range(DC):
        for n in range(2):
            bk = ti % 4
            mm = None
            for kc in range(DC):
                mm = E.t(lambda h, dc=dc, n=n, kc=kc, bk=bk:
                         h.matmul(PS.big[:, bk * 512: bk * 512 + 512],
                                  B.wbig[:, kc * D + dc * P: kc * D + dc * P + P],
                                  B.wbig2[:, kc * D + n * 512: kc * D + n * 512 + 512],
                                  start=(kc == 0), stop=(kc == DC - 1)),
                         waits=[t1, t3, f1, f2] if kc == 0 else [])
            ev = E.v(lambda h, bk=bk:
                     h.tensor_copy(B.wv1[:, bk * 512: bk * 512 + 512],
                                   PS.big[:, bk * 512: bk * 512 + 512]),
                     waits=[mm, dlast[bk], d1])
            dlast[bk] = E.dg(scratch["d_rw1"][:, dc * D + n * 512:
                                              dc * D + n * 512 + 512],
                             B.wv1[:, bk * 512: bk * 512 + 512], waits=[ev])
            f2 = ev
            ti += 1
    d2 = dlast[(ti - 1) % 4]
    tok["fence"] = E.snap()
    t5 = loadw(E, B, W, "winv_n", B.wbig2, D, tok)
    t4 = loadw(E, B, W, "vroleIF", B.wv1, 512, tok)
    f3 = emit_fold(E, B, PS, B.wbig2, t5, B.wv1, t4, B.wv2, 512,
                   dlast[0], "ivf", out_free=d1)
    f4 = emit_fold(E, B, PS, B.wbig, t1, B.wv2, f3, B.wv1, 512, f3, "vi",
                   out_free=d2)
    d3 = E.dg(scratch["d_vi"], B.wv1, waits=[f4])
    tok["fence"] = E.snap()
    t6 = loadw(E, B, W, "wk_n", B.wbig, D, tok)
    t7 = loadw(E, B, W, "wfF", B.wv2, 512, tok)
    f5 = emit_fold(E, B, PS, B.wbig, t6, B.wv2, t7, B.wv1, 512, f4,
                   "wf", out_free=d3)
    d4 = E.dg(scratch["d_wf"], B.wv1, waits=[f5])
    tok["fence"] = E.snap()
    t8 = loadw(E, B, W, "wroleIF", B.wv2, 512, tok)
    f6 = emit_fold(E, B, PS, B.wbig2, t5, B.wv2, t8, B.wv1, 512, f5, "iwf",
                   out_free=d4)
    f7 = emit_fold(E, B, PS, B.wbig, t6, B.wv1, f6, B.wv2, 512, f6, "wi",
                   out_free=d4)
    d5 = E.dg(scratch["d_wi"], B.wv2, waits=[f7])
    tok["folds"] = f7
    tok["folds_dma"] = [d1, d2, d3, d4, d5]
    tok["st0"] = f7
    tok["st1"] = f7


# version 09
# speedup vs baseline: 1.0166x; 1.0166x over previous
import numpy as np
import concourse.bass as bass
import concourse.mybir as mybir
from concourse.bass_utils import run_bass_kernel_spmd
from contextlib import ExitStack

F32 = mybir.dt.float32
BF16 = mybir.dt.bfloat16
AF = mybir.ActivationFunctionType
ALU = mybir.AluOpType
AX = mybir.AxisListType

P = 128
T = 2048
D = 1024
DC = D // P
NT = T // 512
NSEQ = 4
NCORES = 8
H = 16
NEG = -1e30

ENG_SEM = {"sync": "Y", "tensor": "T", "vector": "V", "scalar": "S",
           "gpsimd": "G"}


class Emitter:
    ENGINES = ("sync", "tensor", "vector", "scalar", "gpsimd")

    def __init__(self, nc, sems):
        self.nc = nc
        self.sems = sems
        self.cur = None
        self.h = None

    def snap(self):
        return [(s, self.cnt[s]) for s in self.cnt]

    def start(self, name, handle):
        self.cur = name
        self.h = handle
        self.cnt = {k: 0 for k in self.sems}
        self.know = {e: {k: 0 for k in self.sems} for e in self.ENGINES}
        self.dma_rr = {"Y": 0, "GD": 0}

    def _wait(self, eng, sem, val):
        if val > self.know[eng][sem]:
            self.know[eng][sem] = val
            if eng == self.cur:
                self.h.wait_ge(self.sems[sem], val)

    def op(self, eng, fn, waits=(), sem=None, inc=1, lp=False):
        flat = []
        for w in waits:
            if w is None:
                continue
            if isinstance(w, list):
                flat.extend(x for x in w if x is not None)
            else:
                flat.append(w)
        for s, v in flat:
            if v > 0:
                self._wait(eng, s, v)
        if sem is None:
            sem = ENG_SEM[eng]
        if eng == self.cur:
            if lp:
                with self.nc.allow_low_precision("bf16 ok within tolerance"):
                    inst = fn(self.h)
            else:
                inst = fn(self.h)
            if inc:
                inst.then_inc(self.sems[sem], inc)
        if inc:
            self.cnt[sem] += inc
        return (sem, self.cnt[sem])

    def t(self, fn, waits=(), inc=1):
        return self.op("tensor", fn, waits, "T", inc)

    def v(self, fn, waits=(), inc=1, lp=False):
        return self.op("vector", fn, waits, "V", inc, lp)

    def s(self, fn, waits=(), inc=1):
        return self.op("scalar", fn, waits, "S", inc)

    def g(self, fn, waits=(), inc=1):
        return self.op("gpsimd", fn, waits, "G", inc)

    def _dma(self, eng, prefix, nsem, out, in_, waits):
        idx = self.dma_rr[prefix] % nsem
        self.dma_rr[prefix] += 1
        sem = f"{prefix}{idx}"
        waits = list(waits) + [(sem, self.cnt[sem])]
        return self.op(eng, lambda h: h.dma_start(out=out, in_=in_),
                       waits, sem, 16)

    def dy(self, out, in_, waits=()):
        return self._dma("sync", "Y", 4, out, in_, waits)

    def dg(self, out, in_, waits=()):
        return self._dma("gpsimd", "GD", 4, out, in_, waits)


def ceil_div(a, b):
    return (a + b - 1) // b



def host_prep(inputs, core):
    b0 = core * NSEQ
    x = np.asarray(inputs["L0_attn_x"])
    d = {}
    d["xin"] = np.ascontiguousarray(
        x[b0:b0 + NSEQ].transpose(0, 2, 1)).astype(np.float32)
    return d


def host_prep_shared(inputs):
    g = lambda k: np.ascontiguousarray(np.asarray(inputs[k], dtype=np.float32))
    d = {}
    d["w_dqt"] = np.ascontiguousarray(g("L0_attn_W_DQ").T).reshape(32, P, D)
    d["w_uq"] = g("L0_attn_W_UQ").reshape(32, P, D)
    d["w_qr"] = g("L0_attn_W_QR").reshape(32, P, 512)
    d["w_dkv"] = g("L0_attn_W_DKV").reshape(DC, P, 256)
    d["w_uk"] = g("L0_attn_W_UK").reshape(2, P, D)
    d["w_uv"] = g("L0_attn_W_UV").reshape(2, P, D)
    d["w_kr"] = g("L0_attn_W_KR").reshape(DC, P, 32)
    d["w_o"] = g("L0_attn_W_O").reshape(DC, P, D)
    t_ = np.arange(T, dtype=np.float64)
    inv = 1.0 / 10000.0 ** (np.arange(0, 32, 2, dtype=np.float64) / 32)
    fr = t_[:, None] * inv[None, :]
    emb = np.concatenate([fr, fr], -1)
    sc = (np.cos(emb) + np.sin(emb)).astype(np.float32)
    d["rope_rep"] = np.ascontiguousarray(
        np.tile(sc.T, (4, 1))).astype(np.float32)
    tri01 = np.triu(np.ones((P, P), dtype=np.float32))
    d["tri01"] = tri01
    ident = np.eye(P, dtype=np.float32)
    d["ident_f"] = ident
    d["routerT"] = np.ascontiguousarray(
        g("L0_ffn_router_w").T).reshape(DC, P, 64)
    d["corr_rep"] = np.ascontiguousarray(np.broadcast_to(
        g("L0_ffn_correction_bias")[None, :], (P, 64)))
    d["up_w"] = g("L0_ffn_up_w").reshape(64, DC, P, 512)
    d["dn_w"] = g("L0_ffn_down_w").reshape(64, 4, P, D)
    d["shexp_up"] = np.ascontiguousarray(
        g("L0_ffn_shared_up_w").reshape(DC, P, 4, 512).transpose(2, 0, 1, 3))
    d["shexp_dn"] = g("L0_ffn_shared_down_w").reshape(4, 4, P, D)
    oh = np.zeros((64, 64 * P), dtype=np.float32)
    for e in range(64):
        oh[e, e * P:(e + 1) * P] = 1.0
    d["onehot_f"] = oh
    params = g("L1_attn_params")
    rows, cols = np.tril_indices(64)
    tril = np.zeros((16, 64, 64), dtype=np.float32)
    tril[:, rows, cols] = params
    trilT_f = np.zeros((P, 1024), dtype=np.float32)
    for h_ in range(16):
        trilT_f[0:64, h_ * 64:(h_ + 1) * 64] = tril[h_].T
    d["trilT_f"] = trilT_f
    d["wvT"] = np.ascontiguousarray(g("L1_attn_Wv").T).reshape(DC, P, D)
    d["woT2"] = np.ascontiguousarray(g("L1_attn_Wo").T).reshape(DC, P, D)
    d["bv_rep"] = np.ascontiguousarray(np.broadcast_to(
        g("L1_attn_bv")[None, :], (P, D)))
    d["boF_in"] = np.ascontiguousarray(g("L1_attn_bo").reshape(DC, P).T)
    d["wq_n"] = g("L1_ffn_W_Q").reshape(DC, P, D)
    d["wk_n"] = g("L1_ffn_W_K").reshape(DC, P, D)
    d["winv_n"] = g("L1_ffn_W_inv").reshape(DC, P, D)
    d["winvT"] = np.ascontiguousarray(g("L1_ffn_W_inv").T).reshape(DC, P, D)
    flat = lambda a: np.ascontiguousarray(
        a.transpose(1, 0, 2)).reshape(D, 512).reshape(DC, P, 512)
    d["vfF"] = flat(g("L1_ffn_V_fwd"))
    d["wfF"] = flat(g("L1_ffn_W_fwd"))
    d["vroleIF"] = flat(g("L1_ffn_W_inv_exp"))
    d["wroleIF"] = flat(g("L1_ffn_V_inv"))
    uflat = lambda a: np.ascontiguousarray(
        a.transpose(0, 2, 1)).reshape(512, D).reshape(4, P, D)
    d["ufui"] = np.concatenate([uflat(g("L1_ffn_U_fwd")),
                                uflat(g("L1_ffn_U_inv"))], axis=0)
    d["rw1T"] = np.ascontiguousarray(g("L1_ffn_router_w1").T).reshape(DC, P, D)
    d["rw2T_in"] = np.ascontiguousarray(
        g("L1_ffn_router_w2").T).reshape(DC, P, 8)
    d["woT3"] = np.ascontiguousarray(g("L1_ffn_W_O").T).reshape(DC, P, D)
    rep8 = np.zeros((8, 512), dtype=np.float32)
    for c in range(4):
        for pp in range(128):
            rep8[2 * c + pp // 64, c * P + pp] = 1.0
    d["rep8_in"] = rep8
    d["bfw_in"] = g("L1_ffn_b_fwd")
    d["binv_in"] = g("L1_ffn_b_inv")
    d["b1F_in"] = np.ascontiguousarray(g("L1_ffn_router_b1").reshape(DC, P).T)
    d["b2rep_in"] = np.ascontiguousarray(np.broadcast_to(
        g("L1_ffn_router_b2")[None, :], (P, 8)))
    d["ebrep_in"] = np.ascontiguousarray(np.broadcast_to(
        g("L1_ffn_expert_bias")[None, :], (P, 8)))
    d["alpha_in"] = np.full((P, 1), float(np.asarray(inputs["L1_ffn_alpha_bi"])),
                            dtype=np.float32)
    d["invn_in"] = np.ascontiguousarray(np.broadcast_to(
        (1.0 / np.arange(1, T + 1, dtype=np.float64)).astype(np.float32)[None, :],
        (P, T)))[None]
    return d



def alloc_phaseA(nc, st, bufs):
    sb = lambda n, s, dt: st.enter_context(nc.sbuf_tensor(n, s, dt))
    B = bufs
    B.xbf = sb("xbf", [P, DC * T], BF16)
    B.stage = [sb(f"stage{i}", [P, T], F32) for i in range(2)]
    B.qc = sb("qc", [P, DC * T], BF16)
    B.qr = sb("qr", [P, 4 * T], BF16)
    B.ckv = sb("ckv", [P, 2 * T], BF16)
    B.krs = sb("krs", [32, T], BF16)
    B.qstage = [sb(f"qstage{i}", [P, T], BF16) for i in range(2)]
    B.kstage = [sb(f"kstage{i}", [P, T], BF16) for i in range(2)]
    B.vh = [sb(f"vh{i}", [P, 16 * 66], BF16) for i in range(2)]
    B.probsT = sb("probsT", [P, 8 * 512], BF16)
    B.weffF = sb("weffF", [P, DC * D], BF16)
    B.wqreffF = sb("wqreffF", [P, DC * 512], BF16)
    B.wdkvF = sb("wdkvF", [P, DC * 256], BF16)
    B.wukF = sb("wukF", [P, 2 * D], BF16)
    B.wuvF = sb("wuvF", [P, 2 * D], BF16)
    B.wkrF = sb("wkrF", [P, DC * 32], BF16)
    B.woF = sb("woF", [P, DC * D], BF16)
    B.rope = sb("rope", [P, T], BF16)
    B.tri01 = sb("tri01_sb", [P, P], BF16)
    B.rrow = sb("rrow", [1, 512], BF16)
    B.rbc = sb("rbc", [64, 512], BF16)
    B.wstageA = lambda i, o=0, w=D: B.stage[i][:, o:o + w]
    B.wstageB = lambda i, o=0, w=D: B.stage[i][:, D + o:D + o + w]
    B.wbfA = lambda i, o=0, w=D: B.qstage[i][:, o: o + w]
    B.wbfB = lambda i, o=0, w=D: B.kstage[i][:, o: o + w]
    B.statrow = sb("statrow", [1, 512], F32)
    B.statbf = sb("statbf", [1, 512], BF16)
    B.statbf2 = sb("statbf2", [1, 512], BF16)
    B.sq = [sb(f"sq{i}", [P, 512], BF16) for i in range(2)]
    B.ones1 = sb("ones1", [1, P], BF16)
    B.ones128 = sb("ones128", [P, 1], BF16)


def emit_load_x(E, B, xin, s_, tok):
    fence = tok.get("fence", [])
    cast = [tok.get("st0"), tok.get("st1")]
    for c in range(DC):
        wl = [cast[c % 2], tok.get(f"ost{c % 2}")] + fence
        ld = E.dy(B.stage[c % 2][:], xin[s_, c * P:(c + 1) * P, :], waits=wl)
        cast[c % 2] = E.v(
            lambda h, c=c: h.tensor_copy(B.xbf[:, c * T:(c + 1) * T],
                                         B.stage[c % 2][:]),
            waits=[ld] + fence)
    tok["st0"] = cast[0]
    tok["st1"] = cast[1]
    tok["xbf"] = cast[(DC - 1) % 2]
    tok["xbf2"] = cast[DC % 2]


def emit_weff_prologue(E, B, W, PS, scratch, tok0):
    sweeps = (
        [("w_uq", n, dh, "weff") for n in range(2) for dh in range(2)]
        + [("w_qr", 0, dh, "qreff") for dh in range(2)]
    )
    lastA = [tok0.get("st0"), tok0.get("st1")]
    lastB = [tok0.get("st0"), tok0.get("st1")]
    ev_tok = None
    for rhs_name, n, dh, out_name in sweeps:
        mm_tok = None
        rw = int(W[rhs_name].shape[2])
        for kc in range(32):
            i = kc % 2
            la = E.dy(B.wstageA(i), W["w_dqt"][kc],
                      waits=[lastA[i]] if lastA[i] else [])
            lb = E.dy(B.wstageB(i, 0, rw),
                      W[rhs_name][kc], waits=[lastB[i]] if lastB[i] else [])
            ca = E.v(lambda h, i=i: h.tensor_copy(B.wbfA(i), B.wstageA(i)),
                     waits=[la])
            cb = E.v(lambda h, i=i, rw=rw:
                     h.tensor_copy(B.wbfB(i, 0, rw), B.wstageB(i, 0, rw)),
                     waits=[lb])
            for dj in range(4):
                dc = dh * 4 + dj
                c0 = n * 512
                mm_tok = E.t(
                    lambda h, kc=kc, dc=dc, c0=c0, dj=dj, i=i:
                    h.matmul(PS.big[:, dj * 512:(dj + 1) * 512],
                             B.wbfA(i, dc * P, P),
                             B.wbfB(i, c0, 512),
                             start=(kc == 0), stop=(kc == 31)),
                    waits=[ca, cb] + ([ev_tok] if (kc == 0 and ev_tok) else []))
            lastA[i] = mm_tok
            lastB[i] = mm_tok
        for dj in range(4):
            dc = dh * 4 + dj
            if out_name == "weff":
                dst = B.weffF[:, dc * D + n * 512: dc * D + n * 512 + 512]
            else:
                dst = B.wqreffF[:, dc * 512: dc * 512 + 512]
            ev_tok = E.v(
                lambda h, dst=dst, dj=dj:
                h.tensor_copy(dst, PS.big[:, dj * 512:(dj + 1) * 512]),
                waits=[mm_tok])
    d1 = E.dg(scratch["weff"], B.weffF[:], waits=[ev_tok])
    d2 = E.dg(scratch["qreff"], B.wqreffF[:], waits=[ev_tok])
    tok0["st0"] = lastA[0]
    tok0["st1"] = lastA[1]
    return ev_tok, (d1, d2)


def emit_small_weights_load(E, B, W, tok):
    jobs = [("w_dkv", B.wdkvF, 256), ("w_uk", B.wukF, D), ("w_uv", B.wuvF, D),
            ("w_kr", B.wkrF, 32), ("w_o", B.woF, D)]
    last = [tok.get("st0"), tok.get("st1")]
    casts = []
    for name, dst, cw in jobs:
        nch = W[name].shape[0]
        for c in range(nch):
            i = len(casts) % 2
            wl = [last[i]] if last[i] else []
            ld = E.dy(B.stage[i][:, 0:cw], W[name][c], waits=wl)
            ct = E.v(lambda h, dst=dst, c=c, cw=cw, i=i:
                     h.tensor_copy(dst[:, c * cw:(c + 1) * cw],
                                   B.stage[i][:, 0:cw]), waits=[ld])
            last[i] = ct
            casts.append(ct)
    tok["wsmall"] = casts[-1]
    tok["st0"] = last[0]
    tok["st1"] = last[1]


def emit_phaseA_seq(E, B, PS, tok, with_out=None):
    fence = tok.get("fence", [])
    xw = [tok["xbf"], tok["xbf2"]]

    banks = [PS.a, PS.b, PS.c]
    bank_free = [tok.get("psa_free"), tok.get("psb_free"), tok.get("psc_free")]
    bi = 0

    def next_bank():
        nonlocal bi
        b = bi % 3
        bi += 1
        return b

    banks4 = [PS.a[:, 0:512], PS.b[:, 0:512], PS.c[:, 0:512],
              PS.big[:, 0:512]]
    b4_free = [tok.get("psa_free"), tok.get("psb_free"),
               tok.get("psc_free"), None]

    def proj4(w_img, wof, M, nkc, dst_fn, wtok, rope_mul=False, base=0):
        nonlocal b4_free
        mm4 = [None] * 4
        for kc in range(nkc):
            for nt in range(NT):
                mm4[nt] = E.t(
                    lambda h, kc=kc, nt=nt, M=M, wof=wof:
                    h.matmul(banks4[nt][0:M, 0:512] if M < P else banks4[nt],
                             w_img[:, wof(kc): wof(kc) + M],
                             B.xbf[:, kc * T + nt * 512: kc * T + nt * 512 + 512],
                             start=(kc == 0), stop=(kc == nkc - 1)),
                    waits=([wtok, xw[0], xw[1], b4_free[nt]] + fence)
                    if kc == 0 else [])
        for nt in range(NT):
            if rope_mul:
                b4_free[nt] = E.v(
                    lambda h, nt=nt, M=M: h.tensor_tensor(
                        dst_fn(nt),
                        banks4[nt][0:M, 0:512] if M < P else banks4[nt],
                        B.rope[0:M, nt * 512: nt * 512 + 512] if M < P
                        else B.rope[:, nt * 512: nt * 512 + 512],
                        op=ALU.mult),
                    waits=[mm4[nt], tok["rope"]] + fence)
            else:
                b4_free[nt] = E.v(
                    lambda h, nt=nt, M=M: h.tensor_copy(
                        dst_fn(nt),
                        banks4[nt][0:M, 0:512] if M < P else banks4[nt]),
                    waits=[mm4[nt]] + fence)

    for oc in range(2):
        proj4(B.wdkvF, lambda kc, oc=oc: kc * 256 + oc * P, P, DC,
              lambda nt, oc=oc: B.ckv[:, oc * T + nt * 512: oc * T + nt * 512 + 512],
              tok["wsmall"])
    tok["ckv"] = b4_free[3]
    ckv_toks = [tok["ckv"]]
    proj4(B.wkrF, lambda kc: kc * 32, 32, DC,
          lambda nt: B.krs[:, nt * 512: nt * 512 + 512],
          tok["wsmall"], rope_mul=True)
    tok["krs"] = b4_free[3]
    for oc in range(DC):
        proj4(B.weffF, lambda kc, oc=oc: kc * D + oc * P, P, DC,
              lambda nt, oc=oc: B.qc[:, oc * T + nt * 512: oc * T + nt * 512 + 512],
              tok["weff"])
    qc_tok = b4_free[3]
    for c in range(4):
        proj4(B.wqreffF, lambda kc, c=c: kc * 512 + c * P, P, DC,
              lambda nt, c=c: B.qr[:, c * T + nt * 512: c * T + nt * 512 + 512],
              tok["weff"], rope_mul=True)
    qr_tok = b4_free[3]
    tok["psa_free"] = b4_free[0]
    tok["psb_free"] = b4_free[1]
    tok["psc_free"] = b4_free[2]

    banks = [PS.a, PS.b, PS.c]
    bank_free = [b4_free[0], b4_free[1], b4_free[2]]
    bi = 0

    def next_bank():
        nonlocal bi
        b = bi % 3
        bi += 1
        return b

    stage_free = [tok.get("stage_free0"), tok.get("stage_free1")]
    vh_free = [tok.get("vh_free0"), tok.get("vh_free1")]
    vh_ones = [tok.get("vh_ones0"), tok.get("vh_ones1")]
    y_toks = []
    for h_ in range(H):
        i = h_ % 2
        kst = []
        for nt in range(NT):
            b = next_bank()
            mm = None
            for kc in range(2):
                mm = E.t(lambda h, h_=h_, nt=nt, kc=kc, b=b:
                         h.matmul(banks[b][0:64, 0:512],
                                  B.wukF[:, kc * D + h_ * 64: kc * D + h_ * 64 + 64],
                                  B.ckv[:, kc * T + nt * 512: kc * T + nt * 512 + 512],
                                  start=(kc == 0), stop=(kc == 1)),
                         waits=([bank_free[b], stage_free[i]] + ckv_toks)
                         if kc == 0 else [])
            bank_free[b] = E.v(lambda h, i=i, nt=nt, b=b:
                               h.tensor_copy(B.kstage[i][0:64, nt * 512: nt * 512 + 512],
                                             banks[b][0:64, 0:512]), waits=[mm])
            kst.append(bank_free[b])
        kst.append(E.v(lambda h, i=i: h.tensor_copy(B.kstage[i][64:96, :],
                                                    B.krs[:]),
                       waits=[tok["krs"], stage_free[i]]))
        ASCL = 1.0 / float(np.sqrt(96.0))
        qst1 = E.v(lambda h, i=i, h_=h_:
                   h.tensor_scalar_mul(B.qstage[i][0:64, :],
                                       B.qc[(h_ % 2) * 64:(h_ % 2) * 64 + 64,
                                            (h_ // 2) * T:(h_ // 2) * T + T],
                                       ASCL),
                   waits=[qc_tok, stage_free[i]])
        r0 = (h_ % 4) * 32
        if r0 < 96:
            qst2 = E.v(lambda h, i=i, h_=h_, r0=r0:
                       h.tensor_scalar_mul(B.qstage[i][64:96, :],
                                           B.qr[r0:r0 + 32,
                                                (h_ // 4) * T:(h_ // 4) * T + T],
                                           ASCL),
                       waits=[qr_tok, qst1])
        else:
            qd = E.dg(B.qstage[i][64:96, :],
                      B.qr[96:128, (h_ // 4) * T:(h_ // 4) * T + T],
                      waits=[qr_tok, qst1])
            qst2 = E.v(lambda h, i=i:
                       h.tensor_scalar_mul(B.qstage[i][64:96, :],
                                           B.qstage[i][64:96, :], ASCL),
                       waits=[qd, qst1])
        if vh_ones[i] is None:
            vh_ones[i] = E.v(lambda h, i=i: h.memset(B.vh[i][:], 1.0),
                             waits=[vh_free[i]] if vh_free[i] else [])
        vtoks = []
        for tt in range(16):
            b = next_bank()
            mm = None
            for kc in range(2):
                mm = E.t(lambda h, h_=h_, tt=tt, kc=kc, b=b:
                         h.matmul(banks[b][:, 0:64],
                                  B.ckv[:, kc * T + tt * P: kc * T + tt * P + P],
                                  B.wuvF[:, kc * D + h_ * 64: kc * D + h_ * 64 + 64],
                                  start=(kc == 0), stop=(kc == 1)),
                         waits=[bank_free[b], vh_free[i], vh_ones[i]]
                         if kc == 0 else [])
            bank_free[b] = E.v(lambda h, i=i, tt=tt, b=b:
                               h.tensor_copy(B.vh[i][:, tt * 66: tt * 66 + 64],
                                             banks[b][:, 0:64]), waits=[mm])
            vtoks.append(bank_free[b])

        kq_waits = kst + [qst2] + vtoks
        half_free = [None, None]
        last_av = tok.get("av_prev")
        prob_tok = {}
        pool_toks = []
        for qg in range(NT):
            nkt = 4 * qg + 4
            av = None
            for kb in range(ceil_div(nkt, 2)):
                kts = [kt for kt in (2 * kb, 2 * kb + 1) if kt < nkt]
                hf = kb % 2
                mms = []
                for kt in kts:
                    sl = (hf * 2 + (kt % 2)) * 512
                    mms.append(E.t(
                        lambda h, kt=kt, qg=qg, sl=sl:
                        h.matmul(PS.big[:, sl:sl + 512],
                                 B.kstage[i][0:96, kt * P: kt * P + P],
                                 B.qstage[i][0:96, qg * 512: qg * 512 + 512],
                                 start=True, stop=True),
                        waits=kq_waits + [half_free[hf], last_av]))
                sl0 = hf * 1024
                rg = kts[0] % 8
                region_free = av if kts[0] >= 8 else last_av
                ex = E.s(
                    lambda h, rg=rg, nk_=len(kts), sl0=sl0:
                    h.activation(B.probsT[:, rg * 512: rg * 512 + nk_ * 512],
                                 PS.big[:, sl0: sl0 + nk_ * 512], AF.Exp),
                    waits=mms + [region_free] + pool_toks)
                half_free[hf] = ex
                for kt in kts:
                    prob_tok[kt] = ex
                if nkt > 8 and kts[-1] == 7:
                    avb0 = PS.a if qg % 2 == 0 else PS.c
                    for kt2 in range(8):
                        av = E.t(lambda h, i=i, kt2=kt2, avb0=avb0,
                                 first=(kt2 == 0):
                                 h.matmul(avb0[0:65, 0:512],
                                          B.vh[i][:, kt2 * 66: kt2 * 66 + 65],
                                          B.probsT[:, kt2 * 512: kt2 * 512 + 512],
                                          start=first, stop=False),
                                 waits=([prob_tok[k2] for k2 in range(8)] +
                                        [tok.get(f"yev{qg % 2}"), tok.get("rrow_used")])
                                 if kt2 == 0 else [])
            pool_toks = []
            for kt in range(4 * qg, 4 * qg + 4):
                dq = kt - 4 * qg
                rg = kt % 8
                if dq > 0:
                    pool_toks.append(E.g(
                        lambda h, rg=rg, dq=dq:
                        h.memset(B.probsT[:, rg * 512: rg * 512 + dq * P], 0.0),
                        waits=[prob_tok[kt]]))
                pool_toks.append(E.g(
                    lambda h, rg=rg, dq=dq:
                    h.tensor_tensor(B.probsT[:, rg * 512 + dq * P: rg * 512 + dq * P + P],
                                    B.probsT[:, rg * 512 + dq * P: rg * 512 + dq * P + P],
                                    B.tri01[:], op=ALU.mult),
                    waits=[prob_tok[kt]]))
            avb = PS.a if qg % 2 == 0 else PS.c
            k_lo = 8 if nkt > 8 else 0
            for kt in range(k_lo, nkt):
                av = E.t(lambda h, i=i, kt=kt, avb=avb,
                         first=(kt == k_lo and k_lo == 0),
                         last=(kt == nkt - 1):
                         h.matmul(avb[0:65, 0:512],
                                  B.vh[i][:, kt * 66: kt * 66 + 65],
                                  B.probsT[:, (kt % 8) * 512: (kt % 8) * 512 + 512],
                                  start=first, stop=last),
                         waits=([prob_tok[k2] for k2 in range(k_lo, nkt)] + pool_toks +
                                [tok.get(f"yev{qg % 2}"), tok.get("rrow_used")])
                         if kt == k_lo else [])
            last_av = av
            rr0 = E.s(lambda h, avb=avb: h.activation(B.rrowf[:],
                                                      avb[64:65, 0:512],
                                                      AF.Ln),
                      waits=[av, tok.get("rbc_used"), tok.get("rrowf_used")])
            rr = E.s(lambda h: h.activation(B.rrow[:], B.rrowf[:], AF.Exp,
                                            scale=-1.0),
                     waits=[rr0])
            tok["rrowf_used"] = rr
            bc = E.t(lambda h: h.matmul(PS.b[0:64, 0:512], B.ones1[:, 0:64],
                                        B.rrow[:], start=True, stop=True),
                     waits=[rr, tok.get("rbc_used")])
            rbc_ev = E.v(lambda h: h.tensor_copy(B.rbc[:], PS.b[0:64, 0:512]),
                         waits=[bc, tok.get("yev")])
            tok["rbc_used"] = rbc_ev
            tok["rrow_used"] = rbc_ev
            yev = E.v(lambda h, h_=h_, qg=qg, avb=avb:
                      h.tensor_tensor(B.qc[(h_ % 2) * 64:(h_ % 2) * 64 + 64,
                                           (h_ // 2) * T + qg * 512:
                                           (h_ // 2) * T + qg * 512 + 512],
                                      avb[0:64, 0:512], B.rbc[:],
                                      op=ALU.mult), waits=[rbc_ev])
            tok["yev"] = yev
            tok[f"yev{qg % 2}"] = yev
            y_toks.append(yev)
            bank_free[0 if qg % 2 == 0 else 2] = yev
            bank_free[1] = rbc_ev
        tok["av_prev"] = last_av
        stage_free[i] = last_av
        vh_free[i] = last_av
    tok["vh_ones0"] = vh_ones[0]
    tok["vh_ones1"] = vh_ones[1]
    tok["stage_free0"] = stage_free[0]
    tok["stage_free1"] = stage_free[1]
    tok["vh_free0"] = vh_free[0]
    tok["vh_free1"] = vh_free[1]

    emit_out_ln(E, B, PS, tok,
                w_img=B.woF, w_tok=tok["wsmall"], src=B.qc,
                src_toks=[y_toks[-1]], nk=DC, with_out=with_out,
                bank_free=[bank_free[0], bank_free[1], bank_free[2]])


def emit_out_ln(E, B, PS, tok, w_img, w_tok, src, src_toks, nk, with_out,
                bank_free=None, bias_cols=None, bias_tok=()):
    banks4 = [PS.a[:, 0:512], PS.b[:, 0:512], PS.c[:, 0:512],
              PS.big[:, 0:512]]
    if bank_free is None:
        bank_free = [None, None, None]
    b4_free = [bank_free[0], bank_free[1], bank_free[2], None]
    extra = [tok.get("yev"), tok.get("rbc_used"), tok.get("av_prev")]
    bias_tok = list(bias_tok)
    xr_toks = []
    for oc in range(DC):
        mm4 = [None] * 4
        for kc in range(nk):
            for nt in range(NT):
                mm4[nt] = E.t(
                    lambda h, oc=oc, nt=nt, kc=kc:
                    h.matmul(banks4[nt],
                             w_img[:, kc * D + oc * P: kc * D + oc * P + P],
                             src[:, kc * T + nt * 512: kc * T + nt * 512 + 512],
                             start=(kc == 0), stop=(kc == nk - 1)),
                    waits=(src_toks + [w_tok, b4_free[nt]] + extra)
                    if kc == 0 else [])
        for nt in range(NT):
            if bias_cols is None:
                b4_free[nt] = E.v(
                    lambda h, oc=oc, nt=nt:
                    h.tensor_tensor(B.xbf[:, oc * T + nt * 512: oc * T + nt * 512 + 512],
                                    banks4[nt],
                                    B.xbf[:, oc * T + nt * 512: oc * T + nt * 512 + 512],
                                    op=ALU.add), waits=[mm4[nt]])
            else:
                b4_free[nt] = E.v(
                    lambda h, oc=oc, nt=nt:
                    h.scalar_tensor_tensor(
                        B.xbf[:, oc * T + nt * 512: oc * T + nt * 512 + 512],
                        banks4[nt],
                        bias_cols[:, oc:oc + 1],
                        B.xbf[:, oc * T + nt * 512: oc * T + nt * 512 + 512],
                        op0=ALU.add, op1=ALU.add),
                    waits=[mm4[nt]] + bias_tok)
            xr_toks.append(b4_free[nt])
    tok["psa_free"] = b4_free[0]
    tok["psb_free"] = b4_free[1]
    tok["psc_free"] = b4_free[2]
    emit_ln(E, B, PS, tok, xr_toks, with_out)


def emit_ln(E, B, PS, tok, xr_toks, with_out):
    onetok = tok["ones"]
    out_toks = []
    prev_bc = []
    for nt in range(NT):
        mm1 = None
        for oc in range(DC):
            mm1 = E.t(lambda h, oc=oc, nt=nt:
                      h.matmul(PS.c[0:1, 0:512], B.ones128[:],
                               B.xbf[:, oc * T + nt * 512: oc * T + nt * 512 + 512],
                               start=(oc == 0), stop=(oc == DC - 1)),
                      waits=(xr_toks + [onetok, tok.get("psc_free")])
                      if oc == 0 else [])
        m_ev = E.v(lambda h, nt=nt:
                   h.tensor_scalar_mul(B.statrow[:],
                                       PS.c[0:1, 0:512], 1.0 / D),
                   waits=[mm1] + prev_bc)
        mbf = E.v(lambda h, nt=nt:
                  h.tensor_copy(B.statbf[:],
                                B.statrow[:]),
                  waits=[m_ev])
        mm2 = None
        sqt = [None, None]
        for oc in range(DC):
            sqt[oc % 2] = E.s(
                lambda h, oc=oc, nt=nt:
                h.activation(B.sq[oc % 2][:],
                             B.xbf[:, oc * T + nt * 512: oc * T + nt * 512 + 512],
                             AF.Square),
                waits=xr_toks + ([mm2] if mm2 else []))
            mm2 = E.t(lambda h, oc=oc, nt=nt:
                      h.matmul(PS.c[0:1, 0:512], B.ones128[:],
                               B.sq[oc % 2][:],
                               start=(oc == 0), stop=(oc == DC - 1)),
                      waits=[sqt[oc % 2]] + ([m_ev] if oc == 0 else []))
        e2 = E.v(lambda h, nt=nt:
                 h.scalar_tensor_tensor(B.statrow[:],
                                        B.statrow[:],
                                        -1.0,
                                        B.statrow[:],
                                        op0=ALU.mult, op1=ALU.mult),
                 waits=[mm2, mbf])
        e3 = E.v(lambda h, nt=nt:
                 h.scalar_tensor_tensor(B.statrow[:],
                                        PS.c[0:1, 0:512], 1.0 / D,
                                        B.statrow[:],
                                        op0=ALU.mult, op1=ALU.add),
                 waits=[e2])
        tok["psc_free"] = e3
        e3b = E.v(lambda h: h.tensor_scalar_add(B.statrow[:], B.statrow[:],
                                                1e-5), waits=[e3])
        e4 = E.s(lambda h, nt=nt:
                 h.activation(B.statrow[:],
                              B.statrow[:],
                              AF.Ln), waits=[e3b])
        e5 = E.s(lambda h, nt=nt:
                 h.activation(B.statrow[:],
                              B.statrow[:],
                              AF.Exp, scale=-0.5), waits=[e4])
        rbf = E.v(lambda h, nt=nt:
                  h.tensor_copy(B.statbf2[:],
                                B.statrow[:]),
                  waits=[e5])
        bcm = E.t(lambda h, nt=nt:
                  h.matmul(PS.a[:, 0:512], B.ones1[:],
                           B.statbf[:],
                           start=True, stop=True),
                  waits=[mbf, tok.get("psa_free"), tok.get("av_prev"),
                         tok.get("yev")])
        bcr = E.t(lambda h, nt=nt:
                  h.matmul(PS.b[:, 0:512], B.ones1[:],
                           B.statbf2[:],
                           start=True, stop=True),
                  waits=[rbf, tok.get("psb_free"), tok.get("rbc_used")])
        prev_bc = [bcm, bcr]
        nrm = []
        for oc in range(DC):
            t1 = E.v(lambda h, oc=oc, nt=nt:
                     h.tensor_tensor(B.xbf[:, oc * T + nt * 512: oc * T + nt * 512 + 512],
                                     B.xbf[:, oc * T + nt * 512: oc * T + nt * 512 + 512],
                                     PS.a[:, 0:512], op=ALU.subtract),
                     waits=[bcm] + xr_toks)
            t2 = E.v(lambda h, oc=oc, nt=nt:
                     h.tensor_tensor(B.xbf[:, oc * T + nt * 512: oc * T + nt * 512 + 512],
                                     B.xbf[:, oc * T + nt * 512: oc * T + nt * 512 + 512],
                                     PS.b[:, 0:512], op=ALU.mult),
                     waits=[t1, bcr])
            nrm.append(t2)
            if with_out is not None:
                cf = E.s(lambda h, oc=oc, nt=nt, i=(oc % 2):
                         h.activation(B.stage[i][:, nt * 512: nt * 512 + 512],
                                      B.xbf[:, oc * T + nt * 512: oc * T + nt * 512 + 512],
                                      AF.Copy),
                         waits=[t2, tok.get(f"ost{oc % 2}")])
                od = E.dg(with_out[oc * P:(oc + 1) * P, nt * 512: nt * 512 + 512],
                          B.stage[oc % 2][:, nt * 512: nt * 512 + 512],
                          waits=[cf])
                tok[f"ost{oc % 2}"] = od
        tok["psa_free"] = nrm[-1]
        tok["psb_free"] = nrm[-1]
        out_toks.extend(nrm)
    tok["xbf"] = out_toks[-1]
    tok["xbf2"] = out_toks[-2]
    tok["ln_done"] = out_toks
    tok["fence"] = [(s, E.cnt[s]) for s in ("T", "V", "S", "G")]
    for k in ("av_prev", "yev", "rbc_used", "rrow_used"):
        tok.pop(k, None)



class NS:
    pass


def build_phaseA(nseq=NSEQ, final=False):
    nc = bass.Bass("TRN2", target_bir_lowering=False)
    xin = nc.dram_tensor("xin", [nseq, D, T], F32, kind="ExternalInput").ap()
    xout = nc.dram_tensor("xout", [nseq, D, T], F32, kind="ExternalOutput").ap()
    W = {}
    for name, shp in [("w_dqt", (32, P, D)), ("w_uq", (32, P, D)),
                      ("w_qr", (32, P, 512)), ("w_dkv", (DC, P, 256)),
                      ("w_uk", (2, P, D)), ("w_uv", (2, P, D)),
                      ("w_kr", (DC, P, 32)), ("w_o", (DC, P, D)),
                      ("rope_rep", (P, T)), ("tri01", (P, P))]:
        W[name] = nc.dram_tensor(name, list(shp), F32, kind="ExternalInput").ap()
    scratch = {
        "weff": nc.dram_tensor("weff_s", [P, DC * D], BF16, kind="Internal").ap(),
        "qreff": nc.dram_tensor("qreff_s", [P, DC * 512], BF16, kind="Internal").ap(),
    }

    with ExitStack() as st:
        B = NS()
        alloc_phaseA(nc, st, B)
        PS = NS()
        PS.big = st.enter_context(nc.psum_tensor("psbig", [P, 2048], F32))
        PS.a = st.enter_context(nc.psum_tensor("psa", [P, 512], F32))
        PS.b = st.enter_context(nc.psum_tensor("psb", [P, 512], F32))
        PS.c = st.enter_context(nc.psum_tensor("psc", [P, 512], F32))
        sems = {}
        for s in ("T", "V", "S", "G", "Y0", "Y1", "Y2", "Y3",
                  "GD0", "GD1", "GD2", "GD3"):
            sems[s] = st.enter_context(nc.semaphore(s))
        block = st.enter_context(nc.Block())
        E = Emitter(nc, sems)

        def prog():
            tok = {}
            c1 = E.v(lambda h: h.memset(B.ones1[:], 1.0))
            c2 = E.v(lambda h: h.memset(B.ones128[:], 1.0), waits=[c1])
            tok["ones"] = c2
            ld = E.dy(B.stage[0][:, 0:P], W["tri01"])
            tok["tri"] = E.v(lambda h: h.tensor_copy(B.tri01[:], B.stage[0][:, 0:P]),
                             waits=[ld, c2])
            ldr = E.dy(B.stage[1][:], W["rope_rep"])
            tok["rope"] = E.v(lambda h: h.tensor_copy(B.rope[:], B.stage[1][:]),
                              waits=[ldr])
            tok["st0"] = tok["tri"]
            tok["st1"] = tok["rope"]
            emit_small_weights_load(E, B, W, tok)
            wtok, _ = emit_weff_prologue(E, B, W, PS, scratch, tok)
            tok["weff"] = wtok
            for s_ in range(nseq):
                emit_load_x(E, B, xin, s_, tok)
                emit_phaseA_seq(E, B, PS, tok, with_out=xout[s_, :, :])
            for k in ("GD0", "GD1", "GD2", "GD3"):
                E.op("gpsimd",
                     lambda h, k=k: h.wait_ge(sems[k], E.cnt[k]), inc=0)

        for name in Emitter.ENGINES:
            getattr(block, name)(lambda h, name=name: (E.start(name, h), prog()))
    return nc


PHASE_INPUTS = {
    "A": ("w_dqt", "w_uq", "w_qr", "w_dkv", "w_uk", "w_uv", "w_kr", "w_o",
          "rope_rep", "tri01"),
    "B": ("routerT", "corr_rep", "up_w", "dn_w", "shexp_up", "shexp_dn",
          "onehot_f", "ident_f"),
    "C": ("trilT_f", "wvT", "woT2", "bv_rep", "boF_in", "ident_f"),
    "D": ("wq_n", "wk_n", "winv_n", "winvT", "vfF", "wfF", "vroleIF",
          "wroleIF", "ufui", "rw1T", "woT3", "rw2T_in", "rep8_in", "bfw_in",
          "binv_in", "b1F_in", "b2rep_in", "ebrep_in", "alpha_in", "invn_in",
          "ident_f"),
}


FUSED_INPUTS = (
    "w_dqt", "w_uq", "w_qr", "w_dkv", "w_uk", "w_uv", "w_kr", "w_o",
    "rope_rep", "tri01", "routerT", "corr_rep", "up_w", "dn_w", "shexp_up",
    "shexp_dn", "onehot_f", "trilT_f", "wvT", "woT2", "bv_rep", "boF_in",
    "wq_n", "wk_n", "winv_n", "winvT", "vfF", "wfF", "vroleIF", "wroleIF",
    "ufui", "rw1T", "woT3", "rw2T_in", "rep8_in", "bfw_in", "binv_in",
    "b1F_in", "b2rep_in", "ebrep_in", "alpha_in", "invn_in", "ident_f")


def kernel(**inputs):
    W = host_prep_shared(inputs)
    x = np.asarray(inputs["L0_attn_x"], dtype=np.float32)
    xc = [np.ascontiguousarray(x[c * NSEQ:(c + 1) * NSEQ].transpose(0, 2, 1))
          for c in range(NCORES)]
    nc = build_fused(nseq=NSEQ)
    in_maps = []
    for c in range(NCORES):
        m = {"xin": xc[c]}
        for k in FUSED_INPUTS:
            m[k] = W[k]
        in_maps.append(m)
    res = run_bass_kernel_spmd(nc, in_maps, core_ids=list(range(NCORES)))
    cur = [np.asarray(r["xout"]) for r in res.results]
    out = np.concatenate([c.transpose(0, 2, 1) for c in cur], axis=0)
    return np.ascontiguousarray(out.astype(np.float32))



NEXP = 68
GRPS = [list(range(4))] + [[4 + 4 * g + j for j in range(4)] for g in range(16)]


def alloc_phaseB(nc, st, bufs):
    sb = lambda n, s, dt: st.enter_context(nc.sbuf_tensor(n, s, dt))
    B = bufs
    B.xbf = sb("xbf", [P, DC * T], BF16)
    B.stage = [sb(f"stage{i}", [P, T], F32) for i in range(2)]
    B.sq = [sb(f"sq{i}", [P, 512], BF16) for i in range(2)]
    B.statrow = sb("statrow", [1, 512], F32)
    B.statbf = sb("statbf", [1, 512], BF16)
    B.statbf2 = sb("statbf2", [1, 512], BF16)
    B.ones1 = sb("ones1", [1, P], BF16)
    B.ones128 = sb("ones128", [P, 1], BF16)
    B.identbf = sb("identbf", [P, P], BF16)
    B.routerF = sb("routerF", [P, DC * 64], BF16)
    B.corr = sb("corr", [P, 64], F32)
    B.lgall = sb("lgall", [P, 16 * 64], F32)
    B.sc = sb("sc", [P, 256], F32)
    B.sfc = sb("sfc", [P, 256], F32)
    B.msk = sb("msk", [P, 256], F32)
    B.hit = sb("hit", [P, 256], F32)
    B.sel = sb("sel", [P, 256], F32)
    B.gs = sb("gs", [P, 32], F32)
    B.g2 = sb("g2", [P, 32], F32)
    B.gm = sb("gm", [P, 32], F32)
    B.rm = sb("rm", [P, 4], F32)
    B.rs = sb("rs", [P, 4], F32)
    B.combine = sb("combine", [P, 16 * 64], F32)
    B.cmbbf = sb("cmbbf", [P, 64], BF16)
    B.cT = sb("cT", [64, T], BF16)
    B.onehot = sb("onehot", [64, 64 * P], BF16)
    B.wup = [sb(f"wup{i}", [P, 8 * 512], BF16) for i in range(2)]
    B.wdn = [sb(f"wdn{i}", [P, 4 * D], BF16) for i in range(4)]
    B.hbuf = sb("hbuf", [P, 4 * 4 * 2 * 512], BF16)
    B.rbuf = [sb(f"rbuf{i}", [P, 512], BF16) for i in range(2)]
    B.sqe = [sb(f"sqe{i}", [P, 512], BF16) for i in range(2)]
    B.cbc = [sb(f"cbc{i}", [P, 512], BF16) for i in range(2)]
    B.acc = sb("acc", [P, DC * 1024], F32)


def emit_router_logits(E, B, PS, tok):
    fence = tok.get("fence", [])
    xw = [tok["xbf"], tok["xbf2"]]
    lbank = [PS.c[:, 0:64], PS.big[:, 1536:1600]]
    lfree = [tok.get("psc_free"), None]
    e0 = None
    for tt in range(16):
        lb = tt % 2
        mm = None
        for kc in range(DC):
            mm = E.t(lambda h, tt=tt, kc=kc, lb=lb:
                     h.matmul(lbank[lb],
                              B.xbf[:, kc * T + tt * P: kc * T + tt * P + P],
                              B.routerF[:, kc * 64: kc * 64 + 64],
                              start=(kc == 0), stop=(kc == DC - 1)),
                     waits=([tok["router"], xw[0], xw[1],
                             lfree[lb]] + fence) if kc == 0 else [])
        e0 = E.s(lambda h, tt=tt, lb=lb:
                 h.activation(B.lgall[:, tt * 64: tt * 64 + 64], lbank[lb],
                              AF.Copy),
                 waits=[mm, tok.get("lgall_used")] + fence)
        lfree[lb] = e0
    tok["psc_free"] = lfree[0]
    tok["logits"] = e0


def emit_routing_v(E, B, tok):
    fence = tok.get("fence", [])
    bca = bass.broadcast_tensor_aps
    prev = {}
    cmb_toks = []
    for g4 in range(4):
        sl = slice(g4 * 256, g4 * 256 + 256)
        sc3 = B.sc[:].rearrange("p (t e) -> p t e", e=64)
        sfc3 = B.sfc[:].rearrange("p (t e) -> p t e", e=64)
        sfc6 = B.sfc[:].rearrange("p (t g e) -> p t g e", g=8, e=8)
        msk6 = B.msk[:].rearrange("p (t g e) -> p t g e", g=8, e=8)
        msk3 = B.msk[:].rearrange("p (t e) -> p t e", e=64)
        hit6 = B.hit[:].rearrange("p (t g e) -> p t g e", g=8, e=8)
        gs3 = B.gs[:].rearrange("p (t g) -> p t g", g=8)
        g23 = B.g2[:].rearrange("p (t g) -> p t g", g=8)
        gm3 = B.gm[:].rearrange("p (t g) -> p t g", g=8)
        corr_b = bca(sc3, B.corr[:].rearrange("p (t e) -> p t e", t=1))[1]
        gs_b = bca(sfc6, B.gs[:].rearrange("p (t g e) -> p t g e", g=8, e=1))[1]
        gm_b = bca(sfc6, B.gm[:].rearrange("p (t g e) -> p t g e", g=8, e=1))[1]
        rm_b8 = bca(gs3, B.rm[:].rearrange("p (t g) -> p t g", g=1))[1]
        rm_b64 = bca(sfc3, B.rm[:].rearrange("p (t e) -> p t e", e=1))[1]
        rs_b64 = bca(sfc3, B.rs[:].rearrange("p (t e) -> p t e", e=1))[1]
        s0 = E.s(lambda h, sl=sl: h.activation(B.sc[:], B.lgall[:, sl],
                                               AF.Sigmoid),
                 waits=[tok["logits"], prev.get("sc")] + fence)
        v0 = E.v(lambda h: h.tensor_tensor(sfc3, sc3, corr_b, op=ALU.add),
                 waits=[s0, tok["corr"], prev.get("sfc")] + fence)
        v1 = E.v(lambda h: h.reduce_max(B.gs[:], sfc6, axis=AX.X),
                 waits=[v0, prev.get("gs")] + fence)
        v2 = E.v(lambda h: h.tensor_tensor(hit6, sfc6, gs_b, op=ALU.is_ge),
                 waits=[v1, prev.get("hit")])
        v3 = E.v(lambda h: h.scalar_tensor_tensor(B.msk[:], B.hit[:], NEG,
                                                  B.sfc[:], op0=ALU.mult,
                                                  op1=ALU.add),
                 waits=[v2, prev.get("msk")])
        v4 = E.v(lambda h: h.reduce_max(B.g2[:], msk6, axis=AX.X),
                 waits=[v3, prev.get("g2")])
        v5 = E.v(lambda h: h.tensor_add(B.gs[:], B.gs[:], B.g2[:]),
                 waits=[v4])
        v6 = E.v(lambda h: h.memset(B.gm[:], 0.0),
                 waits=[prev.get("gm"), v5])
        for it in range(4):
            v7 = E.v(lambda h: h.reduce_max(B.rm[:], gs3, axis=AX.X),
                     waits=[v6, prev.get("rm")])
            v8 = E.v(lambda h: h.tensor_tensor(g23, gs3, rm_b8, op=ALU.is_ge),
                     waits=[v7])
            v9 = E.v(lambda h: h.tensor_add(B.gm[:], B.gm[:], B.g2[:]),
                     waits=[v8])
            v6 = E.v(lambda h: h.scalar_tensor_tensor(B.gs[:], B.g2[:], NEG,
                                                      B.gs[:], op0=ALU.mult,
                                                      op1=ALU.add),
                     waits=[v9])
            prev["rm"] = v6
        v10 = E.v(lambda h: h.tensor_tensor(msk6, sfc6, gm_b, op=ALU.mult),
                  waits=[v6])
        v11 = E.v(lambda h: h.memset(B.sel[:], 0.0),
                  waits=[prev.get("sel"), v10])
        for it in range(8):
            v12 = E.v(lambda h: h.reduce_max(B.rm[:], msk3, axis=AX.X),
                      waits=[v11, v10])
            v13 = E.v(lambda h: h.tensor_tensor(
                B.hit[:].rearrange("p (t e) -> p t e", e=64), msk3, rm_b64,
                op=ALU.is_ge), waits=[v12])
            v14 = E.v(lambda h: h.tensor_add(B.sel[:], B.sel[:], B.hit[:]),
                      waits=[v13])
            v11 = E.v(lambda h: h.scalar_tensor_tensor(B.msk[:], B.hit[:], NEG,
                                                       B.msk[:], op0=ALU.mult,
                                                       op1=ALU.add),
                      waits=[v14])
        v15 = E.v(lambda h: h.tensor_mul(B.hit[:], B.sc[:], B.sel[:]),
                  waits=[v11])
        v16 = E.v(lambda h: h.reduce_sum(B.rs[:],
                                         B.hit[:].rearrange("p (t e) -> p t e",
                                                            e=64),
                                         axis=AX.X),
                  waits=[v15, prev.get("rs")])
        v17 = E.v(lambda h: h.tensor_scalar_add(B.rs[:], B.rs[:], 1e-20),
                  waits=[v16])
        v18 = E.v(lambda h: h.reciprocal(B.rs[:], B.rs[:]), waits=[v17])
        v19 = E.v(lambda h, sl=sl:
                  h.scalar_tensor_tensor(
                      B.combine[:, sl].rearrange("p (t e) -> p t e", e=64),
                      B.hit[:].rearrange("p (t e) -> p t e", e=64), 2.5,
                      rs_b64, op0=ALU.mult, op1=ALU.mult),
                  waits=[v18, tok.get("combine_used")])
        cmb_toks.append(v19)
        prev = {"sc": v15, "sfc": v10, "gs": v6, "hit": v19,
                "msk": v11, "g2": v6, "gm": v10, "rm": v18, "rs": v19,
                "sel": v15}
    tok["combine"] = cmb_toks
    tok.pop("combine_used", None)


def emit_cT(E, B, PS, tok):
    prev_c = tok.get("cmbbf_used")
    ev2 = None
    for tt in range(16):
        v20 = E.v(lambda h, tt=tt:
                  h.tensor_copy(B.cmbbf[:],
                                B.combine[:, tt * 64: tt * 64 + 64]),
                  waits=tok["combine"] + [prev_c])
        tr = E.t(lambda h: h.transpose(PS.tr[0:64, 0:P], B.cmbbf[:],
                                       B.identbf[:]),
                 waits=[v20, tok.get("pstr_free"), tok["ident"]])
        ev2 = E.v(lambda h, tt=tt:
                  h.tensor_copy(B.cT[0:64, tt * P: tt * P + P],
                                PS.tr[0:64, 0:P]),
                  waits=[tr, tok.get("cT_used")])
        tok["pstr_free"] = ev2
        prev_c = tr
    tok["cmbbf_used"] = prev_c
    tok["cT"] = ev2
    tok["lgall_used"] = ev2
    tok.pop("cT_used", None)


def emit_phaseB_seq(E, B, PS, W, tok, with_out=None):
    fence = tok.get("fence", [])
    emit_router_logits(E, B, PS, tok)
    emit_routing_v(E, B, tok)
    xr_toks = []
    st_tok = [tok.get("st0"), tok.get("st1")]
    wup_tok = [None, None]
    wup_free = [list(fence), list(fence)]
    wdn_tok = [None, None, None, None]
    wdn_free = [list(fence) for _ in range(4)]
    acc_rd = tok.get("acc_read")
    hb_free = None
    prev_relu = [None, None]
    prev_sq = [None, None]
    prev_cbc = [None, None]
    upbank = [PS.a, PS.b]
    upbank_free = [tok.get("psa_free"), tok.get("psb_free")]
    dn_free = [None] * 4
    first_write = {}

    def load_expert_weights(eidx, half):
        pe = eidx % 2
        pd = eidx % 4
        if eidx < 4:
            upd, dnd = W["shexp_up"][eidx], W["shexp_dn"][eidx]
        else:
            upd, dnd = W["up_w"][eidx - 4], W["dn_w"][eidx - 4]
        toks = []
        for hh in range(2):
            si = (eidx * 2 + hh) % 2
            ld = E.dy(B.stage[si][:].rearrange("p (k w) -> p k w", w=512),
                      upd[4 * hh: 4 * hh + 4].rearrange("k p w -> p k w"),
                      waits=[st_tok[si]] + list(wup_free[pe]))
            ct = E.v(lambda h, pe=pe, hh=hh, si=si:
                     h.tensor_copy(B.wup[pe][:, hh * T: hh * T + T],
                                   B.stage[si][:]),
                     waits=[ld])
            st_tok[si] = ct
            toks.append(ct)
        for hh in range(2):
            si = (eidx * 2 + hh) % 2
            ld = E.dy(B.stage[si][:].rearrange("p (k w) -> p k w", w=D),
                      dnd[2 * hh: 2 * hh + 2].rearrange("k p w -> p k w"),
                      waits=[st_tok[si]] + list(wdn_free[pd]))
            ct = E.v(lambda h, pd=pd, hh=hh, si=si:
                     h.tensor_copy(B.wdn[pd][:, hh * T: hh * T + T],
                                   B.stage[si][:]),
                     waits=[ld])
            st_tok[si] = ct
            toks.append(ct)
        wup_tok[pe] = toks[1]
        wdn_tok[pd] = toks[3]

    for half in range(2):
        t0 = half * 1024
        for gi, grp in enumerate(GRPS):
            if half == 0 and gi == 1:
                emit_cT(E, B, PS, tok)
            up_mm_last = None
            for j, eidx in enumerate(grp):
                load_expert_weights(eidx, half)
                pe = eidx % 2
                for nt in range(2):
                    c0 = t0 + nt * 512
                    if eidx >= 4:
                        e_r = eidx - 4
                        cb = j % 2
                        cm = E.t(lambda h, e_r=e_r, c0=c0:
                                 h.matmul(PS.c[:, 0:512],
                                          B.onehot[0:64, e_r * P: e_r * P + P],
                                          B.cT[0:64, c0: c0 + 512],
                                          start=True, stop=True),
                                 waits=[tok["cT"], tok["onehot"],
                                        tok.get("psc_free")])
                        cv = E.v(lambda h, cb=cb:
                                 h.tensor_copy(B.cbc[cb][:], PS.c[:, 0:512]),
                                 waits=[cm, prev_cbc[cb]])
                        tok["psc_free"] = cv
                    for ch in range(4):
                        ub = ch % 2
                        mm = None
                        for kc in range(DC):
                            mm = E.t(
                                lambda h, pe=pe, ch=ch, kc=kc, c0=c0, ub=ub:
                                h.matmul(upbank[ub][:, 0:512],
                                         B.wup[pe][:, kc * 512 + ch * P:
                                                    kc * 512 + ch * P + P],
                                         B.xbf[:, kc * T + c0: kc * T + c0 + 512],
                                         start=(kc == 0), stop=(kc == DC - 1)),
                                waits=[wup_tok[pe], upbank_free[ub],
                                       tok["xbf"], tok["xbf2"]]
                                if kc == 0 else [])
                        up_mm_last = mm
                        wup_free[pe] = [mm]
                        rb = ch % 2
                        rl = E.s(lambda h, rb=rb, ub=ub:
                                 h.activation(B.rbuf[rb][:], upbank[ub][:, 0:512],
                                              AF.Relu),
                                 waits=[mm, prev_relu[rb]])
                        upbank_free[ub] = rl
                        slot = ((j * 4 + ch) * 2 + nt) * 512
                        if eidx < 4:
                            sqv = E.g(lambda h, rb=rb, slot=slot:
                                      h.tensor_mul(B.hbuf[:, slot: slot + 512],
                                                   B.rbuf[rb][:], B.rbuf[rb][:]),
                                      waits=[rl, hb_free])
                            prev_relu[rb] = sqv
                            prev_sq[rb] = sqv
                        else:
                            sqv = E.v(lambda h, rb=rb:
                                      h.tensor_mul(B.sqe[rb][:], B.rbuf[rb][:],
                                                   B.rbuf[rb][:]),
                                      waits=[rl, prev_sq[rb]])
                            prev_relu[rb] = sqv
                            cmu = E.g(lambda h, rb=rb, cb=(j % 2), slot=slot:
                                      h.tensor_tensor(B.hbuf[:, slot: slot + 512],
                                                      B.sqe[rb][:], B.cbc[cb][:],
                                                      op=ALU.mult),
                                      waits=[sqv, cv, hb_free])
                            prev_sq[rb] = cmu
                            prev_cbc[j % 2] = cmu
            for nt in range(2):
                for oc in range(DC):
                    db = oc % 4
                    mm = None
                    for j in range(4):
                        for ch in range(4):
                            slot = ((j * 4 + ch) * 2 + nt) * 512
                            first = (j == 0 and ch == 0)
                            mm = E.t(
                                lambda h, j=j, ch=ch, oc=oc, slot=slot, db=db,
                                first=first:
                                h.matmul(PS.big[:, db * 512: db * 512 + 512],
                                         B.wdn[j][:, ch * D + oc * P:
                                                   ch * D + oc * P + P],
                                         B.hbuf[:, slot: slot + 512],
                                         start=first,
                                         stop=(j == 3 and ch == 3)),
                                waits=[wdn_tok[0], wdn_tok[1], wdn_tok[2],
                                       wdn_tok[3], dn_free[db], prev_sq[0],
                                       prev_sq[1],
                                       E_last_pool(E)] if first else [])
                    key = (half, nt, oc)
                    if gi == 0:
                        ad = E.v(lambda h, oc=oc, nt=nt, db=db:
                                 h.tensor_copy(
                                     B.acc[:, (oc * 2 + nt) * 512:
                                           (oc * 2 + nt) * 512 + 512],
                                     PS.big[:, db * 512: db * 512 + 512]),
                                 waits=[mm, acc_rd])
                    else:
                        ad = E.v(lambda h, oc=oc, nt=nt, db=db:
                                 h.tensor_tensor(
                                     B.acc[:, (oc * 2 + nt) * 512:
                                           (oc * 2 + nt) * 512 + 512],
                                     PS.big[:, db * 512: db * 512 + 512],
                                     B.acc[:, (oc * 2 + nt) * 512:
                                           (oc * 2 + nt) * 512 + 512],
                                     op=ALU.add),
                                 waits=[mm, first_write[key]])
                    first_write[key] = ad
                    dn_free[db] = ad
            hb_free = mm
            for pd in range(4):
                wdn_free[pd] = [mm]
        for oc in range(DC):
            for nt in range(2):
                ntg = half * 2 + nt
                xr_toks.append(E.v(
                    lambda h, oc=oc, nt=nt, ntg=ntg:
                    h.tensor_tensor(
                        B.xbf[:, oc * T + ntg * 512: oc * T + ntg * 512 + 512],
                        B.acc[:, (oc * 2 + nt) * 512: (oc * 2 + nt) * 512 + 512],
                        B.xbf[:, oc * T + ntg * 512: oc * T + ntg * 512 + 512],
                        op=ALU.add),
                    waits=[first_write[(half, nt, oc)], tok["xbf"],
                           tok["xbf2"]]))
        acc_rd = xr_toks[-1]
    tok["acc_read"] = xr_toks[-1]
    tok["combine_used"] = xr_toks[-1]
    tok["cT_used"] = xr_toks[-1]
    tok["st0"] = st_tok[0]
    tok["st1"] = st_tok[1]
    emit_ln(E, B, PS, tok, xr_toks, with_out)


def E_last_pool(E):
    return ("G", E.cnt["G"])


def build_phaseB(nseq=NSEQ):
    nc = bass.Bass("TRN2", target_bir_lowering=False)
    xin = nc.dram_tensor("xin", [nseq, D, T], F32, kind="ExternalInput").ap()
    xout = nc.dram_tensor("xout", [nseq, D, T], F32, kind="ExternalOutput").ap()
    W = {}
    for name, shp in [("routerT", (DC, P, 64)), ("corr_rep", (P, 64)),
                      ("up_w", (64, DC, P, 512)), ("dn_w", (64, 4, P, D)),
                      ("shexp_up", (4, DC, P, 512)), ("shexp_dn", (4, 4, P, D)),
                      ("onehot_f", (64, 64 * P)), ("ident_f", (P, P))]:
        W[name] = nc.dram_tensor(name, list(shp), F32, kind="ExternalInput").ap()

    with ExitStack() as st:
        B = NS()
        alloc_phaseB(nc, st, B)
        PS = NS()
        PS.big = st.enter_context(nc.psum_tensor("psbig", [P, 2048], F32))
        PS.a = st.enter_context(nc.psum_tensor("psa", [P, 512], F32))
        PS.b = st.enter_context(nc.psum_tensor("psb", [P, 512], F32))
        PS.c = st.enter_context(nc.psum_tensor("psc", [P, 512], F32))
        PS.tr = st.enter_context(nc.psum_tensor("pstr", [P, 1024], BF16))
        sems = {}
        for s in ("T", "V", "S", "G", "Y0", "Y1", "Y2", "Y3",
                  "GD0", "GD1", "GD2", "GD3"):
            sems[s] = st.enter_context(nc.semaphore(s))
        block = st.enter_context(nc.Block())
        E = Emitter(nc, sems)

        def prog():
            tok = {}
            c1 = E.v(lambda h: h.memset(B.ones1[:], 1.0))
            c2 = E.v(lambda h: h.memset(B.ones128[:], 1.0), waits=[c1])
            tok["ones"] = c2
            ld = E.dy(B.stage[0][:, 0:P], W["ident_f"])
            tok["ident"] = E.v(lambda h: h.tensor_copy(B.identbf[:],
                                                       B.stage[0][:, 0:P]),
                               waits=[ld, c2])
            ld2 = E.dy(B.stage[1][:, 0:64], W["corr_rep"])
            tok["corr"] = E.v(lambda h: h.tensor_copy(B.corr[:],
                                                      B.stage[1][:, 0:64]),
                              waits=[ld2])
            rt = None
            for kc in range(DC):
                si = kc % 2
                w0 = [tok["ident"] if si == 0 else tok["corr"]] if kc < 2 else                      [rt]
                ld3 = E.dy(B.stage[si][:, 0:64], W["routerT"][kc], waits=w0)
                rt = E.v(lambda h, kc=kc, si=si:
                         h.tensor_copy(B.routerF[:, kc * 64: kc * 64 + 64],
                                       B.stage[si][:, 0:64]), waits=[ld3])
            tok["router"] = rt
            oh = rt
            for c in range(4):
                si = c % 2
                ld4 = E.dy(B.stage[si][0:64, 0:T],
                           W["onehot_f"][0:64, c * T:(c + 1) * T], waits=[oh])
                oh = E.v(lambda h, c=c, si=si:
                         h.tensor_copy(B.onehot[0:64, c * T:(c + 1) * T],
                                       B.stage[si][0:64, 0:T]), waits=[ld4])
            tok["onehot"] = oh
            tok["st0"] = oh
            tok["st1"] = oh
            for s_ in range(nseq):
                emit_load_x(E, B, xin, s_, tok)
                emit_phaseB_seq(E, B, PS, W, tok, with_out=xout[s_, :, :])
            for k in ("GD0", "GD1", "GD2", "GD3"):
                E.op("gpsimd",
                     lambda h, k=k: h.wait_ge(sems[k], E.cnt[k]), inc=0)

        for name in Emitter.ENGINES:
            getattr(block, name)(lambda h, name=name: (E.start(name, h), prog()))
    return nc



def alloc_phaseC(nc, st, bufs):
    sb = lambda n, s, dt: st.enter_context(nc.sbuf_tensor(n, s, dt))
    B = bufs
    B.xbf = sb("xbf", [P, DC * T], BF16)
    B.stage = [sb(f"stage{i}", [P, T], F32) for i in range(2)]
    B.sq = [sb(f"sq{i}", [P, 512], BF16) for i in range(2)]
    B.statrow = sb("statrow", [1, 512], F32)
    B.statbf = sb("statbf", [1, 512], BF16)
    B.statbf2 = sb("statbf2", [1, 512], BF16)
    B.ones1 = sb("ones1", [1, P], BF16)
    B.ones128 = sb("ones128", [P, 1], BF16)
    B.identbf = sb("identbf", [P, P], BF16)
    B.trilT = sb("trilT", [64, 16 * 64], BF16)
    B.wqk = sb("wqk", [P, DC * P], BF16)
    B.tbuf = sb("tbuf", [P, DC * T], BF16)
    B.vhC = sb("vhC", [P, 16 * 16 * 66], BF16)
    B.probsT = sb("probsT", [P, 8 * 512], BF16)
    B.xst = [sb(f"xst{i}", [P, T], BF16) for i in range(2)]
    B.taug = [sb(f"taug{i}", [P, 512], BF16) for i in range(2)]
    B.mbuf = sb("mbuf", [P, 16], F32)
    B.mbufb = sb("mbufb", [P, 16], BF16)
    B.mtr = sb("mtr", [16, P], BF16)
    B.mrow = [sb(f"mrow{i}", [1, T], BF16) for i in range(2)]
    B.wvF = sb("wvF", [P, DC * D], BF16)
    B.woF = sb("woF", [P, DC * D], BF16)
    B.bvrep = sb("bvrep", [P, D], BF16)
    B.boF = sb("boF", [P, DC], F32)
    B.rrow = sb("rrow", [1, 512], BF16)
    B.rbc = sb("rbc", [64, 512], BF16)


def emit_phaseC_prologue(E, B, PS, W, tok):
    c1 = E.v(lambda h: h.memset(B.ones1[:], 1.0))
    c2 = E.v(lambda h: h.memset(B.ones128[:], 1.0), waits=[c1])
    tok["ones"] = c2
    ld = E.dy(B.stage[0][:, 0:P], W["ident_f"])
    tok["ident"] = E.v(lambda h: h.tensor_copy(B.identbf[:], B.stage[0][:, 0:P]),
                       waits=[ld, c2])
    ld2 = E.dy(B.stage[1][:, 0:1024], W["trilT_f"])
    ttr = E.v(lambda h: h.tensor_copy(B.trilT[:], B.stage[1][0:64, 0:1024]),
              waits=[ld2])
    mz = E.v(lambda h: h.memset(B.wqk[:], 0.0), waits=[ttr])
    wq = None
    for h_ in range(H):
        mm = E.t(lambda h, h_=h_:
                 h.matmul(PS.a[0:64, 0:64],
                          B.trilT[0:64, h_ * 64: h_ * 64 + 64],
                          B.trilT[0:64, h_ * 64: h_ * 64 + 64],
                          start=True, stop=True),
                 waits=[mz, tok.get("psa_free"), wq])
        r0 = (h_ % 2) * 64
        wq = E.v(lambda h, h_=h_, r0=r0:
                 h.tensor_scalar_mul(
                     B.wqk[r0:r0 + 64, (h_ // 2) * P + r0: (h_ // 2) * P + r0 + 64],
                     PS.a[0:64, 0:64], 0.125),
                 waits=[mm])
    tok["wqk"] = wq
    tok["psa_free"] = wq
    jobs = [("wvT", B.wvF, D), ("woT2", B.woF, D)]
    last = [tok["ident"], ttr]
    for name, dst, cw in jobs:
        for c in range(DC):
            i = c % 2
            ldw = E.dy(B.stage[i][:, 0:cw], W[name][c], waits=[last[i]])
            last[i] = E.v(lambda h, dst=dst, c=c, cw=cw, i=i:
                          h.tensor_copy(dst[:, c * cw:(c + 1) * cw],
                                        B.stage[i][:, 0:cw]), waits=[ldw])
    ldb = E.dy(B.stage[0][:, 0:D], W["bv_rep"], waits=[last[0]])
    bv = E.v(lambda h: h.tensor_copy(B.bvrep[:], B.stage[0][:, 0:D]),
             waits=[ldb])
    ldb2 = E.dy(B.boF[:], W["boF_in"], waits=[])
    tok["wsmallC"] = bv
    tok["boF"] = ldb2
    tok["st0"] = bv
    tok["st1"] = last[1]
    o1 = E.v(lambda h: h.memset(B.xst[0][:], 1.0), waits=[bv])
    o2 = E.v(lambda h: h.memset(B.xst[1][:], 1.0), waits=[o1])
    tok["xst_ones"] = o2
    vz = E.v(lambda h: h.memset(B.vhC[:], 1.0), waits=[o2])
    tok["vh_ones"] = vz


def emit_phaseC_seq(E, B, PS, tok, with_out=None):
    fence = tok.get("fence", [])
    xw = [tok["xbf"], tok["xbf2"]]
    banks = [PS.a, PS.b, PS.c]
    bank_free = [tok.get("psa_free"), tok.get("psb_free"), tok.get("psc_free")]
    bi = 0

    def next_bank():
        nonlocal bi
        b = bi % 3
        bi += 1
        return b

    for c in range(DC):
        for nt in range(NT):
            b = next_bank()
            mm = E.t(lambda h, c=c, nt=nt, b=b:
                     h.matmul(banks[b][:, 0:512],
                              B.wqk[:, c * P: c * P + P],
                              B.xbf[:, c * T + nt * 512: c * T + nt * 512 + 512],
                              start=True, stop=True),
                     waits=[tok["wqk"], xw[0], xw[1], bank_free[b]] + fence)
            bank_free[b] = E.v(lambda h, c=c, nt=nt, b=b:
                               h.tensor_copy(B.tbuf[:, c * T + nt * 512:
                                                    c * T + nt * 512 + 512],
                                             banks[b][:, 0:512]),
                               waits=[mm] + fence)
    t_tok = bank_free[(bi - 1) % 3]

    vh_toks = []
    for tt in range(16):
        for dh in range(2):
            b = next_bank()
            mm = None
            for kc in range(DC):
                mm = E.t(lambda h, tt=tt, dh=dh, kc=kc, b=b:
                         h.matmul(banks[b][:, 0:512],
                                  B.xbf[:, kc * T + tt * P: kc * T + tt * P + P],
                                  B.wvF[:, kc * D + dh * 512: kc * D + dh * 512 + 512],
                                  start=(kc == 0), stop=(kc == DC - 1)),
                         waits=[tok["wsmallC"], bank_free[b], tok["vh_ones"]]
                         if kc == 0 else [])
            base = tt * 1056 + dh * 8 * 66
            out3 = B.vhC[:, base: base + 8 * 66].rearrange(
                "p (h w) -> p h w", w=66)[:, :, 0:64]
            ps3 = banks[b].rearrange("p (h w) -> p h w", w=64)
            bv3 = B.bvrep[:, dh * 512: dh * 512 + 512].rearrange(
                "p (h w) -> p h w", w=64)
            bank_free[b] = E.v(lambda h, out3=out3, ps3=ps3, bv3=bv3:
                               h.tensor_tensor(out3, ps3, bv3, op=ALU.add),
                               waits=[mm] + fence)
            vh_toks.append(bank_free[b])
    vh_tok = bank_free[(bi - 1) % 3]

    stage_free = [tok.get("stage_freeC0"), tok.get("stage_freeC1")]
    taug_free = [tok.get("taug_free0"), tok.get("taug_free1")]
    y_toks = []
    half_free = [None, None]
    st = NS()
    st.last_av = tok.get("av_prev")
    st.mh = [None, None]
    st.md = [None, None]

    def emit_mpass(h_):
        mx = tok.get("mbuf_used")
        for qt in range(16):
            rms = []
            for kh in range(2):
                hf = (qt * 2 + kh) % 2
                mm = None
                for kn in range(2):
                    mm = E.t(lambda h, h_=h_, qt=qt, kh=kh, kn=kn, hf=hf:
                             h.matmul(PS.big[:, hf * 1024 + kn * 512:
                                             hf * 1024 + kn * 512 + 512],
                                      B.tbuf[(h_ % 2) * 64:(h_ % 2) * 64 + 64,
                                             (h_ // 2) * T + qt * P:
                                             (h_ // 2) * T + qt * P + P],
                                      B.xbf[(h_ % 2) * 64:(h_ % 2) * 64 + 64,
                                            (h_ // 2) * T + kh * 1024 + kn * 512:
                                            (h_ // 2) * T + kh * 1024 + kn * 512 + 512],
                                      start=True, stop=True),
                             waits=[t_tok, xw[0], xw[1], half_free[hf],
                                    st.last_av])
                rm = E.v(lambda h, qt=qt, kh=kh, hf=hf:
                         h.reduce_max(B.mbufb[0:P, kh: kh + 1],
                                      PS.big[:, hf * 1024: hf * 1024 + 1024],
                                      axis=AX.X),
                         waits=[mm, st.mh[kh], mx, tok.get("mbuf_used")])
                rms.append(rm)
                half_free[hf] = rm
                st.mh[kh] = rm
            mx = E.v(lambda h, qt=qt:
                     h.tensor_tensor(B.mbuf[:, qt: qt + 1],
                                     B.mbufb[0:P, 0:1], B.mbufb[0:P, 1:2],
                                     op=ALU.max),
                     waits=rms + [mx])
        cst = E.v(lambda h: h.tensor_scalar_mul(B.mbufb[:], B.mbuf[:], -1.0),
                  waits=[mx])
        tok["mbuf_used"] = cst
        tr = E.t(lambda h: h.transpose(PS.tr[0:16, 0:P], B.mbufb[:],
                                       B.identbf[:]),
                 waits=[cst, tok.get("pstr_free"), tok["ident"]])
        ev = E.v(lambda h: h.tensor_copy(B.mtr[:], PS.tr[0:16, 0:P]),
                 waits=[tr, tok.get("mtr_used")])
        tok["pstr_free"] = ev
        st.md[h_ % 2] = E.dg(B.mrow[h_ % 2], B.mtr[:],
                             waits=[ev, tok.get(f"mrow_used{h_ % 2}")])
        tok["mtr_used"] = st.md[h_ % 2]

    def emit_attn(h_):
        i = h_ % 2
        xs = E.v(lambda h, i=i, h_=h_:
                 h.tensor_copy(B.xst[i][0:64, :],
                               B.xbf[(h_ % 2) * 64:(h_ % 2) * 64 + 64,
                                     (h_ // 2) * T:(h_ // 2) * T + T]),
                 waits=[xw[0], xw[1], stage_free[i], tok["xst_ones"]])
        prob_tok = {}
        for qg in range(NT):
            i2 = qg % 2
            avb = PS.a if qg % 2 == 0 else PS.c
            tg1 = E.v(lambda h, i2=i2, h_=h_, qg=qg:
                      h.tensor_copy(B.taug[i2][0:64, :],
                                    B.tbuf[(h_ % 2) * 64:(h_ % 2) * 64 + 64,
                                           (h_ // 2) * T + qg * 512:
                                           (h_ // 2) * T + qg * 512 + 512]),
                      waits=[t_tok, taug_free[i2]])
            tg2 = E.v(lambda h, i2=i2, qg=qg, i=i:
                      h.tensor_copy(B.taug[i2][64:65, :],
                                    B.mrow[i][0:1, qg * 512: qg * 512 + 512]),
                      waits=[st.md[i], tg1])
            tok[f"mrow_used{i}"] = tg2
            av = None
            for kb in range(8):
                kts = [2 * kb, 2 * kb + 1]
                hf = kb % 2
                mms = []
                for kt in kts:
                    sl = (hf * 2 + (kt % 2)) * 512
                    mms.append(E.t(
                        lambda h, i=i, i2=i2, kt=kt, sl=sl:
                        h.matmul(PS.big[:, sl:sl + 512],
                                 B.xst[i][0:65, kt * P: kt * P + P],
                                 B.taug[i2][0:65, 0:512],
                                 start=True, stop=True),
                        waits=[xs, tg2, half_free[hf], st.last_av]))
                rg = kts[0] % 8
                region_free = av if kts[0] >= 8 else st.last_av
                ex = E.s(
                    lambda h, rg=rg, sl0=hf * 1024:
                    h.activation(B.probsT[:, rg * 512: rg * 512 + 1024],
                                 PS.big[:, sl0: sl0 + 1024], AF.Exp),
                    waits=mms + [region_free])
                half_free[hf] = ex
                for kt in kts:
                    prob_tok[kt] = ex
                if kts[-1] == 7:
                    for kt2 in range(8):
                        av = E.t(lambda h, h_=h_, kt2=kt2, avb=avb,
                                 first=(kt2 == 0):
                                 h.matmul(avb[0:65, 0:512],
                                          B.vhC[:, kt2 * 1056 + h_ * 66:
                                                kt2 * 1056 + h_ * 66 + 65],
                                          B.probsT[:, kt2 * 512: kt2 * 512 + 512],
                                          start=first, stop=False),
                                 waits=([prob_tok[k2] for k2 in range(8)] +
                                        [tok.get(f"yevC{qg % 2}"),
                                         tok.get("rrow_used")] + vh_toks)
                                 if kt2 == 0 else [])
            for kt in range(8, 16):
                av = E.t(lambda h, h_=h_, kt=kt, avb=avb, last=(kt == 15):
                         h.matmul(avb[0:65, 0:512],
                                  B.vhC[:, kt * 1056 + h_ * 66:
                                        kt * 1056 + h_ * 66 + 65],
                                  B.probsT[:, (kt % 8) * 512: (kt % 8) * 512 + 512],
                                  start=False, stop=last),
                         waits=[prob_tok[k2] for k2 in range(8, 16)]
                         if kt == 8 else [])
            st.last_av = av
            taug_free[i2] = av
            rr0 = E.s(lambda h, avb=avb: h.activation(B.rrowf[:],
                                                      avb[64:65, 0:512],
                                                      AF.Ln),
                      waits=[av, tok.get("rbc_used"), tok.get("rrowf_used")])
            rr = E.s(lambda h: h.activation(B.rrow[:], B.rrowf[:], AF.Exp,
                                            scale=-1.0),
                     waits=[rr0])
            tok["rrowf_used"] = rr
            bc = E.t(lambda h: h.matmul(PS.b[0:64, 0:512], B.ones1[:, 0:64],
                                        B.rrow[:], start=True, stop=True),
                     waits=[rr, tok.get("rbc_used")])
            rbc_ev = E.v(lambda h: h.tensor_copy(B.rbc[:], PS.b[0:64, 0:512]),
                         waits=[bc, tok.get("yev")])
            tok["rbc_used"] = rbc_ev
            tok["rrow_used"] = rbc_ev
            yev = E.v(lambda h, h_=h_, qg=qg, avb=avb:
                      h.tensor_tensor(B.tbuf[(h_ % 2) * 64:(h_ % 2) * 64 + 64,
                                             (h_ // 2) * T + qg * 512:
                                             (h_ // 2) * T + qg * 512 + 512],
                                      avb[0:64, 0:512], B.rbc[:],
                                      op=ALU.mult), waits=[rbc_ev])
            tok[f"yevC{qg % 2}"] = yev
            tok["yev"] = yev
            y_toks.append(yev)
        stage_free[i] = st.last_av

    emit_mpass(0)
    for h_ in range(H):
        if h_ + 1 < H:
            emit_mpass(h_ + 1)
        emit_attn(h_)

    last_av = st.last_av
    tok["av_prev"] = last_av
    tok["stage_freeC0"] = stage_free[0]
    tok["stage_freeC1"] = stage_free[1]
    tok["taug_free0"] = taug_free[0]
    tok["taug_free1"] = taug_free[1]

    emit_out_ln(E, B, PS, tok, w_img=B.woF, w_tok=tok["wsmallC"],
                src=B.tbuf, src_toks=[y_toks[-1]], nk=DC, with_out=with_out,
                bank_free=[bank_free[0], bank_free[1], bank_free[2]],
                bias_cols=B.boF, bias_tok=[tok["boF"]])


def build_phaseC(nseq=NSEQ):
    nc = bass.Bass("TRN2", target_bir_lowering=False)
    xin = nc.dram_tensor("xin", [nseq, D, T], F32, kind="ExternalInput").ap()
    xout = nc.dram_tensor("xout", [nseq, D, T], F32, kind="ExternalOutput").ap()
    W = {}
    for name, shp in [("trilT_f", (P, 1024)), ("wvT", (DC, P, D)),
                      ("woT2", (DC, P, D)), ("bv_rep", (P, D)),
                      ("boF_in", (P, DC)), ("ident_f", (P, P))]:
        W[name] = nc.dram_tensor(name, list(shp), F32, kind="ExternalInput").ap()

    with ExitStack() as st:
        B = NS()
        alloc_phaseC(nc, st, B)
        PS = NS()
        PS.big = st.enter_context(nc.psum_tensor("psbig", [P, 2048], F32))
        PS.a = st.enter_context(nc.psum_tensor("psa", [P, 512], F32))
        PS.b = st.enter_context(nc.psum_tensor("psb", [P, 512], F32))
        PS.c = st.enter_context(nc.psum_tensor("psc", [P, 512], F32))
        PS.tr = st.enter_context(nc.psum_tensor("pstr", [P, 1024], BF16))
        sems = {}
        for s in ("T", "V", "S", "G", "Y0", "Y1", "Y2", "Y3",
                  "GD0", "GD1", "GD2", "GD3"):
            sems[s] = st.enter_context(nc.semaphore(s))
        block = st.enter_context(nc.Block())
        E = Emitter(nc, sems)

        def prog():
            tok = {}
            emit_phaseC_prologue(E, B, PS, W, tok)
            for s_ in range(nseq):
                emit_load_x(E, B, xin, s_, tok)
                emit_phaseC_seq(E, B, PS, tok, with_out=xout[s_, :, :])
            for k in ("GD0", "GD1", "GD2", "GD3"):
                E.op("gpsimd",
                     lambda h, k=k: h.wait_ge(sems[k], E.cnt[k]), inc=0)

        for name in Emitter.ENGINES:
            getattr(block, name)(lambda h, name=name: (E.start(name, h), prog()))
    return nc



def alloc_phaseD(nc, st, bufs):
    sb = lambda n, s, dt: st.enter_context(nc.sbuf_tensor(n, s, dt))
    B = bufs
    B.xbf = sb("xbf", [P, DC * T], BF16)
    B.stage = [sb(f"stage{i}", [P, T], F32) for i in range(2)]
    B.sq = [sb(f"sq{i}", [P, 512], BF16) for i in range(2)]
    B.statrow = sb("statrow", [1, 512], F32)
    B.statbf = sb("statbf", [1, 512], BF16)
    B.statbf2 = sb("statbf2", [1, 512], BF16)
    B.ones1 = sb("ones1", [1, P], BF16)
    B.ones128 = sb("ones128", [P, 1], BF16)
    B.identbf = sb("identbf", [P, P], BF16)
    B.wbig = sb("wbig", [P, DC * D], BF16)
    B.wbig2 = sb("wbig2", [P, DC * D], BF16)
    B.wv1 = sb("wv1", [P, DC * 512], BF16)
    B.wv2 = sb("wv2", [P, DC * 512], BF16)
    B.xV = sb("xV", [P, 4 * T], BF16)
    B.xVi = sb("xVi", [P, 4 * T], BF16)
    B.yW = sb("yW", [P, 4 * T], BF16)
    B.cum = sb("cum", [P, 4 * T], BF16)
    B.wrep = sb("wrep", [P, 4 * T], BF16)
    B.h8 = sb("h8", [P, DC * P], BF16)
    B.o1 = sb("o1", [P, DC * 512], BF16)
    B.zer = sb("zer", [P, 512], BF16)
    B.carry = sb("carry", [P, 4], F32)
    B.wtok = sb("wtok", [P, 16 * 8], F32)
    B.wtb = sb("wtb", [P, 8], BF16)
    B.wT = sb("wT", [8, T], BF16)
    B.rw2F = sb("rw2F", [P, DC * 8], BF16)
    B.b1F = sb("b1F", [P, DC], F32)
    B.b2eb = sb("b2eb", [P, 8], F32)
    B.alphac = sb("alphac", [P, 1], F32)
    B.bfw = sb("bfw", [8, D], BF16)
    B.binv = sb("binv", [8, D], BF16)
    B.rep8 = sb("rep8", [8, 512], BF16)
    B.invn = sb("invn", [P, T], BF16)
    B.lgD = sb("lgD", [P, 8], F32)
    B.rmD = sb("rmD", [P, 1], F32)


def emit_fold(E, B, PS, lhs_img, lhs_tok, rhs_img, rhs_tok, out_img, dout,
              free_tok, tag, out_free=None):
    toks = []
    ti = 0
    for dc in range(DC):
        for n in range(dout // 512):
            bk = ti % 4
            ti += 1
            mm = None
            for kc in range(DC):
                mm = E.t(lambda h, dc=dc, n=n, kc=kc, bk=bk:
                         h.matmul(PS.big[:, bk * 512: bk * 512 + 512],
                                  lhs_img[:, kc * D + dc * P: kc * D + dc * P + P],
                                  rhs_img[:, kc * dout + n * 512:
                                          kc * dout + n * 512 + 512],
                                  start=(kc == 0), stop=(kc == DC - 1)),
                         waits=[lhs_tok, rhs_tok, free_tok,
                                toks[-4] if len(toks) >= 4 else None]
                         if kc == 0 else [])
            toks.append(E.v(lambda h, dc=dc, n=n, bk=bk:
                            h.tensor_copy(out_img[:, dc * dout + n * 512:
                                                  dc * dout + n * 512 + 512],
                                          PS.big[:, bk * 512: bk * 512 + 512]),
                            waits=[mm, out_free]))
    return toks[-1]


def loadw(E, B, W, name, dst, cw, tok, key=None):
    nch = W[name].shape[0]
    last = None
    for c in range(nch):
        i = c % 2
        ld = E.dy(B.stage[i][:, 0:cw], W[name][c],
                  waits=[tok.get("st0") if i == 0 else tok.get("st1")] +
                  list(tok.get("fence", [])))
        ct = E.v(lambda h, dst=dst, c=c, cw=cw, i=i:
                 h.tensor_copy(dst[:, c * cw:(c + 1) * cw],
                               B.stage[i][:, 0:cw]), waits=[ld])
        if i == 0:
            tok["st0"] = ct
        else:
            tok["st1"] = ct
        last = ct
    if key:
        tok[key] = last
    return last


def emit_phaseD_prologue(E, B, PS, W, scratch, tok):
    c1 = E.v(lambda h: h.memset(B.ones1[:], 1.0))
    c2 = E.v(lambda h: h.memset(B.ones128[:], 1.0), waits=[c1])
    c3 = E.v(lambda h: h.memset(B.zer[:], 0.0), waits=[c2])
    tok["ones"] = c2
    tok["zer"] = c3
    ld = E.dy(B.stage[0][:, 0:P], W["ident_f"])
    tok["ident"] = E.v(lambda h: h.tensor_copy(B.identbf[:], B.stage[0][:, 0:P]),
                       waits=[ld, c3])
    tok["st0"] = tok["ident"]
    tok["rw2"] = loadw(E, B, W, "rw2T_in", B.rw2F, 8, tok)
    tok["invn_tok"] = loadw(E, B, W, "invn_in", B.invn, T, tok)
    ld8 = E.dy(B.stage[0][0:8, 0:512], W["rep8_in"], waits=[tok.get("st0")])
    tok["rep8"] = E.v(lambda h: h.tensor_copy(B.rep8[:], B.stage[0][0:8, 0:512]),
                      waits=[ld8])
    tok["st0"] = tok["rep8"]
    ld9 = E.dy(B.stage[1][0:8, 0:D], W["bfw_in"], waits=[tok.get("st1")])
    tok["bfw"] = E.v(lambda h: h.tensor_copy(B.bfw[:], B.stage[1][0:8, 0:D]),
                     waits=[ld9])
    tok["st1"] = tok["bfw"]
    ldb = E.dy(B.b1F[:], W["b1F_in"], waits=[tok.get("st0")])
    ldb2 = E.dy(B.stage[0][:, 0:8], W["b2rep_in"], waits=[tok.get("st0")])
    ldb3 = E.dy(B.stage[1][:, 0:8], W["ebrep_in"], waits=[tok.get("st1")])
    b2 = E.v(lambda h: h.tensor_add(B.b2eb[:], B.stage[0][:, 0:8],
                                    B.stage[1][:, 0:8]), waits=[ldb2, ldb3])
    lda = E.dy(B.alphac[:], W["alpha_in"], waits=[])
    tok["b1F"] = ldb
    tok["b2eb"] = b2
    tok["alpha"] = lda
    tok["st0"] = b2
    tok["st1"] = b2
    ldc = E.dy(B.stage[0][0:8, 0:D], W["binv_in"], waits=[tok.get("st0")])
    bi = E.v(lambda h: h.tensor_scalar_mul(B.binv[:], B.stage[0][0:8, 0:D],
                                           B.alphac[0:8, 0:1]),
             waits=[ldc, lda])
    tok["binv"] = bi
    tok["st0"] = bi
    t1 = loadw(E, B, W, "wq_n", B.wbig, D, tok)
    t2 = loadw(E, B, W, "vfF", B.wv1, 512, tok)
    f1 = emit_fold(E, B, PS, B.wbig, t1, B.wv1, t2, B.wv2, 512, None, "vf")
    d1 = E.dg(scratch["d_vf"], B.wv2[:], waits=[f1])
    t3 = loadw(E, B, W, "rw1T", B.wbig2, D, tok)
    dlast = [None, None, None, None]
    ti = 0
    f2 = f1
    for dc in range(DC):
        for n in range(2):
            bk = ti % 4
            mm = None
            for kc in range(DC):
                mm = E.t(lambda h, dc=dc, n=n, kc=kc, bk=bk:
                         h.matmul(PS.big[:, bk * 512: bk * 512 + 512],
                                  B.wbig[:, kc * D + dc * P: kc * D + dc * P + P],
                                  B.wbig2[:, kc * D + n * 512: kc * D + n * 512 + 512],
                                  start=(kc == 0), stop=(kc == DC - 1)),
                         waits=[t1, t3, f1, f2] if kc == 0 else [])
            ev = E.v(lambda h, bk=bk:
                     h.tensor_copy(B.wv1[:, bk * 512: bk * 512 + 512],
                                   PS.big[:, bk * 512: bk * 512 + 512]),
                     waits=[mm, dlast[bk], d1])
            dlast[bk] = E.dg(scratch["d_rw1"][:, dc * D + n * 512:
                                              dc * D + n * 512 + 512],
                             B.wv1[:, bk * 512: bk * 512 + 512], waits=[ev])
            f2 = ev
            ti += 1
    d2 = dlast[(ti - 1) % 4]
    t5 = loadw(E, B, W, "winv_n", B.wbig2, D, tok)
    t4 = loadw(E, B, W, "vroleIF", B.wv1, 512, tok)
    f3 = emit_fold(E, B, PS, B.wbig2, t5, B.wv1, t4, B.wv2, 512,
                   dlast[0], "ivf", out_free=d1)
    for dd in dlast:
        f3 = f3
    f4 = emit_fold(E, B, PS, B.wbig, t1, B.wv2, f3, B.wv1, 512, f3, "vi",
                   out_free=d2)
    d3 = E.dg(scratch["d_vi"], B.wv1[:], waits=[f4])
    t6 = loadw(E, B, W, "wk_n", B.wbig, D, tok)
    t7 = loadw(E, B, W, "wfF", B.wv2, 512, tok)
    f5 = emit_fold(E, B, PS, B.wbig, t6, B.wv2, t7, B.wv1, 512, f4,
                   "wf", out_free=d3)
    d4 = E.dg(scratch["d_wf"], B.wv1[:], waits=[f5])
    t8 = loadw(E, B, W, "wroleIF", B.wv2, 512, tok)
    f6 = emit_fold(E, B, PS, B.wbig2, t5, B.wv2, t8, B.wv1, 512, f5, "iwf",
                   out_free=d4)
    f7 = emit_fold(E, B, PS, B.wbig, t6, B.wv1, f6, B.wv2, 512, f6, "wi",
                   out_free=d4)
    d5 = E.dg(scratch["d_wi"], B.wv2[:], waits=[f7])
    tok["folds"] = f7
    tok["folds_dma"] = [d1, d2, d3, d4, d5]
    tok["st0"] = f7
    tok["st1"] = f7


def emit_phaseD_seq(E, B, PS, W, scratch, tok, with_out=None):
    fence = tok.get("fence", [])
    xw = [tok["xbf"], tok["xbf2"]]
    banks = [PS.a, PS.b, PS.c]
    bank_free = [tok.get("psa_free"), tok.get("psb_free"), tok.get("psc_free")]
    bi = 0

    def next_bank():
        nonlocal bi
        b = bi % 3
        bi += 1
        return b

    ld = E.dy(B.wbig[:], scratch["d_rw1"],
              waits=tok["folds_dma"] + [tok.get("wbig_free")] + fence)
    prevh = None
    prevw = {}
    for tt in range(16):
        b = next_bank()
        ht = None
        for oc in range(DC):
            mm = None
            for kc in range(DC):
                mm = E.t(lambda h, tt=tt, oc=oc, kc=kc, b=b:
                         h.matmul(banks[b][:, 0:P],
                                  B.wbig[:, kc * D + oc * P: kc * D + oc * P + P],
                                  B.xbf[:, kc * T + tt * P: kc * T + tt * P + P],
                                  start=(kc == 0), stop=(kc == DC - 1)),
                         waits=[ld, xw[0], xw[1], bank_free[b]] + fence
                         if kc == 0 else [])
            hh = E.s(lambda h, oc=oc, b=b:
                     h.activation(B.h8[:, oc * P: oc * P + P],
                                  banks[b][:, 0:P], AF.Gelu,
                                  bias=B.b1F[:, oc: oc + 1]),
                     waits=[mm, tok["b1F"], prevh])
            bank_free[b] = hh
            ht = hh
        prevh = ht
        b2 = next_bank()
        mm = None
        for kc in range(DC):
            mm = E.t(lambda h, kc=kc, b2=b2:
                     h.matmul(banks[b2][:, 0:8],
                              B.h8[:, kc * P: kc * P + P],
                              B.rw2F[:, kc * 8: kc * 8 + 8],
                              start=(kc == 0), stop=(kc == DC - 1)),
                     waits=[ht, bank_free[b2], tok.get("rw2")]
                     if kc == 0 else [])
        l0 = E.v(lambda h, b2=b2: h.tensor_add(B.lgD[:], banks[b2][:, 0:8],
                                               B.b2eb[:]),
                 waits=[mm, tok["b2eb"], prevw.get("lg")] + fence)
        bank_free[b2] = l0
        l1 = E.v(lambda h: h.reduce_max(B.rmD[:], B.lgD[:], axis=AX.X),
                 waits=[l0, prevw.get("rm")])
        l2 = E.v(lambda h: h.tensor_scalar_mul(B.rmD[:], B.rmD[:], -1.0),
                 waits=[l1])
        l3 = E.s(lambda h: h.activation(B.lgD[:], B.lgD[:], AF.Exp,
                                        bias=B.rmD[:, 0:1]), waits=[l2])
        l4 = E.v(lambda h: h.reduce_sum(B.rmD[:], B.lgD[:], axis=AX.X),
                 waits=[l3])
        l5 = E.v(lambda h: h.reciprocal(B.rmD[:], B.rmD[:]), waits=[l4])
        l6 = E.v(lambda h, tt=tt:
                 h.tensor_scalar_mul(B.wtok[:, tt * 8: tt * 8 + 8],
                                     B.lgD[:], B.rmD[:, 0:1]),
                 waits=[l5, tok.get("wtok_used")])
        l7b = E.v(lambda h, tt=tt:
                  h.tensor_copy(B.wtb[:], B.wtok[:, tt * 8: tt * 8 + 8]),
                  waits=[l6, prevw.get("wtb")])
        tr = E.t(lambda h: h.transpose(PS.tr[0:8, 0:P], B.wtb[:], B.identbf[:]),
                 waits=[l7b, tok.get("pstr_free"), tok["ident"]])
        ev = E.v(lambda h, tt=tt:
                 h.tensor_copy(B.wT[:, tt * P: tt * P + P], PS.tr[0:8, 0:P]),
                 waits=[tr, tok.get("wT_used")])
        tok["pstr_free"] = ev
        prevw = {"lg": l6, "rm": l6, "wtb": tr}
        tok["wtok_wr"] = l6
    wT_tok = ev
    tok.pop("wT_used", None)
    tok.pop("wtok_used", None)

    wrep_toks = []
    for c in range(4):
        for nt in range(NT):
            b = next_bank()
            mm = E.t(lambda h, c=c, nt=nt, b=b:
                     h.matmul(banks[b][:, 0:512],
                              B.rep8[:, c * P: c * P + P],
                              B.wT[:, nt * 512: nt * 512 + 512],
                              start=True, stop=True),
                     waits=[wT_tok, tok["rep8"], bank_free[b]])
            bank_free[b] = E.v(lambda h, c=c, nt=nt, b=b:
                               h.tensor_copy(B.wrep[:, c * T + nt * 512:
                                                    c * T + nt * 512 + 512],
                                             banks[b][:, 0:512]),
                               waits=[mm, tok.get("wrep_used")])
            wrep_toks.append(bank_free[b])
    wrep_tok = wrep_toks[-1]
    tok.pop("wrep_used", None)

    banks4 = [PS.a[:, 0:512], PS.b[:, 0:512], PS.c[:, 0:512],
              PS.big[:, 0:512]]
    b4_free = [bank_free[0], bank_free[1], bank_free[2], None]

    def xv_pass(eff_scr, dst, dst_used, wslot, wfree):
        lw = E.dy(wslot[:], eff_scr, waits=[wfree] + fence)
        last = None
        for c in range(4):
            mm4 = [None] * 4
            for kc in range(DC):
                for nt in range(NT):
                    mm4[nt] = E.t(
                        lambda h, c=c, nt=nt, kc=kc, wslot=wslot:
                        h.matmul(banks4[nt],
                                 wslot[:, kc * 512 + c * P:
                                       kc * 512 + c * P + P],
                                 B.xbf[:, kc * T + nt * 512:
                                       kc * T + nt * 512 + 512],
                                 start=(kc == 0), stop=(kc == DC - 1)),
                        waits=[lw, xw[0], xw[1], b4_free[nt]]
                        if kc == 0 else [])
            for nt in range(NT):
                b4_free[nt] = E.v(
                    lambda h, c=c, nt=nt, dst=dst:
                    h.tensor_copy(dst[:, c * T + nt * 512:
                                      c * T + nt * 512 + 512],
                                  banks4[nt]),
                    waits=[mm4[nt], dst_used])
                last = b4_free[nt]
        return last, lw

    def cum_pass(src_tok, used=None):
        cz = E.v(lambda h: h.memset(B.carry[:], 0.0),
                 waits=[tok.get("carry_used"), src_tok])
        ctk = cz
        cts = []
        for nt in range(NT):
            for c in range(4):
                sc = E.v(lambda h, c=c, nt=nt:
                         h.tensor_tensor_scan(
                             B.cum[:, c * T + nt * 512: c * T + nt * 512 + 512],
                             B.yW[:, c * T + nt * 512: c * T + nt * 512 + 512],
                             B.zer[:], B.carry[:, c: c + 1],
                             op0=ALU.add, op1=ALU.add),
                         waits=[ctk, used, tok["zer"]])
                ctk = E.v(lambda h, c=c, nt=nt:
                          h.tensor_copy(B.carry[:, c: c + 1],
                                        B.cum[:, c * T + nt * 512 + 511:
                                              c * T + nt * 512 + 512]),
                          waits=[sc])
            cts.append(ctk)
        tok["carry_used"] = ctk
        return cts

    def prod_pass(xv_buf, xv_tok, cum_toks, use_alpha):
        pt = None
        out = []
        for nt in range(NT):
            for c in range(4):
                sl = slice(c * T + nt * 512, c * T + nt * 512 + 512)
                p1 = E.v(lambda h, sl=sl, xv_buf=xv_buf:
                         h.tensor_mul(xv_buf[:, sl], xv_buf[:, sl],
                                      B.cum[:, sl]),
                         waits=[xv_tok, cum_toks[nt], pt])
                if use_alpha:
                    pt = E.v(lambda h, sl=sl, xv_buf=xv_buf:
                             h.scalar_tensor_tensor(
                                 xv_buf[:, sl], xv_buf[:, sl],
                                 B.alphac[:, 0:1], B.wrep[:, sl],
                                 op0=ALU.mult, op1=ALU.mult),
                             waits=[p1, wrep_tok, tok["alpha"]])
                else:
                    pt = E.v(lambda h, sl=sl, xv_buf=xv_buf:
                             h.tensor_mul(xv_buf[:, sl], xv_buf[:, sl],
                                          B.wrep[:, sl]),
                             waits=[p1, wrep_tok])
            out.append(pt)
        return out

    xvf_tok, _ = xv_pass(scratch["d_vf"], B.xV, tok.get("xV_used"),
                         B.wv1, tok.get("wv1_free"))
    ywf_tok, _ = xv_pass(scratch["d_wf"], B.yW, tok.get("yW_used"),
                         B.wv2, tok.get("wv2_free"))
    xvi_tok, _ = xv_pass(scratch["d_vi"], B.xVi, tok.get("xVi_used"),
                         B.wv1, xvf_tok)
    cumf_toks = cum_pass(ywf_tok, used=tok.get("cum_used"))
    pwf_toks = prod_pass(B.xV, xvf_tok, cumf_toks, False)
    ywi_tok, _ = xv_pass(scratch["d_wi"], B.yW, cumf_toks[-1], B.wv2,
                         ywf_tok)
    cumi_toks = cum_pass(ywi_tok, used=pwf_toks[-1])
    pwi_toks = prod_pass(B.xVi, xvi_tok, cumi_toks, True)
    pwf = pwf_toks[-1]
    pwi = pwi_toks[-1]

    ldu = E.dy(B.wbig2[:], scratch["d_uf"], waits=[tok.get("wbig2_free")] + fence)
    ldo = E.dy(B.wbig[:], scratch["d_wo"], waits=[prevh] + fence)
    xr_toks = []
    ot_prev = None
    for nt in range(NT):
        o_toks = []
        for oc in range(DC):
            b = (oc % 2)
            bank = PS.big[:, b * 512: b * 512 + 512]
            mm = None
            for kc in range(4):
                mm = E.t(lambda h, oc=oc, nt=nt, kc=kc, bank=bank:
                         h.matmul(bank,
                                  B.wbig2[:, kc * D + oc * P: kc * D + oc * P + P],
                                  B.xV[:, kc * T + nt * 512: kc * T + nt * 512 + 512],
                                  start=(kc == 0), stop=False),
                         waits=[ldu, pwf_toks[nt], pwi_toks[nt],
                                tok.get(f"ubank{b}")]
                         if kc == 0 else [])
            for kc in range(4):
                mm = E.t(lambda h, oc=oc, nt=nt, kc=kc, bank=bank:
                         h.matmul(bank,
                                  B.wbig2[:, (4 + kc) * D + oc * P:
                                          (4 + kc) * D + oc * P + P],
                                  B.xVi[:, kc * T + nt * 512: kc * T + nt * 512 + 512],
                                  start=False, stop=False))
            mm = E.t(lambda h, oc=oc, nt=nt, bank=bank:
                     h.matmul(bank, B.bfw[:, oc * P: oc * P + P],
                              B.wT[:, nt * 512: nt * 512 + 512],
                              start=False, stop=False),
                     waits=[wT_tok])
            mm = E.t(lambda h, oc=oc, nt=nt, bank=bank:
                     h.matmul(bank, B.binv[:, oc * P: oc * P + P],
                              B.wT[:, nt * 512: nt * 512 + 512],
                              start=False, stop=True),
                     waits=[tok["binv"]])
            ot = E.v(lambda h, oc=oc, nt=nt, bank=bank:
                     h.tensor_tensor(B.o1[:, oc * 512: oc * 512 + 512],
                                     bank,
                                     B.invn[:, nt * 512: nt * 512 + 512],
                                     op=ALU.mult),
                     waits=[mm, tok.get("o1_used"), tok.get("invn_tok")])
            tok[f"ubank{b}"] = ot
            o_toks.append(ot)
        for oc2 in range(DC):
            b = next_bank()
            mm = None
            for kc in range(DC):
                mm = E.t(lambda h, oc2=oc2, nt=nt, kc=kc, b=b:
                         h.matmul(banks[b][:, 0:512],
                                  B.wbig[:, kc * D + oc2 * P: kc * D + oc2 * P + P],
                                  B.o1[:, kc * 512: kc * 512 + 512],
                                  start=(kc == 0), stop=(kc == DC - 1)),
                         waits=[ldo, bank_free[b]] + o_toks if kc == 0 else [])
            xr = E.v(lambda h, oc2=oc2, nt=nt, b=b:
                     h.tensor_tensor(
                         B.xbf[:, oc2 * T + nt * 512: oc2 * T + nt * 512 + 512],
                         banks[b][:, 0:512],
                         B.xbf[:, oc2 * T + nt * 512: oc2 * T + nt * 512 + 512],
                         op=ALU.add), waits=[mm])
            bank_free[b] = xr
            xr_toks.append(xr)
        tok["o1_used"] = xr_toks[-1]
    tok["xV_used"] = xr_toks[-1]
    tok["xVi_used"] = xr_toks[-1]
    tok["yW_used"] = cumi_toks[-1]
    tok["wv1_free"] = xr_toks[-1]
    tok["wv2_free"] = xr_toks[-1]
    tok["wbig_free"] = xr_toks[-1]
    tok["wbig2_free"] = xr_toks[-1]
    tok["wrep_used"] = xr_toks[-1]
    tok["wT_used"] = xr_toks[-1]
    tok["wtok_used"] = tok["wtok_wr"]
    tok["cum_used"] = pwi
    tok["xV_used"] = xr_toks[-1]
    emit_ln(E, B, PS, tok, xr_toks, with_out)


def build_phaseD(nseq=NSEQ):
    nc = bass.Bass("TRN2", target_bir_lowering=False)
    xin = nc.dram_tensor("xin", [nseq, D, T], F32, kind="ExternalInput").ap()
    xout = nc.dram_tensor("xout", [nseq, D, T], F32, kind="ExternalOutput").ap()
    W = {}
    for name, shp in [
            ("wq_n", (DC, P, D)), ("wk_n", (DC, P, D)), ("winv_n", (DC, P, D)),
            ("winvT", (DC, P, D)), ("vfF", (DC, P, 512)), ("wfF", (DC, P, 512)),
            ("vroleIF", (DC, P, 512)), ("wroleIF", (DC, P, 512)),
            ("ufui", (DC, P, D)), ("rw1T", (DC, P, D)), ("woT3", (DC, P, D)),
            ("rw2T_in", (DC, P, 8)), ("rep8_in", (8, 512)),
            ("bfw_in", (8, D)), ("binv_in", (8, D)),
            ("b1F_in", (P, DC)), ("b2rep_in", (P, 8)), ("ebrep_in", (P, 8)),
            ("alpha_in", (P, 1)), ("invn_in", (1, P, T)),
            ("ident_f", (P, P))]:
        W[name] = nc.dram_tensor(name, list(shp), F32, kind="ExternalInput").ap()
    scratch = {k: nc.dram_tensor(k, [P, DC * 512], BF16, kind="Internal").ap()
               for k in ("d_vf", "d_wf", "d_vi", "d_wi")}
    scratch["d_rw1"] = nc.dram_tensor("d_rw1", [P, DC * D], BF16,
                                      kind="Internal").ap()
    scratch["d_uf"] = nc.dram_tensor("d_uf", [P, DC * D], BF16,
                                     kind="Internal").ap()
    scratch["d_wo"] = nc.dram_tensor("d_wo", [P, DC * D], BF16,
                                     kind="Internal").ap()

    with ExitStack() as st:
        B = NS()
        alloc_phaseD(nc, st, B)
        PS = NS()
        PS.big = st.enter_context(nc.psum_tensor("psbig", [P, 2048], F32))
        PS.a = st.enter_context(nc.psum_tensor("psa", [P, 512], F32))
        PS.b = st.enter_context(nc.psum_tensor("psb", [P, 512], F32))
        PS.c = st.enter_context(nc.psum_tensor("psc", [P, 512], F32))
        PS.tr = st.enter_context(nc.psum_tensor("pstr", [P, 1024], BF16))
        sems = {}
        for s in ("T", "V", "S", "G", "Y0", "Y1", "Y2", "Y3",
                  "GD0", "GD1", "GD2", "GD3"):
            sems[s] = st.enter_context(nc.semaphore(s))
        block = st.enter_context(nc.Block())
        E = Emitter(nc, sems)

        def prog():
            tok = {}
            emit_phaseD_prologue(E, B, PS, W, scratch, tok)
            lu = loadw(E, B, W, "ufui", B.wbig2, D, tok)
            du = E.dg(scratch["d_uf"], B.wbig2[:], waits=[lu])
            lo = loadw(E, B, W, "woT3", B.wbig, D, tok)
            do = E.dg(scratch["d_wo"], B.wbig[:], waits=[lo])
            tok["folds_dma"] = tok["folds_dma"] + [du, do]
            tok["wbig_free"] = do
            tok["wbig2_free"] = du
            tok["invn_tok"] = tok.get("st1")
            for s_ in range(nseq):
                emit_load_x(E, B, xin, s_, tok)
                emit_phaseD_seq(E, B, PS, W, scratch, tok,
                                with_out=xout[s_, :, :])
            for k in ("GD0", "GD1", "GD2", "GD3"):
                E.op("gpsimd",
                     lambda h, k=k: h.wait_ge(sems[k], E.cnt[k]), inc=0)

        for name in Emitter.ENGINES:
            getattr(block, name)(lambda h, name=name: (E.start(name, h), prog()))
    return nc



POOL_N = 78336


class Carver:
    def __init__(self, pool):
        self.pool = pool
        self.off = 0

    def bf(self, elems, rows=P):
        start = self.off
        self.off += (elems + 1) // 2 * 2
        assert self.off <= POOL_N, f"pool overflow: {self.off} > {POOL_N}"
        ap = self.pool[:, start:start + elems]
        return ap if rows == P else ap[0:rows, :]

    def f32(self, elems, rows=P):
        ap = self.bf(2 * elems)
        ap = ap.bitcast(F32)
        return ap if rows == P else ap[0:rows, :]


def make_shared(nc, st):
    S = NS()
    sb = lambda n, s, dt: st.enter_context(nc.sbuf_tensor(n, s, dt))
    S.xbf = sb("xbf", [P, DC * T], BF16)
    S.stage = [sb(f"stage{i}", [P, T], F32) for i in range(2)]
    S.sq = [sb(f"sq{i}", [P, 512], BF16) for i in range(2)]
    S.statrow = sb("statrow", [1, 512], F32)
    S.statbf = sb("statbf", [1, 512], BF16)
    S.statbf2 = sb("statbf2", [1, 512], BF16)
    S.ones1 = sb("ones1", [1, P], BF16)
    S.ones128 = sb("ones128", [P, 1], BF16)
    S.identbf = sb("identbf", [P, P], BF16)
    S.pool = sb("pool", [P, POOL_N], BF16)
    return S


def _sh(B, S):
    for k in ("xbf", "stage", "sq", "statrow", "statbf", "statbf2",
              "ones1", "ones128", "identbf"):
        setattr(B, k, getattr(S, k))


def carve_A(S):
    B = NS(); _sh(B, S); c = Carver(S.pool)
    B.qc = c.bf(DC * T)
    B.qr = c.bf(4 * T)
    B.ckv = c.bf(2 * T)
    B.krs = c.bf(T, rows=32)
    B.qstage = [c.bf(T) for _ in range(2)]
    B.kstage = [c.bf(T) for _ in range(2)]
    B.vh = [c.bf(16 * 66) for _ in range(2)]
    B.probsT = c.bf(8 * 512)
    B.weffF = c.bf(DC * D)
    B.wqreffF = c.bf(DC * 512)
    B.wdkvF = c.bf(DC * 256)
    B.wukF = c.bf(2 * D)
    B.wuvF = c.bf(2 * D)
    B.wkrF = c.bf(DC * 32)
    B.woF = c.bf(DC * D)
    B.rope = c.bf(T)
    B.tri01 = c.bf(P)
    B.rrow = c.bf(512, rows=1)
    B.rrowf = c.f32(512, rows=1)
    B.rbc = c.bf(512, rows=64)
    B.wstageA = lambda i, o=0, w=D: B.stage[i][:, o:o + w]
    B.wstageB = lambda i, o=0, w=D: B.stage[i][:, D + o:D + o + w]
    B.wbfA = lambda i, o=0, w=D: B.qstage[i][:, o: o + w]
    B.wbfB = lambda i, o=0, w=D: B.kstage[i][:, o: o + w]
    return B


def carve_B(S):
    B = NS(); _sh(B, S); c = Carver(S.pool)
    B.routerF = c.bf(DC * 64)
    B.corr = c.f32(64)
    B.lgall = c.f32(16 * 64)
    B.sc = c.f32(256); B.sfc = c.f32(256)
    B.msk = c.f32(256); B.hit = c.f32(256); B.sel = c.f32(256)
    B.gs = c.f32(32); B.g2 = c.f32(32); B.gm = c.f32(32)
    B.rm = c.f32(4); B.rs = c.f32(4)
    B.combine = c.f32(16 * 64)
    B.cmbbf = c.bf(64)
    B.cT = c.bf(T, rows=64)
    B.onehot = c.bf(64 * P, rows=64)
    B.wup = [c.bf(8 * 512) for _ in range(2)]
    B.wdn = [c.bf(4 * D) for _ in range(4)]
    B.hbuf = c.bf(4 * 4 * 2 * 512)
    B.rbuf = [c.bf(512) for _ in range(2)]
    B.sqe = [c.bf(512) for _ in range(2)]
    B.cbc = [c.bf(512) for _ in range(2)]
    B.acc = c.f32(DC * 1024)
    return B


def carve_C(S):
    B = NS(); _sh(B, S); c = Carver(S.pool)
    B.trilT = c.bf(16 * 64, rows=64)
    B.wqk = c.bf(DC * P)
    B.tbuf = c.bf(DC * T)
    B.vhC = c.bf(16 * 16 * 66)
    B.probsT = c.bf(8 * 512)
    B.xst = [c.bf(T) for _ in range(2)]
    B.taug = [c.bf(512) for _ in range(2)]
    B.mbuf = c.f32(16)
    B.mbufb = c.bf(16)
    B.mtr = c.bf(P, rows=16)
    B.mrow = [c.bf(T, rows=1) for _ in range(2)]
    B.wvF = c.bf(DC * D)
    B.woF = c.bf(DC * D)
    B.bvrep = c.bf(D)
    B.boF = c.f32(DC)
    B.rrow = c.bf(512, rows=1)
    B.rrowf = c.f32(512, rows=1)
    B.rbc = c.bf(512, rows=64)
    return B


def carve_D(S):
    B = NS(); _sh(B, S); c = Carver(S.pool)
    B.wbig = c.bf(DC * D)
    B.wbig2 = c.bf(DC * D)
    B.wv1 = c.bf(DC * 512)
    B.wv2 = c.bf(DC * 512)
    B.xV = c.bf(4 * T)
    B.xVi = c.bf(4 * T)
    B.yW = c.bf(4 * T)
    B.cum = c.bf(4 * T)
    B.wrep = c.bf(4 * T)
    B.h8 = c.bf(DC * P)
    B.o1 = c.bf(DC * 512)
    B.zer = c.bf(512)
    B.carry = c.f32(4)
    B.wtok = c.f32(16 * 8)
    B.wtb = c.bf(8)
    B.wT = c.bf(T, rows=8)
    B.rw2F = c.bf(DC * 8)
    B.b1F = c.f32(DC)
    B.b2eb = c.f32(8)
    B.alphac = c.f32(1)
    B.bfw = c.bf(D, rows=8)
    B.binv = c.bf(D, rows=8)
    B.rep8 = c.bf(512, rows=8)
    B.invn = c.bf(T)
    B.lgD = c.f32(8)
    B.rmD = c.f32(1)
    return B


def emit_castjob(E, B, W, name, scr, cw, tok, rows=P):
    is3 = len(W[name].shape) == 3
    nch = W[name].shape[0] if is3 else 1
    ji = tok.get("_cast_i", 0)
    dmo = None
    for ci in range(nch):
        src = W[name][ci] if is3 else W[name]
        for c0 in range(0, cw, T):
            w = min(T, cw - c0)
            i = ji % 2
            ji += 1
            ld = E.dy(B.stage[i][0:rows, 0:w], src[0:rows, c0:c0 + w],
                      waits=[tok.get("st0") if i == 0 else tok.get("st1")])
            ct = E.v(lambda h, i=i, w=w, rows=rows:
                     h.tensor_copy(B.qc[0:rows, i * T: i * T + w],
                                   B.stage[i][0:rows, 0:w]),
                     waits=[ld, tok.get(f"qcst{i}")])
            base = ci * cw + c0
            dmo = E.dg(scr[0:rows, base: base + w],
                       B.qc[0:rows, i * T: i * T + w], waits=[ct])
            tok[f"qcst{i}"] = dmo
            if i == 0:
                tok["st0"] = ct
            else:
                tok["st1"] = ct
    tok["_cast_i"] = ji
    return dmo


def _ysnap(E):
    return [(s, E.cnt[s]) for s in ("Y0", "Y1", "Y2", "Y3")]


def fused_consts_A(E, B, scr, tok):
    f = tok.get("fence", [])
    for name, dst in [
            ("weff_s", B.weffF), ("qreff_s", B.wqreffF),
            ("wdkv_s", B.wdkvF), ("wuk_s", B.wukF),
            ("wuv_s", B.wuvF), ("wkr_s", B.wkrF),
            ("wo_s", B.woF), ("rope_s", B.rope),
            ("tri_s", B.tri01)]:
        E.dy(dst, scr[name], waits=f)
    ys = _ysnap(E)
    for key in ("weff", "rope", "tri", "wsmall"):
        tok[key] = ys
    for k in ("vh_ones0", "vh_ones1"):
        tok.pop(k, None)


def fused_consts_B(E, B, W, scr, tok):
    f = tok.get("fence", [])
    E.dy(B.routerF, scr["router_s"], waits=f)
    E.dy(B.onehot, scr["onehot_s"], waits=f)
    E.dy(B.corr, W["corr_rep"], waits=f)
    ys = _ysnap(E)
    for key in ("router", "onehot", "corr"):
        tok[key] = ys
    tok.pop("acc_read", None)


def fused_consts_C(E, B, W, scr, tok):
    f = tok.get("fence", [])
    E.dy(B.wqk, scr["wqk_s"], waits=f)
    E.dy(B.wvF, scr["wv_s"], waits=f)
    E.dy(B.woF, scr["wo2_s"], waits=f)
    E.dy(B.bvrep, scr["bv_s"], waits=f)
    E.dy(B.boF, W["boF_in"], waits=f)
    ys = _ysnap(E)
    for key in ("wqk", "wsmallC", "boF"):
        tok[key] = ys
    o1 = E.v(lambda h: h.memset(B.xst[0][:], 1.0), waits=f)
    o2 = E.v(lambda h: h.memset(B.xst[1][:], 1.0), waits=[o1])
    tok["xst_ones"] = o2
    tok["vh_ones"] = E.v(lambda h: h.memset(B.vhC[:], 1.0), waits=[o2])


def fused_consts_D(E, B, W, scr, tok):
    f = tok.get("fence", [])
    E.dy(B.rw2F, scr["rw2_s"], waits=f)
    E.dy(B.rep8, scr["rep8_s"], waits=f)
    E.dy(B.bfw, scr["bfw_s"], waits=f)
    E.dy(B.binv, scr["binv_s"], waits=f)
    E.dy(B.invn, scr["invn_s"], waits=f)
    E.dy(B.b1F, W["b1F_in"], waits=f)
    E.dy(B.alphac, W["alpha_in"], waits=f)
    ys = _ysnap(E)
    for key in ("rw2", "rep8", "bfw", "binv", "invn_tok", "b1F", "alpha"):
        tok[key] = ys
    l1 = E.dy(B.stage[0][:, 0:8], W["b2rep_in"], waits=[tok.get("st0")] + f)
    l2 = E.dy(B.stage[1][:, 0:8], W["ebrep_in"], waits=[tok.get("st1")] + f)
    b2 = E.v(lambda h: h.tensor_add(B.b2eb, B.stage[0][:, 0:8],
                                    B.stage[1][:, 0:8]), waits=[l1, l2])
    tok["b2eb"] = b2
    tok["st0"] = b2
    tok["st1"] = b2
    tok["zer"] = E.v(lambda h: h.memset(B.zer, 0.0), waits=f)
    tok["wbig_free"] = None
    tok["wbig2_free"] = None


def build_fused(nseq=NSEQ):
    nc = bass.Bass("TRN2", target_bir_lowering=False)
    xin = nc.dram_tensor("xin", [nseq, D, T], F32, kind="ExternalInput").ap()
    xout = nc.dram_tensor("xout", [nseq, D, T], F32, kind="ExternalOutput").ap()
    W = {}
    all_ins = [
        ("w_dqt", (32, P, D)), ("w_uq", (32, P, D)), ("w_qr", (32, P, 512)),
        ("w_dkv", (DC, P, 256)), ("w_uk", (2, P, D)), ("w_uv", (2, P, D)),
        ("w_kr", (DC, P, 32)), ("w_o", (DC, P, D)), ("rope_rep", (P, T)),
        ("tri01", (P, P)),
        ("routerT", (DC, P, 64)), ("corr_rep", (P, 64)),
        ("up_w", (64, DC, P, 512)), ("dn_w", (64, 4, P, D)),
        ("shexp_up", (4, DC, P, 512)), ("shexp_dn", (4, 4, P, D)),
        ("onehot_f", (64, 64 * P)),
        ("trilT_f", (P, 1024)), ("wvT", (DC, P, D)), ("woT2", (DC, P, D)),
        ("bv_rep", (P, D)), ("boF_in", (P, DC)),
        ("wq_n", (DC, P, D)), ("wk_n", (DC, P, D)), ("winv_n", (DC, P, D)),
        ("winvT", (DC, P, D)), ("vfF", (DC, P, 512)), ("wfF", (DC, P, 512)),
        ("vroleIF", (DC, P, 512)), ("wroleIF", (DC, P, 512)),
        ("ufui", (DC, P, D)), ("rw1T", (DC, P, D)), ("woT3", (DC, P, D)),
        ("rw2T_in", (DC, P, 8)), ("rep8_in", (8, 512)), ("bfw_in", (8, D)),
        ("binv_in", (8, D)), ("b1F_in", (P, DC)), ("b2rep_in", (P, 8)),
        ("ebrep_in", (P, 8)), ("alpha_in", (P, 1)), ("invn_in", (1, P, T)),
        ("ident_f", (P, P)),
    ]
    for name, shp in all_ins:
        W[name] = nc.dram_tensor(name, list(shp), F32, kind="ExternalInput").ap()
    scr = {}
    for name, shp in [
            ("weff_s", [P, DC * D]), ("qreff_s", [P, DC * 512]),
            ("wdkv_s", [P, DC * 256]), ("wuk_s", [P, 2 * D]),
            ("wuv_s", [P, 2 * D]), ("wkr_s", [P, DC * 32]),
            ("wo_s", [P, DC * D]), ("rope_s", [P, T]), ("tri_s", [P, P]),
            ("router_s", [P, DC * 64]), ("onehot_s", [64, 64 * P]),
            ("wqk_s", [P, DC * P]), ("wv_s", [P, DC * D]),
            ("wo2_s", [P, DC * D]), ("bv_s", [P, D]),
            ("d_vf", [P, DC * 512]), ("d_wf", [P, DC * 512]),
            ("d_vi", [P, DC * 512]), ("d_wi", [P, DC * 512]),
            ("d_rw1", [P, DC * D]), ("d_uf", [P, DC * D]),
            ("d_wo", [P, DC * D]), ("rw2_s", [P, DC * 8]),
            ("rep8_s", [8, 512]), ("bfw_s", [8, D]), ("binv_s", [8, D]),
            ("invn_s", [P, T])]:
        scr[name] = nc.dram_tensor(name, shp, BF16, kind="Internal").ap()

    with ExitStack() as st:
        S = make_shared(nc, st)
        BA, BB, BC, BD = carve_A(S), carve_B(S), carve_C(S), carve_D(S)
        PS = NS()
        PS.big = st.enter_context(nc.psum_tensor("psbig", [P, 2048], F32))
        PS.a = st.enter_context(nc.psum_tensor("psa", [P, 512], F32))
        PS.b = st.enter_context(nc.psum_tensor("psb", [P, 512], F32))
        PS.c = st.enter_context(nc.psum_tensor("psc", [P, 512], F32))
        PS.tr = st.enter_context(nc.psum_tensor("pstr", [P, 1024], BF16))
        sems = {}
        for s in ("T", "V", "S", "G", "Y0", "Y1", "Y2", "Y3",
                  "GD0", "GD1", "GD2", "GD3"):
            sems[s] = st.enter_context(nc.semaphore(s))
        block = st.enter_context(nc.Block())
        E = Emitter(nc, sems)

        def snap():
            return [(s, E.cnt[s]) for s in E.cnt]

        def prog():
            tok = {}
            c1 = E.v(lambda h: h.memset(S.ones1[:], 1.0))
            c2 = E.v(lambda h: h.memset(S.ones128[:], 1.0), waits=[c1])
            tok["ones"] = c2
            ld = E.dy(S.stage[0][:, 0:P], W["ident_f"])
            idt = E.v(lambda h: h.tensor_copy(S.identbf[:], S.stage[0][:, 0:P]),
                      waits=[ld, c2])
            tok["ident"] = idt
            tok["st0"] = idt
            emit_castjob(E, BA, W, "w_dkv", scr["wdkv_s"], 256, tok)
            emit_castjob(E, BA, W, "w_uk", scr["wuk_s"], D, tok)
            emit_castjob(E, BA, W, "w_uv", scr["wuv_s"], D, tok)
            emit_castjob(E, BA, W, "w_kr", scr["wkr_s"], 32, tok)
            emit_castjob(E, BA, W, "w_o", scr["wo_s"], D, tok)
            emit_castjob(E, BA, W, "rope_rep", scr["rope_s"], T, tok)
            emit_castjob(E, BA, W, "tri01", scr["tri_s"], P, tok)
            emit_castjob(E, BA, W, "routerT", scr["router_s"], 64, tok)
            emit_castjob(E, BA, W, "onehot_f", scr["onehot_s"], 64 * P, tok,
                         rows=64)
            emit_castjob(E, BA, W, "wvT", scr["wv_s"], D, tok)
            emit_castjob(E, BA, W, "woT2", scr["wo2_s"], D, tok)
            emit_castjob(E, BA, W, "bv_rep", scr["bv_s"], D, tok)
            emit_castjob(E, BA, W, "ufui", scr["d_uf"], D, tok)
            emit_castjob(E, BA, W, "woT3", scr["d_wo"], D, tok)
            emit_castjob(E, BA, W, "rw2T_in", scr["rw2_s"], 8, tok)
            emit_castjob(E, BA, W, "rep8_in", scr["rep8_s"], 512, tok, rows=8)
            emit_castjob(E, BA, W, "bfw_in", scr["bfw_s"], D, tok, rows=8)
            emit_castjob(E, BA, W, "invn_in", scr["invn_s"], T, tok)
            la = E.dy(BD.alphac, W["alpha_in"], waits=[])
            lb = E.dy(BA.stage[0][0:8, 0:D], W["binv_in"],
                      waits=[tok.get("st0")])
            bi = E.v(lambda h: h.tensor_scalar_mul(BA.qc[0:8, 0:D],
                                                   BA.stage[0][0:8, 0:D],
                                                   BD.alphac[0:8, 0:1]),
                     waits=[lb, la, tok.get("qcst0")])
            tok["qcst0"] = E.dg(scr["binv_s"], BA.qc[0:8, 0:D], waits=[bi])
            tok["st0"] = bi
            tok["fence"] = snap()
            emit_weff_prologue(E, BA, W, PS,
                               {"weff": scr["weff_s"],
                                "qreff": scr["qreff_s"]}, tok)
            ld2 = E.dy(BC.stage[1][:, 0:1024], W["trilT_f"],
                       waits=[tok.get("st1")] + tok["fence"])
            ttr = E.v(lambda h: h.tensor_copy(BC.trilT, BC.stage[1][0:64, 0:1024]),
                      waits=[ld2, ("T", E.cnt["T"])])
            mz = E.v(lambda h: h.memset(BC.wqk, 0.0),
                     waits=[ttr, ("T", E.cnt["T"])])
            wq = None
            for h_ in range(H):
                mm = E.t(lambda h, h_=h_:
                         h.matmul(PS.a[0:64, 0:64],
                                  BC.trilT[0:64, h_ * 64: h_ * 64 + 64],
                                  BC.trilT[0:64, h_ * 64: h_ * 64 + 64],
                                  start=True, stop=True),
                         waits=[mz, wq])
                r0 = (h_ % 2) * 64
                wq = E.v(lambda h, h_=h_, r0=r0:
                         h.tensor_scalar_mul(
                             BC.wqk[r0:r0 + 64,
                                    (h_ // 2) * P + r0: (h_ // 2) * P + r0 + 64],
                             PS.a[0:64, 0:64], 0.125),
                         waits=[mm])
            dq = E.dg(scr["wqk_s"], BC.wqk, waits=[wq])
            tok["psa_free"] = wq
            tok["st1"] = ttr
            tok["fence"] = snap()
            emit_phaseD_prologue_folds(E, BD, PS, W, scr, tok)
            tok["fence"] = snap()
            for s_ in range(nseq):
                emit_load_x(E, S, xin, s_, tok)
                fused_consts_A(E, BA, scr, tok)
                emit_phaseA_seq(E, BA, PS, tok, with_out=None)
                fused_consts_B(E, BB, W, scr, tok)
                emit_phaseB_seq(E, BB, PS, W, tok, with_out=None)
                fused_consts_C(E, BC, W, scr, tok)
                emit_phaseC_seq(E, BC, PS, tok, with_out=None)
                fused_consts_D(E, BD, W, scr, tok)
                emit_phaseD_seq(E, BD, PS, W, scr, tok, with_out=xout[s_, :, :])
            for k in ("GD0", "GD1", "GD2", "GD3"):
                E.op("gpsimd",
                     lambda h, k=k: h.wait_ge(sems[k], E.cnt[k]), inc=0)

        for name in Emitter.ENGINES:
            getattr(block, name)(lambda h, name=name: (E.start(name, h), prog()))
    return nc


def emit_phaseD_prologue_folds(E, B, PS, W, scratch, tok):
    fence = tok.get("fence", [])
    t1 = loadw(E, B, W, "wq_n", B.wbig, D, tok)
    t2 = loadw(E, B, W, "vfF", B.wv1, 512, tok)
    f1 = emit_fold(E, B, PS, B.wbig, t1, B.wv1, t2, B.wv2, 512,
                   fence[0] if fence else None, "vf")
    d1 = E.dg(scratch["d_vf"], B.wv2, waits=[f1])
    t3 = loadw(E, B, W, "rw1T", B.wbig2, D, tok)
    dlast = [None, None, None, None]
    ti = 0
    f2 = f1
    for dc in range(DC):
        for n in range(2):
            bk = ti % 4
            mm = None
            for kc in range(DC):
                mm = E.t(lambda h, dc=dc, n=n, kc=kc, bk=bk:
                         h.matmul(PS.big[:, bk * 512: bk * 512 + 512],
                                  B.wbig[:, kc * D + dc * P: kc * D + dc * P + P],
                                  B.wbig2[:, kc * D + n * 512: kc * D + n * 512 + 512],
                                  start=(kc == 0), stop=(kc == DC - 1)),
                         waits=[t1, t3, f1, f2] if kc == 0 else [])
            ev = E.v(lambda h, bk=bk:
                     h.tensor_copy(B.wv1[:, bk * 512: bk * 512 + 512],
                                   PS.big[:, bk * 512: bk * 512 + 512]),
                     waits=[mm, dlast[bk], d1])
            dlast[bk] = E.dg(scratch["d_rw1"][:, dc * D + n * 512:
                                              dc * D + n * 512 + 512],
                             B.wv1[:, bk * 512: bk * 512 + 512], waits=[ev])
            f2 = ev
            ti += 1
    d2 = dlast[(ti - 1) % 4]
    tok["fence"] = E.snap()
    t5 = loadw(E, B, W, "winv_n", B.wbig2, D, tok)
    t4 = loadw(E, B, W, "vroleIF", B.wv1, 512, tok)
    f3 = emit_fold(E, B, PS, B.wbig2, t5, B.wv1, t4, B.wv2, 512,
                   dlast[0], "ivf", out_free=d1)
    f4 = emit_fold(E, B, PS, B.wbig, t1, B.wv2, f3, B.wv1, 512, f3, "vi",
                   out_free=d2)
    d3 = E.dg(scratch["d_vi"], B.wv1, waits=[f4])
    tok["fence"] = E.snap()
    t6 = loadw(E, B, W, "wk_n", B.wbig, D, tok)
    t7 = loadw(E, B, W, "wfF", B.wv2, 512, tok)
    f5 = emit_fold(E, B, PS, B.wbig, t6, B.wv2, t7, B.wv1, 512, f4,
                   "wf", out_free=d3)
    d4 = E.dg(scratch["d_wf"], B.wv1, waits=[f5])
    tok["fence"] = E.snap()
    t8 = loadw(E, B, W, "wroleIF", B.wv2, 512, tok)
    f6 = emit_fold(E, B, PS, B.wbig2, t5, B.wv2, t8, B.wv1, 512, f5, "iwf",
                   out_free=d4)
    f7 = emit_fold(E, B, PS, B.wbig, t6, B.wv1, f6, B.wv2, 512, f6, "wi",
                   out_free=d4)
    d5 = E.dg(scratch["d_wi"], B.wv2, waits=[f7])
    tok["folds"] = f7
    tok["folds_dma"] = [d1, d2, d3, d4, d5]
    tok["st0"] = f7
    tok["st1"] = f7
